# version 02
# baseline (speedup 1.0000x reference)
import contextlib
import numpy as np
import ml_dtypes

import concourse.bass as bass
import concourse.mybir as mybir
from concourse.bass_utils import run_bass_kernel_spmd

F32 = mybir.dt.float32
BF16 = mybir.dt.bfloat16
AF = mybir.ActivationFunctionType
ALU = mybir.AluOpType
AX = mybir.AxisListType

T = 2048
D = 1024
NT = 16
ND = 8

ENGINES = ("PE", "DVE", "ACT", "POOL", "SP")


class _Node:
    __slots__ = ("idx", "engine", "kind", "emit", "deps", "pos", "semname", "semval",
                 "need_inc", "incval", "retained", "know", "tag")

    def __init__(self, idx, engine, kind, emit, deps, pos, tag):
        self.idx = idx
        self.engine = engine
        self.kind = kind
        self.emit = emit
        self.deps = deps
        self.pos = pos
        self.semname = None
        self.semval = None
        self.need_inc = False
        self.incval = None
        self.retained = None
        self.know = None
        self.tag = tag


class MK:

    SERIAL = False
    ENDCLEAR = True
    SAME_ENGINE_SYNC = True

    def __init__(self, nc):
        self.nc = nc
        self.es = contextlib.ExitStack()
        self.nodes = []
        self.res_w = {}
        self.res_r = {}
        self.eng_pos = {e: 0 for e in ENGINES}
        self.streams = {}
        self.sem_names = ["sem_" + e for e in ENGINES]
        self.out_events = []
        self.last_barrier = []

    def sb(self, name, shape, dtype):
        return self.es.enter_context(self.nc.sbuf_tensor(name, shape, dtype))

    def ps(self, name, shape, dtype):
        return self.es.enter_context(self.nc.psum_tensor(name, shape, dtype))

    def _deps_for(self, reads, writes):
        deps = []
        seen = set()

        def add(n):
            if n is not None and id(n) not in seen:
                seen.add(id(n))
                deps.append(n)
        for r in reads:
            add(self.res_w.get(r))
        for w in writes:
            add(self.res_w.get(w))
            for rd in self.res_r.get(w, ()):
                add(rd)
        return deps

    def _update_res(self, node, reads, writes):
        for r in reads:
            self.res_r.setdefault(r, []).append(node)
        for w in writes:
            self.res_w[w] = node
            self.res_r[w] = []

    def op(self, engine, emit, reads=(), writes=(), tag=""):
        deps = self._deps_for(reads, writes)
        for n in self.last_barrier:
            if n not in deps:
                deps.append(n)
        if MK.SERIAL and self.nodes and self.nodes[-1] not in deps:
            deps.append(self.nodes[-1])
        node = _Node(len(self.nodes), engine, "c", emit, deps, self.eng_pos[engine], tag)
        self.eng_pos[engine] += 1
        self.nodes.append(node)
        self._update_res(node, reads, writes)
        return node

    def stream(self, name, queue, parity=2):
        if name not in self.streams:
            sems = []
            for i in range(parity):
                sn = f"dma_{name}_{i}"
                sems.append(sn)
                self.sem_names.append(sn)
            self.streams[name] = dict(queue=queue, sems=sems, cnt=[0] * parity,
                                      last=[None] * parity, i=0)
        return self.streams[name]

    def dma(self, stream_name, out_ap, in_ap, reads=(), writes=(), tag="", queue="POOL",
            parity=2, is_output=False):
        st = self.stream(stream_name, queue, parity)
        queue = st["queue"]
        slot = st["i"] % len(st["sems"])
        st["i"] += 1
        deps = self._deps_for(reads, writes)
        for n in self.last_barrier:
            if n not in deps:
                deps.append(n)
        if MK.SERIAL and self.nodes and self.nodes[-1] not in deps:
            deps.append(self.nodes[-1])
        prev = st["last"][slot]
        if prev is not None and prev not in deps:
            deps.append(prev)
        node = _Node(len(self.nodes), queue, "d",
                     (lambda eng, o=out_ap, i=in_ap: eng.dma_start(out=o, in_=i)),
                     deps, self.eng_pos[queue], tag)
        st["cnt"][slot] += 1
        node.semname = st["sems"][slot]
        node.semval = 16 * st["cnt"][slot]
        st["last"][slot] = node
        self.eng_pos[queue] += 1
        self.nodes.append(node)
        self._update_res(node, reads, writes)
        if is_output:
            self.out_events.append(node)
        return node

    def barrier(self):
        lasts = {}
        for n in reversed(self.nodes):
            if n.engine not in lasts:
                lasts[n.engine] = n
            if len(lasts) == len(ENGINES):
                break
        ln = list(lasts.values())
        for st in self.streams.values():
            for nd in st["last"]:
                if nd is not None and nd not in ln:
                    ln.append(nd)
        self.last_barrier = ln

    def finalize(self):
        know_stream = {e: {} for e in ENGINES}
        for node in self.nodes:
            know = dict(know_stream[node.engine])
            retained = []
            for dep in node.deps:
                if (not MK.SAME_ENGINE_SYNC) and dep.kind == "c" and node.kind == "c" \
                        and dep.engine == node.engine:
                    continue
                if dep.kind == "c":
                    key, val = ("E:" + dep.engine), dep.pos
                else:
                    key, val = dep.semname, dep.semval
                if know.get(key, -1) >= val:
                    continue
                retained.append(dep)
                if dep.know:
                    for k2, v2 in dep.know.items():
                        if know.get(k2, -1) < v2:
                            know[k2] = v2
                if know.get(key, -1) < val:
                    know[key] = val
            node.retained = retained
            for dep in retained:
                if dep.kind == "c":
                    dep.need_inc = True
            myknow = dict(know)
            if node.kind == "d":
                myknow[node.semname] = node.semval
            else:
                myknow["E:" + node.engine] = node.pos
            node.know = myknow
            know_stream[node.engine] = know
        cnt = {e: 0 for e in ENGINES}
        for node in self.nodes:
            if node.kind == "c" and node.need_inc:
                cnt[node.engine] += 1
                node.incval = cnt[node.engine]

    def emit(self):
        self.finalize()
        nc = self.nc
        self.sem_names.append("done")
        assert len(self.sem_names) <= 56, len(self.sem_names)
        sems = {}
        for i, sn in enumerate(self.sem_names):
            sems[sn] = self.es.enter_context(nc.semaphore(sn, num=255 - i))
        per_engine = {e: [] for e in ENGINES}
        for node in self.nodes:
            per_engine[node.engine].append(node)
        final_out = [(n.semname, n.semval) for n in self.out_events]
        dma_totals = []
        for st in self.streams.values():
            for sn, c in zip(st["sems"], st["cnt"]):
                if c:
                    dma_totals.append((sn, 16 * c))

        def emit_engine(eng_obj, ename):
            for node in per_engine[ename]:
                for dep in node.retained:
                    if dep.kind == "c":
                        assert dep.incval is not None, (dep.tag, node.tag)
                        eng_obj.wait_ge(sems["sem_" + dep.engine], dep.incval)
                    else:
                        eng_obj.wait_ge(sems[dep.semname], dep.semval)
                instr = node.emit(eng_obj)
                if node.kind == "d":
                    instr.then_inc(sems[node.semname], 16)
                elif node.need_inc:
                    if instr is None:
                        raise RuntimeError(f"op {node.tag} returned no instruction")
                    instr.then_inc(sems["sem_" + node.engine], 1)
            if ename == "POOL":
                for sn, v in final_out:
                    eng_obj.wait_ge(sems[sn], v)
                if False:
                    for sn, v in dma_totals:
                        eng_obj.wait_ge(sems[sn], v)
                    eng_obj.wait_ge(sems["done"], 4)
                    for sn in self.sem_names:
                        eng_obj.sem_clear(sems[sn])
            elif MK.ENDCLEAR:
                eng_obj.sem_inc(sems["done"], 1)

        with nc.Block() as block:
            block.gpsimd(lambda g: emit_engine(g, "POOL"))
            block.sync(lambda s: emit_engine(s, "SP"))
            block.vector(lambda v: emit_engine(v, "DVE"))
            block.scalar(lambda s: emit_engine(s, "ACT"))
            block.tensor(lambda t: emit_engine(t, "PE"))

    def close(self):
        self.es.close()



def _cols(v, n=128):
    v = np.asarray(v, np.float32)
    assert v.size % n == 0
    return np.ascontiguousarray(v.reshape(-1, n).T)


def _host_tables():
    ident_f32 = np.eye(128, dtype=np.float32)
    ident_bf = np.eye(128, dtype=np.float64)
    cm = np.where(np.tril(np.ones((128, 128), bool)), 0.0, -1e30).astype(np.float32)
    il = np.arange(128)[:, None]
    jl = np.arange(384)[None, :]
    ok = (jl >= il) & (jl <= il + 256)
    lm = np.where(ok, 0.0, -1e30).astype(np.float32)
    DIM_HEAD, WINDOW, n = 64, 256, 2048
    half = np.arange(0, DIM_HEAD, 2, dtype=np.float64)
    inv_freq = 1.0 / 10000.0 ** (half / DIM_HEAD)
    t = np.arange(n, dtype=np.float64)
    freqs = t[:, None] * inv_freq[None, :]
    freqs = np.concatenate([freqs, freqs], -1)
    base_scale = (half + 0.4 * DIM_HEAD) / (1.4 * DIM_HEAD)
    power = (t - n // 2) / (WINDOW // 2)
    scale = base_scale[None, :] ** power[:, None]
    scale = np.concatenate([scale, scale], -1)
    cos = np.cos(freqs)
    sin = np.sin(freqs)
    sign = np.where(np.arange(64) < 32, -1.0, 1.0)
    QK = 8.0
    cosS8 = (cos * scale * QK).T
    sinS8 = (sin * scale * QK * sign[None, :]).T
    cosI = (cos / scale).T
    sinI = (sin / scale * sign[None, :]).T

    def dup(a):
        return np.concatenate([a, a], axis=0)
    rot = [dup(a) for a in (cosS8, sinS8, cosI, sinI)]
    P = np.zeros((128, 128), dtype=np.float64)
    for m in range(128):
        base = (m // 64) * 64
        d = m % 64
        P[base + (d + 32) % 64, m] = 1.0
    pidx = np.arange(128)
    ones2 = np.stack([(pidx < 64).astype(np.float64), (pidx >= 64).astype(np.float64)], 1)
    bcast2 = np.zeros((128, 128), np.float64)
    bcast2[0, 0:64] = 1.0
    bcast2[1, 64:128] = 1.0
    ones1 = np.ones((128, 1), np.float64)
    lmaskbf = np.zeros((128, 384), np.float64)
    lmaskbf[:, :] = lm
    sel = np.zeros((128, 8 * 128), np.float64)
    for a in range(8):
        for i in range(16):
            sel[i, a * 128 + 16 * a + i] = 1.0
    tab_bf = np.concatenate([ident_bf] + rot + [P, ones2, bcast2, ones1, lmaskbf, sel],
                            axis=1).astype(ml_dtypes.bfloat16)
    onesrow_f = np.zeros((128, 128), np.float32)
    onesrow_f[0, :] = 1.0
    tab_f32 = np.concatenate([ident_f32, cm, lm, onesrow_f], axis=1).astype(np.float32)
    return np.ascontiguousarray(tab_bf), np.ascontiguousarray(tab_f32)


TB_IDENT = 0
TB_COSS8 = 128
TB_SINS8 = 128 + 2048
TB_COSI = 128 + 2 * 2048
TB_SINI = 128 + 3 * 2048
TB_PSHIFT = 128 + 4 * 2048
TB_ONES2 = TB_PSHIFT + 128
TB_BCAST2 = TB_ONES2 + 2
TB_ONES1 = TB_BCAST2 + 128
TB_LMASKBF = TB_ONES1 + 1
TB_SEL = TB_LMASKBF + 384
TB_COLS = TB_SEL + 8 * 128

TF_IDENT = 0
TF_CAUSAL = 128
TF_LOCALMASK = 256
TF_ONESROW = 256 + 384
TF_COLS = TF_ONESROW + 128


def _host_consts(inputs):
    cols = []
    layout = {}

    def add(name, arr):
        arr = np.asarray(arr, np.float32)
        layout[name] = (sum(c.shape[1] for c in cols), arr.shape[1])
        cols.append(arr)

    add("bqkv_a", _cols(inputs["L0_attn_b_qkv"]))
    for i in range(6):
        add(f"qs{i}", _cols(np.tile(np.asarray(inputs["L0_ffn_q_scale"])[i], 2)))
        add(f"ks{i}", _cols(np.tile(np.asarray(inputs["L0_ffn_k_scale"])[i], 2)))
        add(f"lng{i}", _cols(inputs["L0_ffn_ln_g"][i]))
        add(f"lnb{i}", _cols(inputs["L0_ffn_ln_b"][i]))
        b1 = np.asarray(inputs["L0_ffn_b1"][i], np.float32)
        b1p = np.zeros(5632, np.float32)
        b1p[:2730] = b1[:2730]
        b1p[2816:2816 + 2730] = b1[2730:]
        add(f"b1{i}", _cols(b1p))
        add(f"b2{i}", _cols(inputs["L0_ffn_b2"][i]))
    add("bgate", _cols(inputs["L1_attn_bgate"]))
    add("decay", _cols(np.repeat(np.asarray(inputs["L1_attn_decay_param"]), 64)))
    consts = np.concatenate(cols, axis=1).astype(np.float32)
    return np.ascontiguousarray(consts), layout


def _host_prep(inputs):
    tab_bf, tab_f32 = _host_tables()
    consts, clayout = _host_consts(inputs)
    W1 = np.asarray(inputs["L0_ffn_W1"], np.float32)
    W2 = np.asarray(inputs["L0_ffn_W2"], np.float32)
    W1p = np.zeros((6, 1024, 5632), np.float32)
    W1p[:, :, :2730] = W1[:, :, :2730]
    W1p[:, :, 2816:2816 + 2730] = W1[:, :, 2730:]
    W2p = np.zeros((6, 2816, 1024), np.float32)
    W2p[:, :2730, :] = W2
    shared = {
        "tab_bf": tab_bf,
        "tab_f32": tab_f32,
        "consts": consts,
        "wqkvT_a": np.ascontiguousarray(np.asarray(inputs["L0_attn_W_qkv"], np.float32).T),
        "wqkv_b": np.ascontiguousarray(np.asarray(inputs["L0_ffn_Wqkv"], np.float32)),
        "wo_b": np.ascontiguousarray(np.asarray(inputs["L0_ffn_Wo"], np.float32)),
        "w1p": W1p,
        "w2p": W2p,
        "wqkvT_c": np.ascontiguousarray(np.asarray(inputs["L1_attn_Wqkv"], np.float32).T),
        "wgateT": np.ascontiguousarray(np.asarray(inputs["L1_attn_Wgate"], np.float32).T),
        "woutT": np.ascontiguousarray(np.asarray(inputs["L1_attn_Wout"], np.float32).T),
        "rmT": np.ascontiguousarray(np.asarray(inputs["L1_ffn_rail_maker"], np.float32).transpose(0, 1, 3, 2)),
        "ruT": np.ascontiguousarray(np.asarray(inputs["L1_ffn_rail_up_proj"], np.float32).transpose(0, 1, 3, 2)),
        "wcT": np.ascontiguousarray(np.asarray(inputs["L1_ffn_w_cproj"], np.float32).T),
    }
    return shared, clayout



PBF_COLS = 38912


class Prog:
    def __init__(self, clayout, stages, debug_out=None):
        self.nc = bass.Bass("TRN2", target_bir_lowering=False)
        self.b = MK(self.nc)
        self.clayout = clayout
        self.stages = stages
        self.debug_out = debug_out
        nc, b = self.nc, self.b

        ap = lambda t: t.ap()
        self.h_x = ap(nc.dram_tensor("x", [T, D], F32, kind="ExternalInput"))
        self.h_tab_bf = ap(nc.dram_tensor("tab_bf", [128, TB_COLS], BF16, kind="ExternalInput"))
        self.h_tab_f32 = ap(nc.dram_tensor("tab_f32", [128, TF_COLS], F32, kind="ExternalInput"))
        ccols = max(off + w for off, w in clayout.values())
        self.h_consts = ap(nc.dram_tensor("consts", [128, ccols], F32, kind="ExternalInput"))
        self.h_wqkvT_a = ap(nc.dram_tensor("wqkvT_a", [D, 3 * D], F32, kind="ExternalInput"))
        self.h_wqkv_b = ap(nc.dram_tensor("wqkv_b", [6, D, 3 * D], F32, kind="ExternalInput"))
        self.h_wo_b = ap(nc.dram_tensor("wo_b", [6, D, D], F32, kind="ExternalInput"))
        self.h_w1p = ap(nc.dram_tensor("w1p", [6, D, 5632], F32, kind="ExternalInput"))
        self.h_w2p = ap(nc.dram_tensor("w2p", [6, 2816, D], F32, kind="ExternalInput"))
        self.h_wqkvT_c = ap(nc.dram_tensor("wqkvT_c", [D, 3 * D], F32, kind="ExternalInput"))
        self.h_wgateT = ap(nc.dram_tensor("wgateT", [D, D], F32, kind="ExternalInput"))
        self.h_woutT = ap(nc.dram_tensor("woutT", [D, D], F32, kind="ExternalInput"))
        self.h_rmT = ap(nc.dram_tensor("rmT", [4, 64, 16, 16], F32, kind="ExternalInput"))
        self.h_ruT = ap(nc.dram_tensor("ruT", [4, 64, 16, 64], F32, kind="ExternalInput"))
        self.h_wcT = ap(nc.dram_tensor("wcT", [16 * D, D], F32, kind="ExternalInput"))
        self.h_out = ap(nc.dram_tensor("out", [T, D], F32, kind="ExternalOutput"))
        self.h_x1sp = ap(nc.dram_tensor("x1sp", [128, 16384], F32, kind="Internal"))
        self.h_vsp = ap(nc.dram_tensor("vsp", [128, 16384], BF16, kind="Internal"))
        self.h_dbg1 = ap(nc.dram_tensor("dbg1", [128, 16384], F32, kind="ExternalOutput"))
        self.h_dbg2 = ap(nc.dram_tensor("dbg2", [128, 16384], F32, kind="ExternalOutput"))
        self.h_dbg3 = ap(nc.dram_tensor("dbg3", [128, 16384], F32, kind="ExternalOutput"))
        self.h_dbg4 = ap(nc.dram_tensor("dbg4", [128, 4096], F32, kind="ExternalOutput"))

        sb = b.sb
        self.xf32 = sb("xf32", [128, 16 * 1024], F32)
        self.xbf = sb("xbf", [128, 8 * 2049], BF16)
        self.tab_bf = sb("tab_bf_sb", [128, TB_COLS], BF16)
        self.tab_f32 = sb("tab_f32_sb", [128, TF_COLS], F32)
        self.consts = sb("consts_sb", [128, ccols], F32)
        self.scr = sb("scr", [128, 96], F32)
        self.wbuf = sb("wbuf", [128, 2 * 2816], BF16)
        self.pbf = sb("pbf", [128, PBF_COLS], BF16)
        self.pbf_f32 = self.pbf.bitcast(F32)

        self.psA = b.ps("psA", [128, 2048], F32)
        self.psB = b.ps("psB", [128, 1024], F32)
        self.psD = b.ps("psD", [128, 2048], BF16)
        rem = nc.sbuf_bytes_remaining
        self.sbuf_remaining = rem

    def ccol(self, name, j0=0, w=1):
        off, width = self.clayout[name]
        assert j0 + w <= width
        return self.consts[:, off + j0: off + j0 + w]

    def load_commons(self):
        b = self.b
        b.dma("tabs", self.tab_bf[:, 0:128], self.h_tab_bf[:, 0:128],
              writes=["tb_ident"], queue="SP", tag="ld_ident_bf")
        b.dma("tabs", self.tab_bf[:, TB_PSHIFT:TB_COLS],
              self.h_tab_bf[:, TB_PSHIFT:TB_COLS],
              writes=["tb_pshift", "tb_tail"], queue="SP", tag="ld_pshift_tail")
        b.dma("tabs", self.tab_f32[:], self.h_tab_f32,
              writes=["tab_f32"], queue="SP", tag="ld_tabf32")
        b.dma("tabs", self.consts[:], self.h_consts,
              writes=["consts"], queue="SP", tag="ld_consts")

    def load_rotary(self):
        self.b.dma("tabs", self.tab_bf[:, 128:128 + 4 * 2048],
                   self.h_tab_bf[:, 128:128 + 4 * 2048],
                   writes=["tb_rot", "wqA"], queue="SP", tag="ld_rot")

    @property
    def ident_bf(self):
        return self.tab_bf[:, 0:128]

    @property
    def ident_f32(self):
        return self.tab_f32[:, 0:128]



def stage_a(p: Prog):
    b = p.b
    IDb = p.ident_bf

    k_fm = p.pbf[:, 0:16384]
    xbftm = p.pbf[:, 0:16384]
    v_tm = p.pbf[:, 16384:32768]
    q_q = p.pbf[:, 32768:36864]
    v_stage = p.pbf[:, 36864:38912]
    dump = p.pbf[:, 36864:38912]
    xfm = lambda d0: p.xbf[:, d0 * 2049 + 1: d0 * 2049 + 1 + 2048]
    wq = p.tab_bf[:, 128:128 + 8192]
    wkv0 = p.wbuf[:, 0:1024]
    wkv1 = p.wbuf[:, 1024:2048]
    attn_bf = p.wbuf[:, 2048:4096]
    attnT = [p.wbuf[:, 4096:4224], p.wbuf[:, 4224:4352]]
    stats = p.scr[:, 0:16]
    bq32 = p.scr[:, 16:24]
    eps_col = p.scr[:, 24:25]

    b.op("DVE", lambda v: v.memset(eps_col, 1e-5), writes=["eps"], tag="eps")
    b.op("DVE", lambda v: v.memset(p.scr[:, 25:26], 1e-24), writes=["eps"], tag="e24")

    b.dma("xin", p.xf32[:].rearrange("p (a d) -> p a d", d=1024),
          p.h_x.rearrange("(a p) d -> p a d", p=128),
          writes=[f"xtm{t}" for t in range(NT)], queue="SP", tag="ld_x")

    b.op("DVE", lambda v: v.memset(p.xbf[:].rearrange("p (a c) -> p a c", c=2049)[:, :, 0:1], 0.0),
         writes=["xbf_zc"], tag="zc")

    b.op("DVE", lambda v: v.tensor_copy(xbftm, p.xf32[:]),
         reads=[f"xtm{t}" for t in range(NT)], writes=["xbftm"], tag="cast_xtm")

    for dt in range(ND):
        for half in range(2):
            def em(t, dt=dt, half=half):
                ins = None
                for g in range(8):
                    tt = half * 8 + g
                    ins = t.transpose(p.psD[:, half * 1024 + g * 128: half * 1024 + (g + 1) * 128],
                                      xbftm[:, tt * 1024 + dt * 128: tt * 1024 + dt * 128 + 128],
                                      IDb)
                return ins
            b.op("PE", em, reads=["xbftm", "tb_ident"], writes=[f"psD{half}"], tag=f"trA{dt}_{half}")
            b.op("DVE", lambda v, dt=dt, half=half: v.tensor_copy(
                xfm(dt)[:, half * 1024: half * 1024 + 1024],
                p.psD[:, half * 1024: half * 1024 + 1024]),
                reads=[f"psD{half}"], writes=[f"xfmbf{dt}"], tag=f"evA{dt}_{half}")

    XFM_KEYS = [f"xfmbf{d}" for d in range(ND)]

    b.dma("wA", wq.rearrange("p (k c) -> p k c", c=1024),
          p.h_wqkvT_a[:, 0:1024].rearrange("(k p) c -> p k c", p=128),
          writes=["wqA"], tag="ld_wq")
    b.op("DVE", lambda v: v.tensor_scalar_mul(bq32, p.ccol("bqkv_a", 0, 8), 1.0 / 32.0),
         reads=["consts"], writes=["bq32"], tag="bq32")

    for m in range(8, 24):
        slot = [wkv0, wkv1][m % 2]
        skey = f"wkv{m % 2}"
        b.dma("wA", slot.rearrange("p (k c) -> p k c", c=128),
              p.h_wqkvT_a[:, m * 128:(m + 1) * 128].rearrange("(k p) c -> p k c", p=128),
              writes=[skey], tag=f"wA{m}")
        for tc in range(4):
            bank = tc % 2
            def em(t, slot=slot, tc=tc, bank=bank):
                ins = None
                for k in range(ND):
                    ins = t.matmul(p.psB[:, bank * 512:(bank + 1) * 512],
                                   slot[:, k * 128:(k + 1) * 128],
                                   xfm(k)[:, tc * 512:(tc + 1) * 512],
                                   start=(k == 0), stop=(k == ND - 1))
                return ins
            b.op("PE", em, reads=[skey] + XFM_KEYS, writes=[f"psB{bank}"],
                 tag=f"kv_mm{m}_{tc}")
            if m < 16:
                dst = k_fm[:, (m - 8) * 2048 + tc * 512: (m - 8) * 2048 + (tc + 1) * 512]
                b.op("ACT", lambda s, dst=dst, bank=bank, m=m: s.activation(
                    dst, p.psB[:, bank * 512:(bank + 1) * 512], AF.Identity,
                    bias=p.ccol("bqkv_a", m, 1)),
                    reads=[f"psB{bank}", "consts"],
                    writes=[f"k{m - 8}", "xbftm"], tag=f"ev_k{m}_{tc}")
            else:
                dst = v_stage[:, tc * 512:(tc + 1) * 512]
                b.op("ACT", lambda s, dst=dst, bank=bank, m=m: s.activation(
                    dst, p.psB[:, bank * 512:(bank + 1) * 512], AF.Identity,
                    bias=p.ccol("bqkv_a", m, 1)),
                    reads=[f"psB{bank}", "consts"], writes=[f"vs{tc}"], tag=f"ev_v{m}_{tc}")
        if m >= 16:
            vd = m - 16
            for half in range(2):
                def em(t, half=half):
                    ins = None
                    for g in range(8):
                        tt = half * 8 + g
                        ins = t.transpose(p.psD[:, half * 1024 + g * 128: half * 1024 + (g + 1) * 128],
                                          v_stage[:, tt * 128:(tt + 1) * 128], IDb)
                    return ins
                b.op("PE", em, reads=[f"vs{half * 2}", f"vs{half * 2 + 1}", "tb_ident"],
                     writes=[f"psD{half}"], tag=f"trV{vd}_{half}")
                def evv(v, vd=vd, half=half):
                    ins = None
                    for g in range(8):
                        tt = half * 8 + g
                        ins = v.tensor_copy(
                            v_tm[:, tt * 1024 + vd * 128: tt * 1024 + vd * 128 + 128],
                            p.psD[:, half * 1024 + g * 128: half * 1024 + (g + 1) * 128])
                    return ins
                b.op("DVE", evv, reads=[f"psD{half}"], writes=["vtm", f"vs{half*2}", f"vs{half*2+1}"],
                     tag=f"evV{vd}_{half}")

    if p.debug_out == "dbgA":
        b.dma("dbg", p.h_dbg2, k_fm, reads=[f"k{kk}" for kk in range(8)],
              tag="dbg_k", is_output=True)
        b.dma("dbg", p.h_dbg3, v_tm, reads=["vtm"], tag="dbg_v", is_output=True)

    for qq in range(4):
        for mq in range(8):
            bank = mq % 2
            def em(t, mq=mq, qq=qq, bank=bank):
                ins = None
                for k in range(ND):
                    ins = t.matmul(p.psB[:, bank * 512:(bank + 1) * 512],
                                   wq[:, k * 1024 + mq * 128: k * 1024 + mq * 128 + 128],
                                   xfm(k)[:, qq * 512:(qq + 1) * 512],
                                   start=(k == 0), stop=(k == ND - 1))
                return ins
            b.op("PE", em, reads=["wqA"] + XFM_KEYS, writes=[f"psB{bank}"], tag=f"qmm{qq}_{mq}")
            b.op("ACT", lambda s, mq=mq, bank=bank: s.activation(
                q_q[:, mq * 512:(mq + 1) * 512],
                p.psB[:, bank * 512:(bank + 1) * 512],
                AF.Identity, bias=bq32[:, mq:mq + 1], scale=1.0 / 32.0),
                reads=[f"psB{bank}", "bq32"], writes=["qq"], tag=f"qev{qq}_{mq}")

        if p.debug_out == "dbgA" and qq == 0:
            b.dma("dbg", p.h_dbg1[:, 0:4096], q_q, reads=["qq"], tag="dbg_q", is_output=True)
        for il in range(4):
            i = qq * 4 + il
            ncol = (i + 1) * 128
            nj = i + 1
            def em_s(t, i=i, ncol=ncol, il=il):
                ins = None
                for k in range(ND):
                    q_l = q_q[:, k * 512 + il * 128: k * 512 + il * 128 + 128]
                    for c0 in range(0, ncol, 512):
                        w = min(512, ncol - c0)
                        ins = t.matmul(p.psA[:, c0:c0 + w], q_l,
                                       k_fm[:, k * 2048 + c0: k * 2048 + c0 + w],
                                       start=(k == 0), stop=(k == ND - 1))
                return ins
            b.op("PE", em_s, reads=["qq"] + [f"k{kk}" for kk in range(ND)],
                 writes=["psA0", "psA1", "psA2", "psA3"], tag=f"sc{i}")
            b.op("DVE", lambda v, ncol=ncol: v.tensor_tensor(
                p.psA[:, ncol - 128:ncol], p.psA[:, ncol - 128:ncol],
                p.tab_f32[:, TF_CAUSAL:TF_CAUSAL + 128], ALU.add),
                reads=["psA0", "psA1", "psA2", "psA3", "tab_f32"],
                writes=[f"psA{(ncol - 1) // 512}"], tag=f"cm{i}")
            b.op("DVE", lambda v, ncol=ncol: v.reduce_max(
                stats[:, 0:1], p.psA[:, 0:ncol], axis=AX.X, negate=True),
                reads=["psA0", "psA1", "psA2", "psA3"], writes=["negmax"], tag=f"mx{i}")
            b.op("ACT", lambda s, ncol=ncol: s.activation(
                attn_bf[:, 0:ncol], p.psA[:, 0:ncol], AF.Exp,
                bias=stats[:, 0:1], accum_out=stats[:, 1:2]),
                reads=["psA0", "psA1", "psA2", "psA3", "negmax"],
                writes=["attn", "rowsum", "wkv0", "wkv1"], tag=f"exp{i}")
            for jc in range(nj):
                sl = jc % 2
                b.op("PE", lambda t, jc=jc, sl=sl: t.transpose(
                    p.psD[:, sl * 1024:sl * 1024 + 128],
                    attn_bf[:, jc * 128:(jc + 1) * 128], IDb),
                    reads=["attn", "tb_ident"], writes=[f"psD{sl}"], tag=f"trS{i}_{jc}")
                b.op("DVE", lambda v, sl=sl: v.tensor_copy(
                    attnT[sl], p.psD[:, sl * 1024:sl * 1024 + 128]),
                    reads=[f"psD{sl}"], writes=[f"attnT{sl}"], tag=f"evS{i}_{jc}")
                def em_av(t, jc=jc, nj=nj, sl=sl):
                    ins = None
                    at = attnT[sl]
                    for h in range(2):
                        ins = t.matmul(p.psB[:, h * 512:(h + 1) * 512], at,
                                       v_tm[:, jc * 1024 + h * 512: jc * 1024 + (h + 1) * 512],
                                       start=(jc == 0), stop=(jc == nj - 1))
                    return ins
                b.op("PE", em_av, reads=[f"attnT{sl}", "vtm"], writes=["psB0", "psB1"],
                     tag=f"av{i}_{jc}")
            if p.debug_out == "dbgA" and i == 2:
                b.dma("dbg", p.h_dbg1[:, 4096:4096 + 2048], attn_bf[:, 0:2048],
                      reads=["attn"], tag="dbg_attn", is_output=True)
                def cpy(v):
                    return v.tensor_copy(p.pbf_f32[:, 16384:16384 + 1024], p.psB[:])
                b.op("DVE", cpy, reads=["psB0", "psB1"], writes=["dbgy", "qq"], tag="dbg_ycp")
                b.dma("dbg", p.h_dbg4[:, 0:1024], p.pbf_f32[:, 16384:16384 + 1024],
                      reads=["dbgy"], tag="dbg_y", is_output=True)
                b.dma("dbg", p.h_dbg4[:, 1024:1040], stats, reads=["rowsum", "negmax"],
                      tag="dbg_st", is_output=True)
            xt = p.xf32[:, i * 1024:(i + 1) * 1024]
            b.op("DVE", lambda v: v.reciprocal(stats[:, 2:3], stats[:, 1:2]),
                 reads=["rowsum"], writes=["rsinv"], tag=f"ri{i}")
            b.op("DVE", lambda v, xt=xt: v.scalar_tensor_tensor(
                xt, p.psB[:], stats[:, 2:3], xt, ALU.mult, ALU.add),
                reads=["psB0", "psB1", "rsinv", f"xtm{i}"], writes=[f"xtm{i}"],
                tag=f"res{i}")
            b.op("DVE", lambda v, xt=xt: v.reduce_sum(stats[:, 3:4], xt, axis=AX.X),
                 reads=[f"xtm{i}"], writes=["musum"], tag=f"mu{i}")
            b.op("DVE", lambda v: v.tensor_scalar_mul(stats[:, 4:5], stats[:, 3:4], 1.0 / 1024.0),
                 reads=["musum"], writes=["mu"], tag=f"mu2{i}")
            b.op("DVE", lambda v, xt=xt: v.tensor_scalar(
                xt, xt, stats[:, 4:5], None, ALU.subtract),
                reads=[f"xtm{i}", "mu"], writes=[f"xtm{i}"], tag=f"ctr{i}")
            b.op("ACT", lambda s, xt=xt: s.activation(
                dump[:, 0:1024], xt, AF.Square, accum_out=stats[:, 5:6]),
                reads=[f"xtm{i}"], writes=["sqsum", "vs0", "vs1", "vs2", "vs3"], tag=f"sq{i}")
            b.op("ACT", lambda s: s.activation(stats[:, 6:7], stats[:, 5:6], AF.Ln,
                                               scale=1.0 / 1024.0, bias=eps_col),
                 reads=["sqsum", "eps"], writes=["lnvar"], tag=f"lnv{i}")
            b.op("ACT", lambda s: s.activation(stats[:, 7:8], stats[:, 6:7], AF.Exp,
                                               scale=-0.5),
                 reads=["lnvar"], writes=["rstd"], tag=f"rstd{i}")
            b.op("DVE", lambda v, xt=xt: v.tensor_scalar(
                xt, xt, stats[:, 7:8], None, ALU.mult),
                reads=[f"xtm{i}", "rstd"], writes=[f"xtm{i}"], tag=f"ln{i}")

    if p.debug_out in ("x1", "dbgA"):
        b.dma("out", p.h_out.rearrange("(a p) d -> p a d", p=128),
              p.xf32[:].rearrange("p (a d) -> p a d", d=1024),
              reads=[f"xtm{t}" for t in range(NT)], queue="SP", tag="st_x1",
              is_output=True)





def conv_a2b(p: Prog):
    b = p.b
    M = p.pbf_f32
    oldkeys = [f"k{kk}" for kk in range(8)] + ["vtm", "qq", "xbftm"]
    oldx = [f"xfmbf{d}" for d in range(ND)]
    for dt in range(ND):
        for g4 in range(4):
            bank = g4 % 4
            def em(t, dt=dt, g4=g4, bank=bank):
                ins = None
                for q in range(4):
                    tt = g4 * 4 + q
                    ins = t.transpose(p.psA[:, bank * 512 + q * 128: bank * 512 + (q + 1) * 128],
                                      p.xf32[:, tt * 1024 + dt * 128: tt * 1024 + dt * 128 + 128],
                                      p.ident_f32)
                return ins
            b.op("PE", em,
                 reads=[f"xtm{g4 * 4 + q}" for q in range(4)] + ["tab_f32"],
                 writes=[f"psA{bank}"], tag=f"cv{dt}_{g4}")
            wk = [f"M{dt}"] + (oldkeys if (dt == 0 and g4 == 0) else [])
            b.op("DVE", lambda v, dt=dt, g4=g4, bank=bank: v.tensor_copy(
                M[:, dt * 2048 + g4 * 512: dt * 2048 + (g4 + 1) * 512],
                p.psA[:, bank * 512:(bank + 1) * 512]),
                reads=[f"psA{bank}"], writes=wk, tag=f"cvM{dt}_{g4}")
            wk2 = [f"xb{dt}"] + (oldx if (dt == 0 and g4 == 0) else [])
            b.op("DVE", lambda v, dt=dt, g4=g4, bank=bank: v.tensor_copy(
                p.xbf[:, dt * 2049 + 1 + g4 * 512: dt * 2049 + 1 + (g4 + 1) * 512],
                p.psA[:, bank * 512:(bank + 1) * 512]),
                reads=[f"psA{bank}"], writes=wk2, tag=f"cvX{dt}_{g4}")



def stage_b_layer(p: Prog, li: int):
    b = p.b
    IDb = p.ident_bf
    M = p.pbf_f32
    S = p.xf32.bitcast(BF16)
    Sf = p.xf32
    PT = p.pbf_f32
    lnss = PT[:, 16384:18432]
    statsA = PT[0:1, 16896:17408]
    statsB = PT[0:1, 17408:17920]
    w2tmp = PT[:, 16896:17408]

    otm = S[:, 0:16384]
    qp = S[:, 16384:18432]
    kp = S[:, 18432:20480]
    qrot = S[:, 20480:22528]
    krot = S[:, 22528:24576]
    vstage = S[:, 24576:26624]
    vtm = S[:, 26624:28672]
    tmp2 = S[:, 28672:30720]
    attnS = [S[:, 30720:31104], S[:, 31104:31488]]
    attnTS = [S[:, 31488:31872], S[:, 31872:32256]]
    ofm = S[:, 16384:32768]
    ff = S[:, 0:22528]
    xn = S[:, 22528:30720]
    a_sb = S[:, 30720:31744]
    g_sb = S[:, 31744:32768]
    sqb = [p.pbf[:, 32768:33280], p.pbf[:, 33280:33792]]
    w2slotB = p.pbf[:, 35840:38656]
    wsl = [p.wbuf[:, 0:1024], p.wbuf[:, 1024:2048]]
    w2slotA = p.wbuf[:, 2048:4864]
    wosl = [p.wbuf[:, 2048:3072], p.wbuf[:, 3072:4096]]

    stats = p.scr
    eps_col = p.scr[:, 24:25]
    e24_col = p.scr[:, 25:26]
    negmaxS = [p.scr[:, 26:27], p.scr[:, 27:28]]
    rs = p.scr[:, 32:48]
    rcp = p.scr[:, 48:64]
    qs_col = p.ccol(f"qs{li}")
    ks_col = p.ccol(f"ks{li}")

    def xsh(k):
        def sl(t0, w):
            base = k * 2049 + 1 + t0 - (1 if k >= 4 else 0)
            return p.xbf[:, base: base + w]
        return sl

    L = f"L{li}_"
    XB = [f"xb{d}" for d in range(ND)]
    MK_ = [f"M{d}" for d in range(ND)]

    for pr in range(8):
        pk = f"{L}p{pr}_"
        for sl_i, mbase, dstbuf, dkey in ((0, pr * 128, qp, pk + "qp"),
                                          (1, 1024 + pr * 128, kp, pk + "kp"),
                                          (2, 2048 + pr * 128, vstage, pk + "vs")):
            slot = wsl[sl_i % 2]
            skey = f"wB{sl_i % 2}"
            b.dma("w", slot.rearrange("p (k c) -> p k c", c=128),
                  p.h_wqkv_b[li, :, mbase:mbase + 128].rearrange("(k p) c -> p k c", p=128),
                  writes=[skey], tag=pk + f"wd{sl_i}")
            for tc in range(4):
                bank = 2 + tc % 2
                def em(t, slot=slot, tc=tc, bank=bank):
                    ins = None
                    for k in range(ND):
                        ins = t.matmul(p.psA[:, bank * 512:(bank + 1) * 512],
                                       slot[:, k * 128:(k + 1) * 128],
                                       xsh(k)(tc * 512, 512),
                                       start=(k == 0), stop=(k == ND - 1))
                    return ins
                b.op("PE", em, reads=[skey] + XB, writes=[f"psA{bank}"],
                     tag=pk + f"qkv{sl_i}_{tc}")
                b.op("ACT", lambda s, dstbuf=dstbuf, tc=tc, bank=bank: s.activation(
                    dstbuf[:, tc * 512:(tc + 1) * 512],
                    p.psA[:, bank * 512:(bank + 1) * 512], AF.Identity),
                    reads=[f"psA{bank}"], writes=[dkey], tag=pk + f"qkvev{sl_i}_{tc}")
        for half in range(2):
            def em(t, half=half):
                ins = None
                for g in range(8):
                    tt = half * 8 + g
                    ins = t.transpose(p.psD[:, half * 1024 + g * 128: half * 1024 + (g + 1) * 128],
                                      vstage[:, tt * 128:(tt + 1) * 128], IDb)
                return ins
            b.op("PE", em, reads=[pk + "vs", "tb_ident"], writes=[f"psD{half}"],
                 tag=pk + f"vtr{half}")
            def evv(v, half=half):
                ins = None
                for g in range(8):
                    tt = half * 8 + g
                    ins = v.tensor_copy(vtm[:, tt * 128:(tt + 1) * 128],
                                        p.psD[:, half * 1024 + g * 128: half * 1024 + (g + 1) * 128])
                return ins
            b.op("DVE", evv, reads=[f"psD{half}"], writes=[pk + "vtm"], tag=pk + f"vev{half}")

        T2 = pk + "tmp2"
        for which, buf, rot, scol, ctab, stab, bkey in (
                ("q", qp, qrot, qs_col, TB_COSS8, TB_SINS8, pk + "qp"),
                ("k", kp, krot, ks_col, TB_COSI, TB_SINI, pk + "kp")):
            wk = pk + which
            b.op("DVE", lambda v, buf=buf: v.tensor_tensor(tmp2, buf, buf, ALU.mult),
                 reads=[bkey], writes=[T2], tag=wk + "sq")
            def em(t):
                ins = None
                for c0 in range(0, 2048, 512):
                    ins = t.matmul(p.psA[0:2, c0:c0 + 512],
                                   p.tab_bf[:, TB_ONES2:TB_ONES2 + 2],
                                   tmp2[:, c0:c0 + 512], start=True, stop=True)
                return ins
            b.op("PE", em, reads=[T2, "tb_tail"],
                 writes=["psA0", "psA1", "psA2", "psA3"], tag=wk + "ss")
            b.op("ACT", lambda s: s.activation(lnss[0:2, :], p.psA[0:2, 0:2048], AF.Ln,
                                               bias=e24_col[0:2, :]),
                 reads=["psA0", "psA1", "psA2", "psA3", "eps"], writes=[wk + "lnss"],
                 tag=wk + "ln")
            b.op("ACT", lambda s: s.activation(tmp2[0:2, 0:2048], lnss[0:2, :], AF.Exp,
                                               scale=-0.5),
                 reads=[wk + "lnss"], writes=[T2], tag=wk + "exp")
            def em2(t):
                ins = None
                for c0 in range(0, 2048, 512):
                    ins = t.matmul(p.psA[:, c0:c0 + 512],
                                   p.tab_bf[0:2, TB_BCAST2:TB_BCAST2 + 128],
                                   tmp2[0:2, c0:c0 + 512], start=True, stop=True)
                return ins
            b.op("PE", em2, reads=[T2, "tb_tail"],
                 writes=["psA0", "psA1", "psA2", "psA3"], tag=wk + "bc")
            b.op("DVE", lambda v, buf=buf, scol=scol: v.scalar_tensor_tensor(
                buf, buf, scol, p.psA[:, 0:2048], ALU.mult, ALU.mult),
                reads=[bkey, "consts", "psA0", "psA1", "psA2", "psA3"],
                writes=[bkey], tag=wk + "nrm")
            def em3(t, buf=buf):
                ins = None
                for c0 in range(0, 2048, 512):
                    ins = t.matmul(p.psA[:, c0:c0 + 512],
                                   p.tab_bf[:, TB_PSHIFT:TB_PSHIFT + 128],
                                   buf[:, c0:c0 + 512], start=True, stop=True)
                return ins
            b.op("PE", em3, reads=[bkey, "tb_pshift"],
                 writes=["psA0", "psA1", "psA2", "psA3"], tag=wk + "sh")
            b.op("DVE", lambda v, buf=buf, rot=rot, ctab=ctab: v.tensor_tensor(
                rot, buf, p.tab_bf[:, ctab:ctab + 2048], ALU.mult),
                reads=[bkey, "tb_rot"], writes=[wk + "rc"], tag=wk + "rc")
            b.op("DVE", lambda v, stab=stab: v.tensor_tensor(
                tmp2, p.psA[:, 0:2048], p.tab_bf[:, stab:stab + 2048], ALU.mult),
                reads=["psA0", "psA1", "psA2", "psA3", "tb_rot"],
                writes=[T2], tag=wk + "rs2")
            b.op("DVE", lambda v, rot=rot: v.tensor_tensor(rot, rot, tmp2, ALU.add),
                 reads=[wk + "rc", T2], writes=[pk + which + "rot"], tag=wk + "rot")

        for hh in range(2):
            hbase = hh * 64
            hk = pk + f"h{hh}_"
            for qt in range(NT):
                j0t = max(0, qt - 2)
                nj = qt - j0t + 1
                w = nj * 128
                sl2 = qt % 2
                sb_ = qt % 2
                attn = attnS[sl2]
                attnT = attnTS[sl2]
                negmax = negmaxS[sl2]
                def em(t, qt=qt, j0t=j0t, w=w, sb_=sb_, hbase=hbase):
                    t.matmul(p.psA[:, sb_ * 512: sb_ * 512 + w],
                             qrot[hbase:hbase + 64, qt * 128:(qt + 1) * 128],
                             krot[hbase:hbase + 64, j0t * 128: j0t * 128 + w],
                             start=True, stop=False)
                    return t.matmul(p.psA[:, sb_ * 512: sb_ * 512 + w],
                                    IDb,
                                    p.tab_bf[:, TB_LMASKBF + 384 - w: TB_LMASKBF + 384],
                                    start=False, stop=True)
                b.op("PE", em, reads=[pk + "qrot", pk + "krot", "tb_ident", "tb_tail"],
                     writes=[f"psA{sb_}"], tag=hk + f"sc{qt}")
                b.op("DVE", lambda v, w=w, sb_=sb_, negmax=negmax: v.reduce_max(
                    negmax, p.psA[:, sb_ * 512: sb_ * 512 + w], axis=AX.X, negate=True),
                    reads=[f"psA{sb_}"], writes=[f"negmax{sl2}"], tag=hk + f"mx{qt}")
                b.op("ACT", lambda s, w=w, sb_=sb_, qt=qt, attn=attn, negmax=negmax: s.activation(
                    attn[:, 0:w], p.psA[:, sb_ * 512: sb_ * 512 + w], AF.Exp,
                    bias=negmax, accum_out=rs[:, qt:qt + 1]),
                    reads=[f"psA{sb_}", f"negmax{sl2}"], writes=[f"attn{sl2}", f"rs{qt}"],
                    tag=hk + f"exp{qt}")
                dsl = qt % 2
                def emt(t, nj=nj, dsl=dsl, attn=attn):
                    ins = None
                    for jj in range(nj):
                        ins = t.transpose(
                            p.psD[:, dsl * 1024 + jj * 128: dsl * 1024 + (jj + 1) * 128],
                            attn[:, jj * 128:(jj + 1) * 128], IDb)
                    return ins
                b.op("PE", emt, reads=[f"attn{sl2}", "tb_ident"], writes=[f"psD{dsl}"],
                     tag=hk + f"tr{qt}")
                b.op("DVE", lambda v, w=w, dsl=dsl, attnT=attnT: v.tensor_copy(
                    attnT[:, 0:w], p.psD[:, dsl * 1024: dsl * 1024 + w]),
                    reads=[f"psD{dsl}"], writes=[f"attnT{sl2}"], tag=hk + f"trev{qt}")
                abank = qt // 8
                def emav(t, qt=qt, nj=nj, j0t=j0t, hbase=hbase, attnT=attnT):
                    ins = None
                    for jj in range(nj):
                        ins = t.matmul(
                            p.psB[:, qt * 64:(qt + 1) * 64],
                            attnT[:, jj * 128:(jj + 1) * 128],
                            vtm[:, (j0t + jj) * 128 + hbase: (j0t + jj) * 128 + hbase + 64],
                            start=(jj == 0), stop=(jj == nj - 1))
                    return ins
                b.op("PE", emav, reads=[f"attnT{sl2}", pk + "vtm"], writes=[f"psB{abank}"],
                     tag=hk + f"av{qt}")
            b.op("DVE", lambda v: v.reciprocal(rcp, rs),
                 reads=[f"rs{qt}" for qt in range(NT)], writes=["rcp"], tag=hk + "rcp")
            for qt in range(NT):
                abank = qt // 8
                h = 2 * pr + hh
                b.op("ACT", lambda s, qt=qt, h=h: s.activation(
                    otm[:, qt * 1024 + h * 64: qt * 1024 + h * 64 + 64],
                    p.psB[:, qt * 64:(qt + 1) * 64], AF.Identity,
                    scale=rcp[:, qt:qt + 1]),
                    reads=[f"psB{abank}", "rcp"], writes=[f"otm{qt}"], tag=hk + f"oev{qt}")

    OTM = [f"otm{t}" for t in range(NT)]
    for dd in range(8):
        for half in range(2):
            def em(t, dd=dd, half=half):
                ins = None
                for g in range(8):
                    tt = half * 8 + g
                    ins = t.transpose(p.psD[:, half * 1024 + g * 128: half * 1024 + (g + 1) * 128],
                                      otm[:, tt * 1024 + dd * 128: tt * 1024 + dd * 128 + 128],
                                      IDb)
                return ins
            b.op("PE", em, reads=OTM + ["tb_ident"], writes=[f"psD{half}"],
                 tag=L + f"otr{dd}_{half}")
            wk = [L + f"ofm{dd}"] + (["ofm_init"] + [pk + x for pk in [f"{L}p{q}_" for q in range(8)]
                                                    for x in ("qp", "kp", "vs", "vtm", "qrot", "krot")]
                   if (dd == 0 and half == 0) else [])
            b.op("DVE", lambda v, dd=dd, half=half: v.tensor_copy(
                ofm[:, dd * 2048 + half * 1024: dd * 2048 + (half + 1) * 1024],
                p.psD[:, half * 1024:(half + 1) * 1024]),
                reads=[f"psD{half}"], writes=wk, tag=L + f"oev{dd}_{half}")
    OFM = [L + f"ofm{d}" for d in range(8)]
    for m in range(8):
        slot = wosl[m % 2]
        skey = f"wO{m % 2}"
        b.dma("w", slot.rearrange("p (k c) -> p k c", c=128),
              p.h_wo_b[li, :, m * 128:(m + 1) * 128].rearrange("(k p) c -> p k c", p=128),
              writes=[skey], tag=L + f"wod{m}")
        for tc in range(4):
            bank = tc % 2
            def em(t, slot=slot, tc=tc, bank=bank):
                ins = None
                for k in range(ND):
                    ins = t.matmul(p.psB[:, bank * 512:(bank + 1) * 512],
                                   slot[:, k * 128:(k + 1) * 128],
                                   ofm[:, k * 2048 + tc * 512: k * 2048 + (tc + 1) * 512],
                                   start=(k == 0), stop=(k == ND - 1))
                return ins
            b.op("PE", em, reads=[skey] + OFM, writes=[f"psB{bank}"], tag=L + f"wo{m}_{tc}")
            Msl = M[:, m * 2048 + tc * 512: m * 2048 + (tc + 1) * 512]
            b.op("DVE", lambda v, Msl=Msl, bank=bank: v.tensor_tensor(
                Msl, Msl, p.psB[:, bank * 512:(bank + 1) * 512], ALU.add),
                reads=[f"psB{bank}", f"M{m}"], writes=[f"M{m}"], tag=L + f"wor{m}_{tc}")
            b.op("DVE", lambda v, m=m, tc=tc, Msl=Msl: v.tensor_copy(
                p.xbf[:, m * 2049 + 1 + tc * 512: m * 2049 + 1 + (tc + 1) * 512], Msl),
                reads=[f"M{m}"], writes=[f"xb{m}"], tag=L + f"wox{m}_{tc}")

    if p.debug_out == f"b{li}h":
        b.dma("out", p.h_out.rearrange("(a p) d -> p a d", p=128),
              M[:, 0:16384].rearrange("p (a d) -> p a d", d=1024),
              reads=[f"M{d}" for d in range(ND)], queue="SP", tag="st_Mh",
              is_output=True)
    if p.debug_out == f"dbgB{li}":
        b.dma("dbg", p.h_dbg1, qp, reads=[f"{L}p7_qp"], tag="dbg_qp", is_output=True)
        b.dma("dbg", p.h_dbg2, krot, reads=[f"{L}p7_krot"], tag="dbg_krot", is_output=True)
        b.dma("dbg", p.h_dbg3, otm, reads=[f"otm{t}" for t in range(NT)],
              tag="dbg_otm", is_output=True)

    for half in range(2):
        fk = L + f"f{half}_"
        t0h = half * 1024
        for tci in range(2):
            t0 = t0h + tci * 512
            ck = fk + f"c{tci}_"
            for k in range(ND):
                sq = sqb[k % 2]
                b.op("DVE", lambda v, k=k, t0=t0, sq=sq: v.tensor_tensor(
                    sq, xsh(k)(t0, 512), xsh(k)(t0, 512), ALU.mult),
                    reads=[f"xb{k}"], writes=[L + f"sqb{k % 2}"], tag=ck + f"sq{k}")
                def em(t, k=k, t0=t0, sq=sq):
                    t.matmul(p.psB[0:1, 0:512], p.tab_bf[:, TB_ONES1:TB_ONES1 + 1],
                             xsh(k)(t0, 512), start=(k == 0), stop=(k == ND - 1))
                    return t.matmul(p.psB[0:1, 512:1024], p.tab_bf[:, TB_ONES1:TB_ONES1 + 1],
                                    sq, start=(k == 0), stop=(k == ND - 1))
                b.op("PE", em, reads=[f"xb{k}", L + f"sqb{k % 2}", "tb_tail"],
                     writes=["psB0", "psB1"], tag=ck + f"ssum{k}")
            b.op("DVE", lambda v: v.tensor_scalar_mul(statsA, p.psB[0:1, 0:512], 1.0 / 1024.0),
                 reads=["psB0"], writes=[ck + "mu", "PTtmp"], tag=ck + "mu")
            b.op("DVE", lambda v: v.tensor_scalar_mul(statsB, p.psB[0:1, 512:1024], 1.0 / 1024.0),
                 reads=["psB1"], writes=[ck + "e2"], tag=ck + "e2")
            b.op("DVE", lambda v: v.tensor_tensor(p.psB[0:1, 0:512], statsA, statsA, ALU.mult),
                 reads=[ck + "mu"], writes=["psB0"], tag=ck + "mu2")
            b.op("DVE", lambda v: v.tensor_tensor(statsB, statsB, p.psB[0:1, 0:512], ALU.subtract),
                 reads=[ck + "e2", "psB0"], writes=[ck + "var"], tag=ck + "var")
            b.op("ACT", lambda s: s.activation(statsB, statsB, AF.Ln, bias=eps_col[0:1, :]),
                 reads=[ck + "var", "eps"], writes=[ck + "lnv"], tag=ck + "lnv")
            b.op("ACT", lambda s: s.activation(statsB, statsB, AF.Exp, scale=-0.5),
                 reads=[ck + "lnv"], writes=[ck + "rstd"], tag=ck + "rstd")
            b.op("DVE", lambda v: v.tensor_tensor(statsA, statsA, statsB, ALU.mult),
                 reads=[ck + "mu", ck + "rstd"], writes=[ck + "mur", "PTtmp"], tag=ck + "mur")
            b.op("DVE", lambda v: v.tensor_scalar_mul(statsA, statsA, -1.0),
                 reads=[ck + "mur"], writes=[ck + "B", "PTtmp"], tag=ck + "negB")
            def emb(t):
                t.matmul(p.psB[:, 0:512], p.tab_f32[0:1, TF_ONESROW:TF_ONESROW + 128],
                         statsB, start=True, stop=True)
                return t.matmul(p.psB[:, 512:1024], p.tab_f32[0:1, TF_ONESROW:TF_ONESROW + 128],
                                statsA, start=True, stop=True)
            b.op("PE", emb, reads=[ck + "rstd", ck + "B", "tab_f32"],
                 writes=["psB0", "psB1"], tag=ck + "bc")
            for k in range(ND):
                sq = sqb[k % 2]
                b.op("DVE", lambda v, k=k, t0=t0, sq=sq: v.tensor_tensor(
                    sq, xsh(k)(t0, 512), p.psB[:, 0:512], ALU.mult),
                    reads=[f"xb{k}", "psB0"], writes=[L + f"sqb{k % 2}"], tag=ck + f"xa{k}")
                b.op("DVE", lambda v, sq=sq, k=k: v.tensor_tensor(
                    sq, sq, p.psB[:, 512:1024], ALU.add),
                    reads=[L + f"sqb{k % 2}", "psB1"], writes=[L + f"sqb{k % 2}"],
                    tag=ck + f"xb{k}")
                b.op("ACT", lambda s, k=k, tci=tci, sq=sq: s.activation(
                    xn[:, k * 1024 + tci * 512: k * 1024 + (tci + 1) * 512], sq,
                    AF.Identity, scale=p.ccol(f"lng{li}", k, 1),
                    bias=p.ccol(f"lnb{li}", k, 1)),
                    reads=[L + f"sqb{k % 2}", "consts"],
                    writes=[fk + f"xn{k}"], tag=ck + f"xn{k}")
        XN = [fk + f"xn{k}" for k in range(ND)]
        for r in range(22):
            for part, roff, func in ((0, r, AF.Identity), (1, r + 22, AF.Gelu)):
                slot = wsl[part]
                skey = f"wB{part}"
                b.dma("w", slot.rearrange("p (k c) -> p k c", c=128),
                      p.h_w1p[li, :, roff * 128:(roff + 1) * 128].rearrange("(k p) c -> p k c", p=128),
                      writes=[skey], tag=fk + f"w1d{r}_{part}")
                for tci in range(2):
                    bank = (r * 2 + tci) % 4
                    def em(t, slot=slot, tci=tci, bank=bank):
                        ins = None
                        for k in range(ND):
                            ins = t.matmul(p.psA[:, bank * 512:(bank + 1) * 512],
                                           slot[:, k * 128:(k + 1) * 128],
                                           xn[:, k * 1024 + tci * 512: k * 1024 + (tci + 1) * 512],
                                           start=(k == 0), stop=(k == ND - 1))
                        return ins
                    b.op("PE", em, reads=[skey] + XN, writes=[f"psA{bank}"],
                         tag=fk + f"w1{r}_{part}_{tci}")
                    dst = (a_sb if part == 0 else g_sb)
                    b.op("ACT", lambda s, dst=dst, tci=tci, bank=bank, func=func, roff=roff: s.activation(
                        dst[:, tci * 512:(tci + 1) * 512], p.psA[:, bank * 512:(bank + 1) * 512],
                        func, bias=p.ccol(f"b1{li}", roff, 1)),
                        reads=[f"psA{bank}", "consts"], writes=[fk + ("a" if part == 0 else "g") + str(tci)],
                        tag=fk + f"w1ev{r}_{part}_{tci}")
                    if part == 1:
                        b.op("DVE", lambda v, r=r, tci=tci: v.tensor_tensor(
                            ff[:, r * 1024 + tci * 512: r * 1024 + (tci + 1) * 512],
                            a_sb[:, tci * 512:(tci + 1) * 512],
                            g_sb[:, tci * 512:(tci + 1) * 512], ALU.mult),
                            reads=[fk + f"a{tci}", fk + f"g{tci}"],
                            writes=[fk + f"ff{r}", fk + f"a{tci}", fk + f"g{tci}"],
                            tag=fk + f"ffm{r}_{tci}")
        FF = [fk + f"ff{r}" for r in range(22)]
        for m2 in range(8):
            slot = w2slotA if m2 % 2 == 0 else w2slotB
            skey = f"wW2{m2 % 2}"
            b.dma("w", slot.rearrange("p (k c) -> p k c", c=128),
                  p.h_w2p[li, :, m2 * 128:(m2 + 1) * 128].rearrange("(k p) c -> p k c", p=128),
                  writes=[skey], tag=fk + f"w2d{m2}")
            for tci in range(2):
                bank = tci % 2
                def em(t, slot=slot, tci=tci, bank=bank):
                    ins = None
                    for k2 in range(22):
                        ins = t.matmul(p.psB[:, bank * 512:(bank + 1) * 512],
                                       slot[:, k2 * 128:(k2 + 1) * 128],
                                       ff[:, k2 * 1024 + tci * 512: k2 * 1024 + (tci + 1) * 512],
                                       start=(k2 == 0), stop=(k2 == 21))
                    return ins
                b.op("PE", em, reads=[skey] + FF, writes=[f"psB{bank}"],
                     tag=fk + f"w2{m2}_{tci}")
                b.op("ACT", lambda s, bank=bank, m2=m2: s.activation(
                    w2tmp, p.psB[:, bank * 512:(bank + 1) * 512], AF.Identity,
                    bias=p.ccol(f"b2{li}", m2, 1)),
                    reads=[f"psB{bank}", "consts"], writes=[fk + "w2t", "PTtmp"],
                    tag=fk + f"w2b{m2}_{tci}")
                t0 = t0h + tci * 512
                Msl = M[:, m2 * 2048 + t0: m2 * 2048 + t0 + 512]
                b.op("DVE", lambda v, Msl=Msl: v.tensor_tensor(Msl, Msl, w2tmp, ALU.add),
                     reads=[fk + "w2t", f"M{m2}"], writes=[f"M{m2}"],
                     tag=fk + f"w2r{m2}_{tci}")
                b.op("DVE", lambda v, m2=m2, t0=t0, Msl=Msl: v.tensor_copy(
                    p.xbf[:, m2 * 2049 + 1 + t0: m2 * 2049 + 1 + t0 + 512], Msl),
                    reads=[f"M{m2}"], writes=[f"xb{m2}"], tag=fk + f"w2x{m2}_{tci}")



def block_ln_fm(p: Prog, tag: str, add_x1sp: bool):
    b = p.b
    M = p.pbf_f32
    Sf = p.xf32
    x1c = Sf[:, 0:4096]
    sq0 = p.pbf[:, 32768:33280]
    sq1 = p.pbf[:, 33280:33792]
    statsA = p.pbf_f32[0:1, 16896:17408]
    statsB = p.pbf_f32[0:1, 17408:17920]
    eps_col = p.scr[:, 24:25]
    for tc in range(4):
        t0 = tc * 512
        ck = tag + f"c{tc}_"
        if add_x1sp:
            b.dma("spill", x1c.rearrange("p (a c) -> p a c", c=512),
                  p.h_x1sp.rearrange("p (a c) -> p a c", c=2048)[:, :, t0:t0 + 512],
                  reads=["x1sp"], writes=[tag + "x1c"], queue="SP", tag=ck + "ldx1")
            for d in range(ND):
                Msl = M[:, d * 2048 + t0: d * 2048 + t0 + 512]
                b.op("DVE", lambda v, d=d, Msl=Msl: v.tensor_tensor(
                    Msl, Msl, x1c[:, d * 512:(d + 1) * 512], ALU.add),
                    reads=[tag + "x1c", f"M{d}"], writes=[f"M{d}"], tag=ck + f"add{d}")
        for d in range(ND):
            sq = (sq0 if d % 2 == 0 else sq1)
            Msl = M[:, d * 2048 + t0: d * 2048 + t0 + 512]
            b.op("DVE", lambda v, sq=sq, Msl=Msl: v.tensor_copy(sq, Msl),
                 reads=[f"M{d}"], writes=[tag + f"zb{d % 2}"], tag=ck + f"cz{d}")
            def em(t, sq=sq, d=d):
                return t.matmul(p.psB[0:1, 0:512], p.tab_bf[:, TB_ONES1:TB_ONES1 + 1],
                                sq, start=(d == 0), stop=(d == ND - 1))
            b.op("PE", em, reads=[tag + f"zb{d % 2}", "tb_tail"],
                 writes=["psB0"], tag=ck + f"s1{d}")
            b.op("DVE", lambda v, sq=sq: v.tensor_tensor(sq, sq, sq, ALU.mult),
                 reads=[tag + f"zb{d % 2}"], writes=[tag + f"zb{d % 2}"], tag=ck + f"sq{d}")
            def em2(t, sq=sq, d=d):
                return t.matmul(p.psB[0:1, 512:1024], p.tab_bf[:, TB_ONES1:TB_ONES1 + 1],
                                sq, start=(d == 0), stop=(d == ND - 1))
            b.op("PE", em2, reads=[tag + f"zb{d % 2}", "tb_tail"],
                 writes=["psB1"], tag=ck + f"s2{d}")
        b.op("DVE", lambda v: v.tensor_scalar_mul(statsA, p.psB[0:1, 0:512], 1.0 / 1024.0),
             reads=["psB0"], writes=[ck + "mu", "PTtmp"], tag=ck + "mu")
        b.op("DVE", lambda v: v.tensor_scalar_mul(statsB, p.psB[0:1, 512:1024], 1.0 / 1024.0),
             reads=["psB1"], writes=[ck + "e2"], tag=ck + "e2")
        b.op("DVE", lambda v: v.tensor_tensor(p.psB[0:1, 0:512], statsA, statsA, ALU.mult),
             reads=[ck + "mu"], writes=["psB0"], tag=ck + "mu2")
        b.op("DVE", lambda v: v.tensor_tensor(statsB, statsB, p.psB[0:1, 0:512], ALU.subtract),
             reads=[ck + "e2", "psB0"], writes=[ck + "var"], tag=ck + "var")
        b.op("ACT", lambda s: s.activation(statsB, statsB, AF.Ln, bias=eps_col[0:1, :]),
             reads=[ck + "var", "eps"], writes=[ck + "lnv"], tag=ck + "lnv")
        b.op("ACT", lambda s: s.activation(statsB, statsB, AF.Exp, scale=-0.5),
             reads=[ck + "lnv"], writes=[ck + "rstd"], tag=ck + "rstd")
        b.op("DVE", lambda v: v.tensor_tensor(statsA, statsA, statsB, ALU.mult),
             reads=[ck + "mu", ck + "rstd"], writes=[ck + "mur", "PTtmp"], tag=ck + "mur")
        b.op("DVE", lambda v: v.tensor_scalar_mul(statsA, statsA, -1.0),
             reads=[ck + "mur"], writes=[ck + "B", "PTtmp"], tag=ck + "negB")
        def emb(t):
            t.matmul(p.psB[:, 0:512], p.tab_f32[0:1, TF_ONESROW:TF_ONESROW + 128],
                     statsB, start=True, stop=True)
            return t.matmul(p.psB[:, 512:1024], p.tab_f32[0:1, TF_ONESROW:TF_ONESROW + 128],
                            statsA, start=True, stop=True)
        b.op("PE", emb, reads=[ck + "rstd", ck + "B", "tab_f32"],
             writes=["psB0", "psB1"], tag=ck + "bc")
        for d in range(ND):
            Msl = M[:, d * 2048 + t0: d * 2048 + t0 + 512]
            b.op("DVE", lambda v, Msl=Msl: v.tensor_tensor(Msl, Msl, p.psB[:, 0:512], ALU.mult),
                 reads=[f"M{d}", "psB0"], writes=[f"M{d}"], tag=ck + f"na{d}")
            b.op("DVE", lambda v, Msl=Msl: v.tensor_tensor(Msl, Msl, p.psB[:, 512:1024], ALU.add),
                 reads=[f"M{d}", "psB1"], writes=[f"M{d}"], tag=ck + f"nb{d}")
            b.op("DVE", lambda v, d=d, t0=t0, Msl=Msl: v.tensor_copy(
                p.xbf[:, d * 2049 + 1 + t0: d * 2049 + 1 + t0 + 512], Msl),
                reads=[f"M{d}"], writes=[f"xb{d}"], tag=ck + f"nx{d}")



def stage_c(p: Prog):
    b = p.b
    M = p.pbf_f32
    S = p.xf32.bitcast(BF16)
    Sf = p.xf32
    out_fm = S[:, 0:16384]
    qj = S[:, 16384:18432]
    kj = S[:, 18432:20480]
    vj = S[:, 20480:22528]
    decay_full = Sf[:, 11264:13312]
    prod = S[:, 26624:28672]
    phi_tS = [Sf[:, 14336:14848], Sf[:, 14848:15360]]
    phi_t2S = [S[:, 30720:31744], S[:, 31744:32768]]
    out2c = S[:, 16384:20480]
    vloadS = [S[:, 20480:20992], S[:, 20992:21504]]
    gbuf = S[:, 21504:22528]
    kvbuf = p.pbf[:, 32768:34816]
    runb = p.pbf_f32[:, 17408:19456]
    dcol = p.scr[:, 64:72]
    wsl = [p.wbuf[:, 0:1024], p.wbuf[:, 1024:2048]]

    XB = [f"xb{d}" for d in range(ND)]

    b.op("ACT", lambda s: s.activation(dcol, p.ccol("decay", 0, 8), AF.Sigmoid),
         reads=["consts"], writes=["dcol"], tag="c_dcol")

    for j in range(ND):
        pk = f"C{j}_"
        b.op("DVE", lambda v: v.memset(decay_full, 1.0),
             writes=["Cdec"], tag=pk + "dec0")
        b.op("DVE", lambda v, j=j: v.tensor_scalar(
            decay_full, decay_full, dcol[:, j:j + 1], None, ALU.mult),
            reads=["dcol", "Cdec"], writes=["Cdec"], tag="Cdec")
        for sl_i, mbase, dstbuf, dkey, isphi in ((0, j * 128, qj, "Cq", True),
                                                 (1, 1024 + j * 128, kj, "Ck", True),
                                                 (2, 2048 + j * 128, vj, "Cv", False)):
            slot = wsl[sl_i % 2]
            skey = f"wC{sl_i % 2}"
            b.dma("w", slot.rearrange("p (k c) -> p k c", c=128),
                  p.h_wqkvT_c[:, mbase:mbase + 128].rearrange("(k p) c -> p k c", p=128),
                  writes=[skey], tag=pk + f"wd{sl_i}")
            for tc in range(4):
                bank = tc % 4
                def em(t, slot=slot, tc=tc, bank=bank):
                    ins = None
                    for k in range(ND):
                        ins = t.matmul(p.psA[:, bank * 512:(bank + 1) * 512],
                                       slot[:, k * 128:(k + 1) * 128],
                                       p.xbf[:, k * 2049 + 1 + tc * 512: k * 2049 + 1 + (tc + 1) * 512],
                                       start=(k == 0), stop=(k == ND - 1))
                    return ins
                b.op("PE", em, reads=[skey] + XB, writes=[f"psA{bank}"],
                     tag=pk + f"mm{sl_i}_{tc}")
                if not isphi:
                    b.op("ACT", lambda s, dstbuf=dstbuf, tc=tc, bank=bank: s.activation(
                        dstbuf[:, tc * 512:(tc + 1) * 512],
                        p.psA[:, bank * 512:(bank + 1) * 512], AF.Identity),
                        reads=[f"psA{bank}"], writes=[dkey], tag=pk + f"ev{sl_i}_{tc}")
                else:
                    ps_ = tc % 2
                    phi_t = phi_tS[ps_]
                    phi_t2 = phi_t2S[ps_]
                    b.op("DVE", lambda v, bank=bank, phi_t=phi_t: v.tensor_scalar_min(
                        phi_t, p.psA[:, bank * 512:(bank + 1) * 512], 0.0),
                        reads=[f"psA{bank}"], writes=[f"Cpt{ps_}"], tag=pk + f"mn{sl_i}_{tc}")
                    b.op("ACT", lambda s, phi_t=phi_t, phi_t2=phi_t2: s.activation(
                        phi_t2[:, 0:512], phi_t, AF.Exp),
                        reads=[f"Cpt{ps_}"], writes=[f"Cpe{ps_}"], tag=pk + f"ex{sl_i}_{tc}")
                    b.op("ACT", lambda s, bank=bank, phi_t2=phi_t2: s.activation(
                        phi_t2[:, 512:1024], p.psA[:, bank * 512:(bank + 1) * 512], AF.Relu),
                        reads=[f"psA{bank}"], writes=[f"Cpr{ps_}"], tag=pk + f"rl{sl_i}_{tc}")
                    b.op("DVE", lambda v, dstbuf=dstbuf, tc=tc, phi_t2=phi_t2: v.tensor_tensor(
                        dstbuf[:, tc * 512:(tc + 1) * 512], phi_t2[:, 0:512],
                        phi_t2[:, 512:1024], ALU.add),
                        reads=[f"Cpe{ps_}", f"Cpr{ps_}"], writes=[dkey], tag=pk + f"ph{sl_i}_{tc}")
        b.dma("spill", p.h_vsp[:, j * 2048:(j + 1) * 2048], vj,
              reads=["Cv"], writes=["vsp"], queue="SP", tag=pk + "vspill")
        b.op("DVE", lambda v: v.tensor_tensor(kvbuf, kj, vj, ALU.mult),
             reads=["Ck", "Cv"], writes=["Ckv"], tag="Ckv")
        b.op("DVE", lambda v: v.tensor_tensor_scan(runb, decay_full, kj, 0.0,
                                                   ALU.mult, ALU.add),
             reads=["Cdec", "Ck"], writes=["Crun"], tag=pk + "scank")
        b.op("DVE", lambda v: v.tensor_copy(kj, runb),
             reads=["Crun"], writes=["Ck"], tag="Ck")
        b.op("DVE", lambda v: v.tensor_tensor(prod, qj, kj, ALU.mult),
             reads=["Cq", "Ck"], writes=["Cprod"], tag="Cprod")
        def emd(t):
            ins = None
            for c0 in range(0, 2048, 512):
                ins = t.matmul(p.psA[0:2, c0:c0 + 512], p.tab_bf[:, TB_ONES2:TB_ONES2 + 2],
                               prod[:, c0:c0 + 512], start=True, stop=True)
            return ins
        b.op("PE", emd, reads=["Cprod", "tb_tail"],
             writes=["psA0", "psA1", "psA2", "psA3"], tag=pk + "den")
        b.op("DVE", lambda v: v.tensor_scalar_max(p.psA[0:2, 0:2048], p.psA[0:2, 0:2048], 1e-6),
             reads=["psA0", "psA1", "psA2", "psA3"],
             writes=["psA0", "psA1", "psA2", "psA3"], tag=pk + "clip")
        b.op("DVE", lambda v: v.reciprocal(p.psA[0:2, 0:2048], p.psA[0:2, 0:2048]),
             reads=["psA0", "psA1", "psA2", "psA3"],
             writes=["psA0", "psA1", "psA2", "psA3"], tag=pk + "rcp")
        b.op("DVE", lambda v: v.tensor_copy(prod[0:2, 0:2048], p.psA[0:2, 0:2048]),
             reads=["psA0", "psA1", "psA2", "psA3"], writes=["Cdinv"], tag=pk + "dcp")
        def emb2(t):
            ins = None
            for c0 in range(0, 2048, 512):
                ins = t.matmul(p.psA[:, c0:c0 + 512], p.tab_bf[0:2, TB_BCAST2:TB_BCAST2 + 128],
                               prod[0:2, c0:c0 + 512], start=True, stop=True)
            return ins
        b.op("PE", emb2, reads=["Cdinv", "tb_tail"],
             writes=["psA0", "psA1", "psA2", "psA3"], tag=pk + "dbc")
        b.op("DVE", lambda v: v.tensor_tensor_scan(runb, decay_full, kvbuf, 0.0,
                                                   ALU.mult, ALU.add),
             reads=["Cdec", "Ckv"], writes=["Crun"],
             tag=pk + "scankv")
        b.op("DVE", lambda v: v.tensor_copy(kj, runb),
             reads=["Crun"], writes=["Ck"], tag="Ck")
        b.op("DVE", lambda v: v.tensor_tensor(prod, qj, kj, ALU.mult),
             reads=["Cq", "Ck"], writes=["Cprod", "Cdinv"], tag=pk + "num")
        b.op("DVE", lambda v, j=j: v.tensor_tensor(
            out_fm[:, j * 2048:(j + 1) * 2048], prod, p.psA[:, 0:2048], ALU.mult),
            reads=["Cprod", "psA0", "psA1", "psA2", "psA3"],
            writes=[f"cof{j}"], tag=pk + "out")

    COF = [f"cof{j}" for j in range(ND)]
    b.barrier()
    for tc in range(4):
        gk = f"Cg{tc}_"
        t0 = tc * 512
        for m in range(ND):
            slot = wsl[m % 2]
            skey = f"wC{m % 2}"
            b.dma("w", slot.rearrange("p (k c) -> p k c", c=128),
                  p.h_wgateT[:, m * 128:(m + 1) * 128].rearrange("(k p) c -> p k c", p=128),
                  writes=[skey], tag=gk + f"gd{m}")
            def em(t, slot=slot, t0=t0):
                ins = None
                for k in range(ND):
                    ins = t.matmul(p.psB[:, 0:512], slot[:, k * 128:(k + 1) * 128],
                                   out_fm[:, k * 2048 + t0: k * 2048 + t0 + 512],
                                   start=(k == 0), stop=(k == ND - 1))
                return ins
            b.op("PE", em, reads=[skey] + COF, writes=["psB0"], tag=gk + f"gm{m}")
            b.op("ACT", lambda s, m=m: s.activation(
                gbuf[:, 0:512], p.psB[:, 0:512], AF.Sigmoid,
                bias=p.ccol("bgate", m, 1)),
                reads=["psB0", "consts"], writes=["Cgg"], tag=gk + f"gs{m}")
            vsl = m % 2
            vload = vloadS[vsl]
            b.dma("spill", vload, p.h_vsp[:, m * 2048 + t0: m * 2048 + t0 + 512],
                  reads=["vsp"], writes=[f"Cgv{vsl}"], queue="SP", tag=gk + f"vld{m}")
            b.op("DVE", lambda v, m=m, t0=t0, vload=vload: v.tensor_tensor(
                gbuf[:, 512:1024], out_fm[:, m * 2048 + t0: m * 2048 + t0 + 512],
                vload, ALU.subtract),
                reads=[COF[m], f"Cgv{vsl}"], writes=["Cgd"], tag=gk + f"d{m}")
            b.op("DVE", lambda v: v.tensor_tensor(
                gbuf[:, 512:1024], gbuf[:, 512:1024], gbuf[:, 0:512], ALU.mult),
                reads=["Cgd", "Cgg"], writes=["Cgd"], tag=gk + f"gd2{m}")
            b.op("DVE", lambda v, m=m, vload=vload: v.tensor_tensor(
                out2c[:, m * 512:(m + 1) * 512], gbuf[:, 512:1024], vload, ALU.add),
                reads=["Cgd", f"Cgv{vsl}"], writes=[f"Cgo2_{m}"], tag=gk + f"o2{m}")
        for mo in range(ND):
            slot = wsl[mo % 2]
            skey = f"wC{mo % 2}"
            b.dma("w", slot.rearrange("p (k c) -> p k c", c=128),
                  p.h_woutT[:, mo * 128:(mo + 1) * 128].rearrange("(k p) c -> p k c", p=128),
                  writes=[skey], tag=gk + f"od{mo}")
            def em(t, slot=slot):
                ins = None
                for k in range(ND):
                    ins = t.matmul(p.psB[:, 512:1024], slot[:, k * 128:(k + 1) * 128],
                                   out2c[:, k * 512:(k + 1) * 512],
                                   start=(k == 0), stop=(k == ND - 1))
                return ins
            b.op("PE", em, reads=[skey] + [f"Cgo2_{k}" for k in range(ND)],
                 writes=["psB1"], tag=gk + f"om{mo}")
            Msl = M[:, mo * 2048 + t0: mo * 2048 + t0 + 512]
            b.op("DVE", lambda v, Msl=Msl: v.tensor_tensor(
                Msl, Msl, p.psB[:, 512:1024], ALU.add),
                reads=["psB1", f"M{mo}"], writes=[f"M{mo}"], tag=gk + f"or{mo}")
    b.barrier()
    block_ln_fm(p, "LN3_", add_x1sp=False)



def stage_d(p: Prog):
    b = p.b
    M = p.pbf_f32
    S = p.xf32.bitcast(BF16)
    Sf = p.xf32
    IDf = p.ident_f32

    for dt in range(ND):
        for g4 in range(4):
            bank = g4 % 4
            def em(t, dt=dt, g4=g4, bank=bank):
                ins = None
                for q in range(4):
                    tt = g4 * 4 + q
                    ins = t.transpose(p.psA[:, bank * 512 + q * 128: bank * 512 + (q + 1) * 128],
                                      M[:, dt * 2048 + tt * 128: dt * 2048 + tt * 128 + 128],
                                      IDf)
                return ins
            b.op("PE", em, reads=[f"M{dt}", "tab_f32"], writes=[f"psA{bank}"],
                 tag=f"Dtr{dt}_{g4}")
            def ev(v, dt=dt, g4=g4, bank=bank):
                ins = None
                for q in range(4):
                    tt = g4 * 4 + q
                    ins = v.tensor_copy(
                        Sf[:, tt * 1024 + dt * 128: tt * 1024 + dt * 128 + 128],
                        p.psA[:, bank * 512 + q * 128: bank * 512 + (q + 1) * 128])
                return ins
            b.op("DVE", ev, reads=[f"psA{bank}"], writes=[f"Dtm{g4 * 4}"],
                 tag=f"Dte{dt}_{g4}")
    b.barrier()
    b.dma("spill", M[:, 0:16384], Sf[:, 0:16384],
          reads=[f"Dtm{t}" for t in range(0, 16, 4)],
          writes=[f"M{d}" for d in range(ND)] + [f"OT{t}" for t in range(NT)],
          queue="SP", tag="D_tmcopy")
    b.barrier()

    wz = S[:, 0:16384]
    wc = S[:, 16384:24576]
    zblk = S[:, 24576:32768]
    rm_bf = p.pbf[:, 32768:33792]
    ru_bf = p.pbf[:, 33792:37888]
    wzev = p.pbf[:, 37888:38400]
    eps_col = p.scr[:, 24:25]

    for h in range(4):
        hk = f"Dw{h}_"
        b.dma("w", rm_bf[0:16, :].rearrange("p (r c) -> p r c", c=16),
              p.h_rmT[h].rearrange("r c i -> c r i"),
              writes=["rmb"], tag=hk + "rmd")
        b.dma("w", ru_bf[0:16, :].rearrange("p (r c) -> p r c", c=64),
              p.h_ruT[h].rearrange("r c i -> c r i"),
              writes=["rub"], tag=hk + "rud")
        for B in range(8):
            def em(t, B=B):
                ins = None
                for a in range(8):
                    r = B * 8 + a
                    ins = t.matmul(p.psB[0:16, a * 64:(a + 1) * 64],
                                   rm_bf[0:16, r * 16:(r + 1) * 16],
                                   ru_bf[0:16, r * 64:(r + 1) * 64],
                                   start=True, stop=True)
                return ins
            b.op("PE", em, reads=["rmb", "rub"], writes=["psB0", "psB1"], tag=hk + f"mz{B}")
            b.op("DVE", lambda v: v.tensor_copy(wzev[0:16, :], p.psB[0:16, 0:512]),
                 reads=["psB0", "psB1"], writes=["wzev"], tag=hk + f"ze{B}")
            def em2(t, B=B):
                ins = None
                for a in range(8):
                    ins = t.matmul(p.psA[:, a * 64:(a + 1) * 64],
                                   p.tab_bf[0:16, TB_SEL + a * 128: TB_SEL + (a + 1) * 128],
                                   wzev[0:16, a * 64:(a + 1) * 64],
                                   start=True, stop=True)
                return ins
            b.op("PE", em2, reads=["wzev", "tb_tail"], writes=["psA0"], tag=hk + f"pl{B}")
            def ev(v, h=h, B=B):
                return v.tensor_copy(
                    wz[:, (h * 32 + 4 * B) * 128: (h * 32 + 4 * B) * 128 + 512],
                    p.psA[:, 0:512])
            b.op("DVE", ev, reads=["psA0"], writes=[f"wz{h}_{B}"], tag=hk + f"zf{B}")
    WZ = [f"wz{h}_{B}" for h in range(4) for B in range(8)]

    XB = [f"xb{d}" for d in range(ND)]
    for half in range(2):
        t0h = half * 1024
        for blk in range(16):
            bk = f"Dz{half}_{blk}_"
            b.dma("w", wc.rearrange("p (k c) -> p k c", c=1024),
                  p.h_wcT[blk * 1024:(blk + 1) * 1024, :].rearrange("(k p) c -> p k c", p=128),
                  writes=["wcb"], tag=bk + "wcd")
            for zz in range(8):
                zc = blk * 8 + zz
                xt = (zc % 32) // 4
                zb = zz % 2
                for c0 in range(0, 1024, 512):
                    def emz(t, zc=zc, xt=xt, t0h=t0h, zb=zb, c0=c0):
                        return t.matmul(p.psB[:, zb * 512: zb * 512 + 512],
                                        wz[:, zc * 128:(zc + 1) * 128],
                                        p.xbf[:, xt * 2049 + 1 + t0h + c0: xt * 2049 + 1 + t0h + c0 + 512],
                                        start=True, stop=True)
                    b.op("PE", emz, reads=WZ + XB, writes=[f"psB{zb}"], tag=bk + f"zm{zz}_{c0}")
                    b.op("ACT", lambda s, zz=zz, zb=zb, c0=c0: s.activation(
                        zblk[:, zz * 1024 + c0: zz * 1024 + c0 + 512],
                        p.psB[:, zb * 512: zb * 512 + 512], AF.Gelu),
                        reads=[f"psB{zb}"], writes=[bk + f"z{zz}"], tag=bk + f"zg{zz}_{c0}")
            for ttl in range(8):
                tt = half * 8 + ttl
                po = (ttl % 2) * 1024
                pk2 = ["psA0", "psA1"] if ttl % 2 == 0 else ["psA2", "psA3"]
                def em(t, ttl=ttl, blk=blk, po=po):
                    ins = None
                    for zz in range(8):
                        for c0 in range(0, 1024, 512):
                            ins = t.matmul(
                                p.psA[:, po + c0: po + c0 + 512],
                                zblk[:, zz * 1024 + ttl * 128: zz * 1024 + ttl * 128 + 128],
                                wc[:, zz * 1024 + c0: zz * 1024 + c0 + 512],
                                start=(zz == 0), stop=(zz == 7))
                    return ins
                b.op("PE", em, reads=[bk + f"z{zz}" for zz in range(8)] + ["wcb"],
                     writes=pk2, tag=bk + f"cp{ttl}")
                OT = M[:, tt * 1024:(tt + 1) * 1024]
                b.op("DVE", lambda v, OT=OT, po=po: v.tensor_tensor(
                    OT, OT, p.psA[:, po:po + 1024], ALU.add),
                    reads=pk2 + [f"OT{tt}"], writes=[f"OT{tt}"], tag=bk + f"ca{ttl}")

    for tt in range(NT):
        lk = f"LN4_{tt}_"
        OT = M[:, tt * 1024:(tt + 1) * 1024]
        st = p.scr
        b.op("DVE", lambda v, OT=OT: v.reduce_sum(st[:, 80:81], OT, axis=AX.X),
             reads=[f"OT{tt}"], writes=["l4s"], tag=lk + "s")
        b.op("DVE", lambda v: v.tensor_scalar_mul(st[:, 81:82], st[:, 80:81], 1.0 / 1024.0),
             reads=["l4s"], writes=["l4m"], tag=lk + "m")
        b.op("DVE", lambda v, OT=OT: v.tensor_scalar(OT, OT, st[:, 81:82], None, ALU.subtract),
             reads=[f"OT{tt}", "l4m"], writes=[f"OT{tt}"], tag=lk + "c")
        b.op("ACT", lambda s, OT=OT: s.activation(p.pbf[:, 32768:33792], OT, AF.Square,
                                                  accum_out=st[:, 82:83]),
             reads=[f"OT{tt}"], writes=["l4q", "rmb", "rub"], tag=lk + "q")
        b.op("ACT", lambda s: s.activation(st[:, 83:84], st[:, 82:83], AF.Ln,
                                           scale=1.0 / 1024.0, bias=eps_col),
             reads=["l4q", "eps"], writes=["l4l"], tag=lk + "l")
        b.op("ACT", lambda s: s.activation(st[:, 84:85], st[:, 83:84], AF.Exp, scale=-0.5),
             reads=["l4l"], writes=["l4r"], tag=lk + "r")
        b.op("DVE", lambda v, OT=OT: v.tensor_scalar(OT, OT, st[:, 84:85], None, ALU.mult),
             reads=[f"OT{tt}", "l4r"], writes=[f"OT{tt}"], tag=lk + "n")
    b.dma("out", p.h_out.rearrange("(a p) d -> p a d", p=128),
          M[:, 0:16384].rearrange("p (a d) -> p a d", d=1024),
          reads=[f"OT{t}" for t in range(NT)], queue="SP", tag="st_out",
          is_output=True)


def build_kernel_program(stages=("A",), debug_out=None, clayout=None, nlayers_b=6):
    p = Prog(clayout, stages, debug_out)
    p.load_commons()
    if "A" in stages:
        stage_a(p)
    if "B" in stages:
        p.b.barrier()
        conv_a2b(p)
        p.b.dma("spill", p.h_x1sp, p.pbf_f32[:, 0:16384],
                reads=[f"M{d}" for d in range(ND)], writes=["x1sp"], queue="SP",
                tag="sp_x1")
        p.load_rotary()
        for li in range(nlayers_b):
            p.b.barrier()
            stage_b_layer(p, li)
        if p.debug_out in ("b0", "b1", "b2", "b3", "b4", "b5", "xfm"):
            p.b.dma("out", p.h_out.rearrange("(a p) d -> p a d", p=128),
                    p.pbf_f32[:, 0:16384].rearrange("p (a d) -> p a d", d=1024),
                    reads=[f"M{d}" for d in range(ND)], queue="SP", tag="st_M",
                    is_output=True)
    if "C" in stages:
        p.b.barrier()
        block_ln_fm(p, "LN2_", add_x1sp=True)
        p.b.barrier()
        stage_c(p)
        if p.debug_out in ("x2", "x3"):
            p.b.dma("out", p.h_out.rearrange("(a p) d -> p a d", p=128),
                    p.pbf_f32[:, 0:16384].rearrange("p (a d) -> p a d", d=1024),
                    reads=[f"M{d}" for d in range(ND)], queue="SP", tag="st_M",
                    is_output=True)
    if "D" in stages:
        p.b.barrier()
        stage_d(p)
    p.b.emit()
    return p


def kernel(**inputs):
    shared, clayout = _host_prep(inputs)
    x = np.asarray(inputs["L0_attn_x"], np.float32)
    p = build_kernel_program(stages=("A", "B", "C", "D"), debug_out=None,
                             clayout=clayout)
    in_maps = [dict(shared, x=np.ascontiguousarray(x[i])) for i in range(8)]
    res = run_bass_kernel_spmd(p.nc, in_maps, core_ids=list(range(8)))
    p.b.close()
    out = np.stack([r["out"] for r in res.results], axis=0)
    return out.astype(np.float32)


# version 04
# speedup vs baseline: 1.3374x; 1.3374x over previous
import contextlib
import numpy as np
import ml_dtypes

import concourse.bass as bass
import concourse.mybir as mybir
from concourse.bass_utils import run_bass_kernel_spmd

F32 = mybir.dt.float32
BF16 = mybir.dt.bfloat16
AF = mybir.ActivationFunctionType
ALU = mybir.AluOpType
AX = mybir.AxisListType

T = 2048
D = 1024
NT = 16
ND = 8

ENGINES = ("PE", "DVE", "ACT", "POOL", "SP")


class _Node:
    __slots__ = ("idx", "engine", "kind", "emit", "deps", "pos", "semname", "semval",
                 "need_inc", "incval", "retained", "know", "tag", "cost", "nusers",
                 "ready", "fin")

    def __init__(self, idx, engine, kind, emit, deps, pos, tag):
        self.idx = idx
        self.engine = engine
        self.kind = kind
        self.emit = emit
        self.deps = deps
        self.pos = pos
        self.semname = None
        self.semval = None
        self.need_inc = False
        self.incval = None
        self.retained = None
        self.know = None
        self.tag = tag


class MK:

    SERIAL = False
    ENDCLEAR = True
    SAME_ENGINE_SYNC = True

    def __init__(self, nc):
        self.nc = nc
        self.es = contextlib.ExitStack()
        self.nodes = []
        self.res_w = {}
        self.res_r = {}
        self.eng_pos = {e: 0 for e in ENGINES}
        self.streams = {}
        self.sem_names = ["sem_" + e for e in ENGINES]
        self.out_events = []
        self.last_barrier = []

    def sb(self, name, shape, dtype):
        return self.es.enter_context(self.nc.sbuf_tensor(name, shape, dtype))

    def ps(self, name, shape, dtype):
        return self.es.enter_context(self.nc.psum_tensor(name, shape, dtype))

    def _deps_for(self, reads, writes):
        deps = []
        seen = set()

        def add(n):
            if n is not None and id(n) not in seen:
                seen.add(id(n))
                deps.append(n)
        for r in reads:
            add(self.res_w.get(r))
        for w in writes:
            add(self.res_w.get(w))
            for rd in self.res_r.get(w, ()):
                add(rd)
        return deps

    def _update_res(self, node, reads, writes):
        for r in reads:
            self.res_r.setdefault(r, []).append(node)
        for w in writes:
            self.res_w[w] = node
            self.res_r[w] = []

    def op(self, engine, emit, reads=(), writes=(), tag="", cost=None):
        if engine in ("DVE", "ACT"):
            shadow = [r + "#vs" for r in reads
                      if r.startswith("psA") or r.startswith("psB") or r.startswith("psD")]
            if shadow:
                writes = list(writes) + shadow
        deps = self._deps_for(reads, writes)
        for n in self.last_barrier:
            if n not in deps:
                deps.append(n)
        if MK.SERIAL and self.nodes and self.nodes[-1] not in deps:
            deps.append(self.nodes[-1])
        node = _Node(len(self.nodes), engine, "c", emit, deps, self.eng_pos[engine], tag)
        node.cost = cost if cost is not None else {"PE": 400, "DVE": 700, "ACT": 600,
                                                   "POOL": 500, "SP": 500}[engine]
        self.eng_pos[engine] += 1
        self.nodes.append(node)
        self._update_res(node, reads, writes)
        return node

    def stream(self, name, queue, parity=2):
        if name not in self.streams:
            sems = []
            for i in range(parity):
                sn = f"dma_{name}_{i}"
                sems.append(sn)
                self.sem_names.append(sn)
            self.streams[name] = dict(queue=queue, sems=sems, cnt=[0] * parity,
                                      last=[None] * parity, i=0)
        return self.streams[name]

    def dma(self, stream_name, out_ap, in_ap, reads=(), writes=(), tag="", queue="POOL",
            parity=2, is_output=False):
        st = self.stream(stream_name, queue, parity)
        queue = st["queue"]
        slot = st["i"] % len(st["sems"])
        st["i"] += 1
        deps = self._deps_for(reads, writes)
        for n in self.last_barrier:
            if n not in deps:
                deps.append(n)
        if MK.SERIAL and self.nodes and self.nodes[-1] not in deps:
            deps.append(self.nodes[-1])
        prev = st["last"][slot]
        if prev is not None and prev not in deps:
            deps.append(prev)
        node = _Node(len(self.nodes), queue, "d",
                     (lambda eng, o=out_ap, i=in_ap: eng.dma_start(out=o, in_=i)),
                     deps, self.eng_pos[queue], tag)
        node.cost = 3000
        st["cnt"][slot] += 1
        node.semname = st["sems"][slot]
        node.semval = 16 * st["cnt"][slot]
        st["last"][slot] = node
        self.eng_pos[queue] += 1
        self.nodes.append(node)
        self._update_res(node, reads, writes)
        if is_output:
            self.out_events.append(node)
        return node

    def barrier(self):
        lasts = {}
        for n in reversed(self.nodes):
            if n.engine not in lasts:
                lasts[n.engine] = n
            if len(lasts) == len(ENGINES):
                break
        ln = list(lasts.values())
        for st in self.streams.values():
            for nd in st["last"]:
                if nd is not None and nd not in ln:
                    ln.append(nd)
        self.last_barrier = ln

    RESCHEDULE = True

    def schedule(self):
        import heapq
        n = len(self.nodes)
        users = [[] for _ in range(n)]
        indeg = [0] * n
        for nd in self.nodes:
            for d in nd.deps:
                users[d.idx].append(nd.idx)
                indeg[nd.idx] += 1
        eng_clock = {e: 0.0 for e in ENGINES}
        ready = []
        for nd in self.nodes:
            nd.ready = 0.0
            nd.fin = 0.0
            if indeg[nd.idx] == 0:
                heapq.heappush(ready, (0.0, nd.idx))
        order = []
        nodes = self.nodes
        while ready:
            rt, i = heapq.heappop(ready)
            nd = nodes[i]
            start = max(rt, eng_clock[nd.engine])
            issue = 100.0 if nd.kind == "d" else nd.cost
            eng_clock[nd.engine] = start + issue
            nd.fin = start + nd.cost
            order.append(nd)
            for u in users[i]:
                un = nodes[u]
                if nd.fin > un.ready:
                    un.ready = nd.fin
                indeg[u] -= 1
                if indeg[u] == 0:
                    heapq.heappush(ready, (un.ready, u))
        assert len(order) == n, (len(order), n)
        self.nodes = order
        pos = {e: 0 for e in ENGINES}
        for nd in self.nodes:
            nd.pos = pos[nd.engine]
            pos[nd.engine] += 1

    def finalize(self):
        if MK.RESCHEDULE and not MK.SERIAL:
            self.schedule()
        know_stream = {e: {} for e in ENGINES}
        for node in self.nodes:
            know = dict(know_stream[node.engine])
            retained = []
            for dep in node.deps:
                if (not MK.SAME_ENGINE_SYNC) and dep.kind == "c" and node.kind == "c" \
                        and dep.engine == node.engine:
                    continue
                if dep.kind == "c":
                    key, val = ("E:" + dep.engine), dep.pos
                else:
                    key, val = dep.semname, dep.semval
                if know.get(key, -1) >= val:
                    continue
                retained.append(dep)
                if dep.know:
                    for k2, v2 in dep.know.items():
                        if know.get(k2, -1) < v2:
                            know[k2] = v2
                if know.get(key, -1) < val:
                    know[key] = val
            node.retained = retained
            for dep in retained:
                if dep.kind == "c":
                    dep.need_inc = True
            myknow = dict(know)
            if node.kind == "d":
                myknow[node.semname] = node.semval
            else:
                myknow["E:" + node.engine] = node.pos
            node.know = myknow
            know_stream[node.engine] = know
        cnt = {e: 0 for e in ENGINES}
        for node in self.nodes:
            if node.kind == "c" and node.need_inc:
                cnt[node.engine] += 1
                node.incval = cnt[node.engine]

    def emit(self):
        self.finalize()
        nc = self.nc
        self.sem_names.append("done")
        assert len(self.sem_names) <= 56, len(self.sem_names)
        sems = {}
        for i, sn in enumerate(self.sem_names):
            sems[sn] = self.es.enter_context(nc.semaphore(sn, num=255 - i))
        per_engine = {e: [] for e in ENGINES}
        for node in self.nodes:
            per_engine[node.engine].append(node)
        final_out = [(n.semname, n.semval) for n in self.out_events]
        dma_totals = []
        for st in self.streams.values():
            for sn, c in zip(st["sems"], st["cnt"]):
                if c:
                    dma_totals.append((sn, 16 * c))

        def emit_engine(eng_obj, ename):
            for node in per_engine[ename]:
                for dep in node.retained:
                    if dep.kind == "c":
                        assert dep.incval is not None, (dep.tag, node.tag)
                        eng_obj.wait_ge(sems["sem_" + dep.engine], dep.incval)
                    else:
                        eng_obj.wait_ge(sems[dep.semname], dep.semval)
                instr = node.emit(eng_obj)
                if node.kind == "d":
                    instr.then_inc(sems[node.semname], 16)
                elif node.need_inc:
                    if instr is None:
                        raise RuntimeError(f"op {node.tag} returned no instruction")
                    instr.then_inc(sems["sem_" + node.engine], 1)
            if ename == "POOL":
                for sn, v in final_out:
                    eng_obj.wait_ge(sems[sn], v)
                if False:
                    for sn, v in dma_totals:
                        eng_obj.wait_ge(sems[sn], v)
                    eng_obj.wait_ge(sems["done"], 4)
                    for sn in self.sem_names:
                        eng_obj.sem_clear(sems[sn])
            elif MK.ENDCLEAR:
                eng_obj.sem_inc(sems["done"], 1)

        with nc.Block() as block:
            block.gpsimd(lambda g: emit_engine(g, "POOL"))
            block.sync(lambda s: emit_engine(s, "SP"))
            block.vector(lambda v: emit_engine(v, "DVE"))
            block.scalar(lambda s: emit_engine(s, "ACT"))
            block.tensor(lambda t: emit_engine(t, "PE"))

    def close(self):
        self.es.close()



def _cols(v, n=128):
    v = np.asarray(v, np.float32)
    assert v.size % n == 0
    return np.ascontiguousarray(v.reshape(-1, n).T)


def _host_tables():
    ident_f32 = np.eye(128, dtype=np.float32)
    ident_bf = np.eye(128, dtype=np.float64)
    cm = np.where(np.tril(np.ones((128, 128), bool)), 0.0, -1e30).astype(np.float32)
    il = np.arange(128)[:, None]
    jl = np.arange(384)[None, :]
    ok = (jl >= il) & (jl <= il + 256)
    lm = np.where(ok, 0.0, -1e30).astype(np.float32)
    DIM_HEAD, WINDOW, n = 64, 256, 2048
    half = np.arange(0, DIM_HEAD, 2, dtype=np.float64)
    inv_freq = 1.0 / 10000.0 ** (half / DIM_HEAD)
    t = np.arange(n, dtype=np.float64)
    freqs = t[:, None] * inv_freq[None, :]
    freqs = np.concatenate([freqs, freqs], -1)
    base_scale = (half + 0.4 * DIM_HEAD) / (1.4 * DIM_HEAD)
    power = (t - n // 2) / (WINDOW // 2)
    scale = base_scale[None, :] ** power[:, None]
    scale = np.concatenate([scale, scale], -1)
    cos = np.cos(freqs)
    sin = np.sin(freqs)
    sign = np.where(np.arange(64) < 32, -1.0, 1.0)
    QK = 8.0
    cosS8 = (cos * scale * QK).T
    sinS8 = (sin * scale * QK * sign[None, :]).T
    cosI = (cos / scale).T
    sinI = (sin / scale * sign[None, :]).T

    def dup(a):
        return np.concatenate([a, a], axis=0)
    rot = [dup(a) for a in (cosS8, sinS8, cosI, sinI)]
    P = np.zeros((128, 128), dtype=np.float64)
    for m in range(128):
        base = (m // 64) * 64
        d = m % 64
        P[base + (d + 32) % 64, m] = 1.0
    pidx = np.arange(128)
    ones2 = np.stack([(pidx < 64).astype(np.float64), (pidx >= 64).astype(np.float64)], 1)
    bcast2 = np.zeros((128, 128), np.float64)
    bcast2[0, 0:64] = 1.0
    bcast2[1, 64:128] = 1.0
    ones1 = np.ones((128, 1), np.float64)
    lmaskbf = np.zeros((128, 384), np.float64)
    lmaskbf[:, :] = lm
    sel = np.zeros((128, 8 * 128), np.float64)
    for a in range(8):
        for i in range(16):
            sel[i, a * 128 + 16 * a + i] = 1.0
    tab_bf = np.concatenate([ident_bf] + rot + [P, ones2, bcast2, ones1, lmaskbf, sel],
                            axis=1).astype(ml_dtypes.bfloat16)
    onesrow_f = np.zeros((128, 128), np.float32)
    onesrow_f[0, :] = 1.0
    tab_f32 = np.concatenate([ident_f32, cm, lm, onesrow_f], axis=1).astype(np.float32)
    return np.ascontiguousarray(tab_bf), np.ascontiguousarray(tab_f32)


TB_IDENT = 0
TB_COSS8 = 128
TB_SINS8 = 128 + 2048
TB_COSI = 128 + 2 * 2048
TB_SINI = 128 + 3 * 2048
TB_PSHIFT = 128 + 4 * 2048
TB_ONES2 = TB_PSHIFT + 128
TB_BCAST2 = TB_ONES2 + 2
TB_ONES1 = TB_BCAST2 + 128
TB_LMASKBF = TB_ONES1 + 1
TB_SEL = TB_LMASKBF + 384
TB_COLS = TB_SEL + 8 * 128

TF_IDENT = 0
TF_CAUSAL = 128
TF_LOCALMASK = 256
TF_ONESROW = 256 + 384
TF_COLS = TF_ONESROW + 128


def _host_consts(inputs):
    cols = []
    layout = {}

    def add(name, arr):
        arr = np.asarray(arr, np.float32)
        layout[name] = (sum(c.shape[1] for c in cols), arr.shape[1])
        cols.append(arr)

    add("bqkv_a", _cols(inputs["L0_attn_b_qkv"]))
    for i in range(6):
        add(f"qs{i}", _cols(np.tile(np.asarray(inputs["L0_ffn_q_scale"])[i], 2)))
        add(f"ks{i}", _cols(np.tile(np.asarray(inputs["L0_ffn_k_scale"])[i], 2)))
        add(f"lng{i}", _cols(inputs["L0_ffn_ln_g"][i]))
        add(f"lnb{i}", _cols(inputs["L0_ffn_ln_b"][i]))
        b1 = np.asarray(inputs["L0_ffn_b1"][i], np.float32)
        b1p = np.zeros(5632, np.float32)
        b1p[:2730] = b1[:2730]
        b1p[2816:2816 + 2730] = b1[2730:]
        add(f"b1{i}", _cols(b1p))
        add(f"b2{i}", _cols(inputs["L0_ffn_b2"][i]))
    add("bgate", _cols(inputs["L1_attn_bgate"]))
    add("decay", _cols(np.repeat(np.asarray(inputs["L1_attn_decay_param"]), 64)))
    consts = np.concatenate(cols, axis=1).astype(np.float32)
    return np.ascontiguousarray(consts), layout


def _host_prep(inputs):
    tab_bf, tab_f32 = _host_tables()
    consts, clayout = _host_consts(inputs)
    W1 = np.asarray(inputs["L0_ffn_W1"], np.float32)
    W2 = np.asarray(inputs["L0_ffn_W2"], np.float32)
    W1p = np.zeros((6, 1024, 5632), np.float32)
    W1p[:, :, :2730] = W1[:, :, :2730]
    W1p[:, :, 2816:2816 + 2730] = W1[:, :, 2730:]
    W2p = np.zeros((6, 2816, 1024), np.float32)
    W2p[:, :2730, :] = W2
    shared = {
        "tab_bf": tab_bf,
        "tab_f32": tab_f32,
        "consts": consts,
        "wqkvT_a": np.ascontiguousarray(np.asarray(inputs["L0_attn_W_qkv"], np.float32).T),
        "wqkv_b": np.ascontiguousarray(np.asarray(inputs["L0_ffn_Wqkv"], np.float32)),
        "wo_b": np.ascontiguousarray(np.asarray(inputs["L0_ffn_Wo"], np.float32)),
        "w1p": W1p,
        "w2p": W2p,
        "wqkvT_c": np.ascontiguousarray(np.asarray(inputs["L1_attn_Wqkv"], np.float32).T),
        "wgateT": np.ascontiguousarray(np.asarray(inputs["L1_attn_Wgate"], np.float32).T),
        "woutT": np.ascontiguousarray(np.asarray(inputs["L1_attn_Wout"], np.float32).T),
        "rmT": np.ascontiguousarray(np.asarray(inputs["L1_ffn_rail_maker"], np.float32).transpose(0, 1, 3, 2)),
        "ruT": np.ascontiguousarray(np.asarray(inputs["L1_ffn_rail_up_proj"], np.float32).transpose(0, 1, 3, 2)),
        "wcT": np.ascontiguousarray(np.asarray(inputs["L1_ffn_w_cproj"], np.float32).T),
    }
    return shared, clayout



PBF_COLS = 38912


class Prog:
    def __init__(self, clayout, stages, debug_out=None):
        self.nc = bass.Bass("TRN2", target_bir_lowering=False)
        self.b = MK(self.nc)
        self.clayout = clayout
        self.stages = stages
        self.debug_out = debug_out
        nc, b = self.nc, self.b

        ap = lambda t: t.ap()
        self.h_x = ap(nc.dram_tensor("x", [T, D], F32, kind="ExternalInput"))
        self.h_tab_bf = ap(nc.dram_tensor("tab_bf", [128, TB_COLS], BF16, kind="ExternalInput"))
        self.h_tab_f32 = ap(nc.dram_tensor("tab_f32", [128, TF_COLS], F32, kind="ExternalInput"))
        ccols = max(off + w for off, w in clayout.values())
        self.h_consts = ap(nc.dram_tensor("consts", [128, ccols], F32, kind="ExternalInput"))
        self.h_wqkvT_a = ap(nc.dram_tensor("wqkvT_a", [D, 3 * D], F32, kind="ExternalInput"))
        self.h_wqkv_b = ap(nc.dram_tensor("wqkv_b", [6, D, 3 * D], F32, kind="ExternalInput"))
        self.h_wo_b = ap(nc.dram_tensor("wo_b", [6, D, D], F32, kind="ExternalInput"))
        self.h_w1p = ap(nc.dram_tensor("w1p", [6, D, 5632], F32, kind="ExternalInput"))
        self.h_w2p = ap(nc.dram_tensor("w2p", [6, 2816, D], F32, kind="ExternalInput"))
        self.h_wqkvT_c = ap(nc.dram_tensor("wqkvT_c", [D, 3 * D], F32, kind="ExternalInput"))
        self.h_wgateT = ap(nc.dram_tensor("wgateT", [D, D], F32, kind="ExternalInput"))
        self.h_woutT = ap(nc.dram_tensor("woutT", [D, D], F32, kind="ExternalInput"))
        self.h_rmT = ap(nc.dram_tensor("rmT", [4, 64, 16, 16], F32, kind="ExternalInput"))
        self.h_ruT = ap(nc.dram_tensor("ruT", [4, 64, 16, 64], F32, kind="ExternalInput"))
        self.h_wcT = ap(nc.dram_tensor("wcT", [16 * D, D], F32, kind="ExternalInput"))
        self.h_out = ap(nc.dram_tensor("out", [T, D], F32, kind="ExternalOutput"))
        self.h_x1sp = ap(nc.dram_tensor("x1sp", [128, 16384], F32, kind="Internal"))
        self.h_vsp = ap(nc.dram_tensor("vsp", [128, 16384], BF16, kind="Internal"))
        self.h_dbg1 = ap(nc.dram_tensor("dbg1", [128, 16384], F32, kind="ExternalOutput"))
        self.h_dbg2 = ap(nc.dram_tensor("dbg2", [128, 16384], F32, kind="ExternalOutput"))
        self.h_dbg3 = ap(nc.dram_tensor("dbg3", [128, 16384], F32, kind="ExternalOutput"))
        self.h_dbg4 = ap(nc.dram_tensor("dbg4", [128, 4096], F32, kind="ExternalOutput"))

        sb = b.sb
        self.xf32 = sb("xf32", [128, 16 * 1024], F32)
        self.xbf = sb("xbf", [128, 8 * 2049], BF16)
        self.tab_bf = sb("tab_bf_sb", [128, TB_COLS], BF16)
        self.tab_f32 = sb("tab_f32_sb", [128, TF_COLS], F32)
        self.consts = sb("consts_sb", [128, ccols], F32)
        self.scr = sb("scr", [128, 96], F32)
        self.wbuf = sb("wbuf", [128, 2 * 2816], BF16)
        self.pbf = sb("pbf", [128, PBF_COLS], BF16)
        self.pbf_f32 = self.pbf.bitcast(F32)

        self.psA = b.ps("psA", [128, 2048], F32)
        self.psB = b.ps("psB", [128, 1024], F32)
        self.psD = b.ps("psD", [128, 2048], BF16)
        rem = nc.sbuf_bytes_remaining
        self.sbuf_remaining = rem

    def ccol(self, name, j0=0, w=1):
        off, width = self.clayout[name]
        assert j0 + w <= width
        return self.consts[:, off + j0: off + j0 + w]

    def load_commons(self):
        b = self.b
        b.dma("tabs", self.tab_bf[:, 0:128], self.h_tab_bf[:, 0:128],
              writes=["tb_ident"], queue="SP", tag="ld_ident_bf")
        b.dma("tabs", self.tab_bf[:, TB_PSHIFT:TB_COLS],
              self.h_tab_bf[:, TB_PSHIFT:TB_COLS],
              writes=["tb_pshift", "tb_tail"], queue="SP", tag="ld_pshift_tail")
        b.dma("tabs", self.tab_f32[:], self.h_tab_f32,
              writes=["tab_f32"], queue="SP", tag="ld_tabf32")
        b.dma("tabs", self.consts[:], self.h_consts,
              writes=["consts"], queue="SP", tag="ld_consts")

    def load_rotary(self):
        self.b.dma("tabs", self.tab_bf[:, 128:128 + 4 * 2048],
                   self.h_tab_bf[:, 128:128 + 4 * 2048],
                   writes=["tb_rot", "wqA"], queue="SP", tag="ld_rot")

    @property
    def ident_bf(self):
        return self.tab_bf[:, 0:128]

    @property
    def ident_f32(self):
        return self.tab_f32[:, 0:128]



def stage_a(p: Prog):
    b = p.b
    IDb = p.ident_bf

    k_fm = p.pbf[:, 0:16384]
    xbftm = p.pbf[:, 0:16384]
    v_tm = p.pbf[:, 16384:32768]
    q_q = p.pbf[:, 32768:36864]
    v_stage = p.pbf[:, 36864:38912]
    dump = p.pbf[:, 36864:38912]
    xfm = lambda d0: p.xbf[:, d0 * 2049 + 1: d0 * 2049 + 1 + 2048]
    wq = p.tab_bf[:, 128:128 + 8192]
    wkv0 = p.wbuf[:, 0:1024]
    wkv1 = p.wbuf[:, 1024:2048]
    attn_bf = p.wbuf[:, 2048:4096]
    attnT = [p.wbuf[:, 4096:4224], p.wbuf[:, 4224:4352]]
    stats = p.scr[:, 0:16]
    bq32 = p.scr[:, 16:24]
    eps_col = p.scr[:, 24:25]

    b.op("DVE", lambda v: v.memset(eps_col, 1e-5), writes=["eps"], tag="eps")
    b.op("DVE", lambda v: v.memset(p.scr[:, 25:26], 1e-24), writes=["eps"], tag="e24")

    b.dma("xin", p.xf32[:].rearrange("p (a d) -> p a d", d=1024),
          p.h_x.rearrange("(a p) d -> p a d", p=128),
          writes=[f"xtm{t}" for t in range(NT)], queue="SP", tag="ld_x")

    b.op("DVE", lambda v: v.memset(p.xbf[:].rearrange("p (a c) -> p a c", c=2049)[:, :, 0:1], 0.0),
         writes=["xbf_zc"], tag="zc")

    b.op("DVE", lambda v: v.tensor_copy(xbftm, p.xf32[:]),
         reads=[f"xtm{t}" for t in range(NT)], writes=["xbftm"], tag="cast_xtm")

    for dt in range(ND):
        for half in range(2):
            def em(t, dt=dt, half=half):
                ins = None
                for g in range(8):
                    tt = half * 8 + g
                    ins = t.transpose(p.psD[:, half * 1024 + g * 128: half * 1024 + (g + 1) * 128],
                                      xbftm[:, tt * 1024 + dt * 128: tt * 1024 + dt * 128 + 128],
                                      IDb)
                return ins
            b.op("PE", em, reads=["xbftm", "tb_ident"], writes=[f"psD{half}"], tag=f"trA{dt}_{half}")
            b.op("DVE", lambda v, dt=dt, half=half: v.tensor_copy(
                xfm(dt)[:, half * 1024: half * 1024 + 1024],
                p.psD[:, half * 1024: half * 1024 + 1024]),
                reads=[f"psD{half}"], writes=[f"xfmbf{dt}"], tag=f"evA{dt}_{half}")

    XFM_KEYS = [f"xfmbf{d}" for d in range(ND)]

    b.dma("wA", wq.rearrange("p (k c) -> p k c", c=1024),
          p.h_wqkvT_a[:, 0:1024].rearrange("(k p) c -> p k c", p=128),
          writes=["wqA"], tag="ld_wq")
    b.op("DVE", lambda v: v.tensor_scalar_mul(bq32, p.ccol("bqkv_a", 0, 8), 1.0 / 32.0),
         reads=["consts"], writes=["bq32"], tag="bq32")

    for m in range(8, 24):
        slot = [wkv0, wkv1][m % 2]
        skey = f"wkv{m % 2}"
        b.dma("wA", slot.rearrange("p (k c) -> p k c", c=128),
              p.h_wqkvT_a[:, m * 128:(m + 1) * 128].rearrange("(k p) c -> p k c", p=128),
              writes=[skey], tag=f"wA{m}")
        for tc in range(4):
            bank = tc % 2
            def em(t, slot=slot, tc=tc, bank=bank):
                ins = None
                for k in range(ND):
                    ins = t.matmul(p.psB[:, bank * 512:(bank + 1) * 512],
                                   slot[:, k * 128:(k + 1) * 128],
                                   xfm(k)[:, tc * 512:(tc + 1) * 512],
                                   start=(k == 0), stop=(k == ND - 1))
                return ins
            b.op("PE", em, reads=[skey] + XFM_KEYS, writes=[f"psB{bank}"],
                 tag=f"kv_mm{m}_{tc}")
            if m < 16:
                dst = k_fm[:, (m - 8) * 2048 + tc * 512: (m - 8) * 2048 + (tc + 1) * 512]
                b.op("ACT", lambda s, dst=dst, bank=bank, m=m: s.activation(
                    dst, p.psB[:, bank * 512:(bank + 1) * 512], AF.Identity,
                    bias=p.ccol("bqkv_a", m, 1)),
                    reads=[f"psB{bank}", "consts"],
                    writes=[f"k{m - 8}", "xbftm"], tag=f"ev_k{m}_{tc}")
            else:
                dst = v_stage[:, tc * 512:(tc + 1) * 512]
                b.op("ACT", lambda s, dst=dst, bank=bank, m=m: s.activation(
                    dst, p.psB[:, bank * 512:(bank + 1) * 512], AF.Identity,
                    bias=p.ccol("bqkv_a", m, 1)),
                    reads=[f"psB{bank}", "consts"], writes=[f"vs{tc}"], tag=f"ev_v{m}_{tc}")
        if m >= 16:
            vd = m - 16
            for half in range(2):
                def em(t, half=half):
                    ins = None
                    for g in range(8):
                        tt = half * 8 + g
                        ins = t.transpose(p.psD[:, half * 1024 + g * 128: half * 1024 + (g + 1) * 128],
                                          v_stage[:, tt * 128:(tt + 1) * 128], IDb)
                    return ins
                b.op("PE", em, reads=[f"vs{half * 2}", f"vs{half * 2 + 1}", "tb_ident"],
                     writes=[f"psD{half}"], tag=f"trV{vd}_{half}")
                def evv(v, vd=vd, half=half):
                    ins = None
                    for g in range(8):
                        tt = half * 8 + g
                        ins = v.tensor_copy(
                            v_tm[:, tt * 1024 + vd * 128: tt * 1024 + vd * 128 + 128],
                            p.psD[:, half * 1024 + g * 128: half * 1024 + (g + 1) * 128])
                    return ins
                b.op("DVE", evv, reads=[f"psD{half}"], writes=["vtm", f"vs{half*2}", f"vs{half*2+1}"],
                     tag=f"evV{vd}_{half}")

    if p.debug_out == "dbgA":
        b.dma("dbg", p.h_dbg2, k_fm, reads=[f"k{kk}" for kk in range(8)],
              tag="dbg_k", is_output=True)
        b.dma("dbg", p.h_dbg3, v_tm, reads=["vtm"], tag="dbg_v", is_output=True)

    for qq in range(4):
        for mq in range(8):
            bank = mq % 2
            def em(t, mq=mq, qq=qq, bank=bank):
                ins = None
                for k in range(ND):
                    ins = t.matmul(p.psB[:, bank * 512:(bank + 1) * 512],
                                   wq[:, k * 1024 + mq * 128: k * 1024 + mq * 128 + 128],
                                   xfm(k)[:, qq * 512:(qq + 1) * 512],
                                   start=(k == 0), stop=(k == ND - 1))
                return ins
            b.op("PE", em, reads=["wqA"] + XFM_KEYS, writes=[f"psB{bank}"], tag=f"qmm{qq}_{mq}")
            b.op("ACT", lambda s, mq=mq, bank=bank: s.activation(
                q_q[:, mq * 512:(mq + 1) * 512],
                p.psB[:, bank * 512:(bank + 1) * 512],
                AF.Identity, bias=bq32[:, mq:mq + 1], scale=1.0 / 32.0),
                reads=[f"psB{bank}", "bq32"], writes=["qq"], tag=f"qev{qq}_{mq}")

        if p.debug_out == "dbgA" and qq == 0:
            b.dma("dbg", p.h_dbg1[:, 0:4096], q_q, reads=["qq"], tag="dbg_q", is_output=True)
        for il in range(4):
            i = qq * 4 + il
            ncol = (i + 1) * 128
            nj = i + 1
            def em_s(t, i=i, ncol=ncol, il=il):
                ins = None
                for k in range(ND):
                    q_l = q_q[:, k * 512 + il * 128: k * 512 + il * 128 + 128]
                    for c0 in range(0, ncol, 512):
                        w = min(512, ncol - c0)
                        ins = t.matmul(p.psA[:, c0:c0 + w], q_l,
                                       k_fm[:, k * 2048 + c0: k * 2048 + c0 + w],
                                       start=(k == 0), stop=(k == ND - 1))
                return ins
            b.op("PE", em_s, reads=["qq"] + [f"k{kk}" for kk in range(ND)],
                 writes=["psA0", "psA1", "psA2", "psA3"], tag=f"sc{i}")
            b.op("DVE", lambda v, ncol=ncol: v.tensor_tensor(
                p.psA[:, ncol - 128:ncol], p.psA[:, ncol - 128:ncol],
                p.tab_f32[:, TF_CAUSAL:TF_CAUSAL + 128], ALU.add),
                reads=["psA0", "psA1", "psA2", "psA3", "tab_f32"],
                writes=[f"psA{(ncol - 1) // 512}"], tag=f"cm{i}")
            b.op("DVE", lambda v, ncol=ncol: v.reduce_max(
                stats[:, 0:1], p.psA[:, 0:ncol], axis=AX.X, negate=True),
                reads=["psA0", "psA1", "psA2", "psA3"], writes=["negmax"], tag=f"mx{i}")
            b.op("ACT", lambda s, ncol=ncol: s.activation(
                attn_bf[:, 0:ncol], p.psA[:, 0:ncol], AF.Exp,
                bias=stats[:, 0:1], accum_out=stats[:, 1:2]),
                reads=["psA0", "psA1", "psA2", "psA3", "negmax"],
                writes=["attn", "rowsum", "wkv0", "wkv1"], tag=f"exp{i}")
            for jc in range(nj):
                sl = jc % 2
                b.op("PE", lambda t, jc=jc, sl=sl: t.transpose(
                    p.psD[:, sl * 1024:sl * 1024 + 128],
                    attn_bf[:, jc * 128:(jc + 1) * 128], IDb),
                    reads=["attn", "tb_ident"], writes=[f"psD{sl}"], tag=f"trS{i}_{jc}")
                b.op("DVE", lambda v, sl=sl: v.tensor_copy(
                    attnT[sl], p.psD[:, sl * 1024:sl * 1024 + 128]),
                    reads=[f"psD{sl}"], writes=[f"attnT{sl}"], tag=f"evS{i}_{jc}")
                def em_av(t, jc=jc, nj=nj, sl=sl):
                    ins = None
                    at = attnT[sl]
                    for h in range(2):
                        ins = t.matmul(p.psB[:, h * 512:(h + 1) * 512], at,
                                       v_tm[:, jc * 1024 + h * 512: jc * 1024 + (h + 1) * 512],
                                       start=(jc == 0), stop=(jc == nj - 1))
                    return ins
                b.op("PE", em_av, reads=[f"attnT{sl}", "vtm"], writes=["psB0", "psB1"],
                     tag=f"av{i}_{jc}")
            if p.debug_out == "dbgA" and i == 2:
                b.dma("dbg", p.h_dbg1[:, 4096:4096 + 2048], attn_bf[:, 0:2048],
                      reads=["attn"], tag="dbg_attn", is_output=True)
                def cpy(v):
                    return v.tensor_copy(p.pbf_f32[:, 16384:16384 + 1024], p.psB[:])
                b.op("DVE", cpy, reads=["psB0", "psB1"], writes=["dbgy", "qq"], tag="dbg_ycp")
                b.dma("dbg", p.h_dbg4[:, 0:1024], p.pbf_f32[:, 16384:16384 + 1024],
                      reads=["dbgy"], tag="dbg_y", is_output=True)
                b.dma("dbg", p.h_dbg4[:, 1024:1040], stats, reads=["rowsum", "negmax"],
                      tag="dbg_st", is_output=True)
            xt = p.xf32[:, i * 1024:(i + 1) * 1024]
            b.op("DVE", lambda v: v.reciprocal(stats[:, 2:3], stats[:, 1:2]),
                 reads=["rowsum"], writes=["rsinv"], tag=f"ri{i}")
            b.op("DVE", lambda v, xt=xt: v.scalar_tensor_tensor(
                xt, p.psB[:], stats[:, 2:3], xt, ALU.mult, ALU.add),
                reads=["psB0", "psB1", "rsinv", f"xtm{i}"], writes=[f"xtm{i}"],
                tag=f"res{i}")
            b.op("DVE", lambda v, xt=xt: v.reduce_sum(stats[:, 3:4], xt, axis=AX.X),
                 reads=[f"xtm{i}"], writes=["musum"], tag=f"mu{i}")
            b.op("DVE", lambda v: v.tensor_scalar_mul(stats[:, 4:5], stats[:, 3:4], 1.0 / 1024.0),
                 reads=["musum"], writes=["mu"], tag=f"mu2{i}")
            b.op("DVE", lambda v, xt=xt: v.tensor_scalar(
                xt, xt, stats[:, 4:5], None, ALU.subtract),
                reads=[f"xtm{i}", "mu"], writes=[f"xtm{i}"], tag=f"ctr{i}")
            b.op("ACT", lambda s, xt=xt: s.activation(
                dump[:, 0:1024], xt, AF.Square, accum_out=stats[:, 5:6]),
                reads=[f"xtm{i}"], writes=["sqsum", "vs0", "vs1", "vs2", "vs3"], tag=f"sq{i}")
            b.op("ACT", lambda s: s.activation(stats[:, 6:7], stats[:, 5:6], AF.Ln,
                                               scale=1.0 / 1024.0, bias=eps_col),
                 reads=["sqsum", "eps"], writes=["lnvar"], tag=f"lnv{i}")
            b.op("ACT", lambda s: s.activation(stats[:, 7:8], stats[:, 6:7], AF.Exp,
                                               scale=-0.5),
                 reads=["lnvar"], writes=["rstd"], tag=f"rstd{i}")
            b.op("DVE", lambda v, xt=xt: v.tensor_scalar(
                xt, xt, stats[:, 7:8], None, ALU.mult),
                reads=[f"xtm{i}", "rstd"], writes=[f"xtm{i}"], tag=f"ln{i}")

    if p.debug_out in ("x1", "dbgA"):
        b.dma("out", p.h_out.rearrange("(a p) d -> p a d", p=128),
              p.xf32[:].rearrange("p (a d) -> p a d", d=1024),
              reads=[f"xtm{t}" for t in range(NT)], queue="SP", tag="st_x1",
              is_output=True)





def conv_a2b(p: Prog):
    b = p.b
    M = p.pbf_f32
    oldkeys = [f"k{kk}" for kk in range(8)] + ["vtm", "qq", "xbftm"]
    oldx = [f"xfmbf{d}" for d in range(ND)]
    for dt in range(ND):
        for g4 in range(4):
            bank = g4 % 4
            def em(t, dt=dt, g4=g4, bank=bank):
                ins = None
                for q in range(4):
                    tt = g4 * 4 + q
                    ins = t.transpose(p.psA[:, bank * 512 + q * 128: bank * 512 + (q + 1) * 128],
                                      p.xf32[:, tt * 1024 + dt * 128: tt * 1024 + dt * 128 + 128],
                                      p.ident_f32)
                return ins
            b.op("PE", em,
                 reads=[f"xtm{g4 * 4 + q}" for q in range(4)] + ["tab_f32"],
                 writes=[f"psA{bank}"], tag=f"cv{dt}_{g4}")
            wk = [f"M{dt}"] + (oldkeys if (dt == 0 and g4 == 0) else [])
            b.op("DVE", lambda v, dt=dt, g4=g4, bank=bank: v.tensor_copy(
                M[:, dt * 2048 + g4 * 512: dt * 2048 + (g4 + 1) * 512],
                p.psA[:, bank * 512:(bank + 1) * 512]),
                reads=[f"psA{bank}"], writes=wk, tag=f"cvM{dt}_{g4}")
            wk2 = [f"xb{dt}"] + (oldx if (dt == 0 and g4 == 0) else [])
            b.op("DVE", lambda v, dt=dt, g4=g4, bank=bank: v.tensor_copy(
                p.xbf[:, dt * 2049 + 1 + g4 * 512: dt * 2049 + 1 + (g4 + 1) * 512],
                p.psA[:, bank * 512:(bank + 1) * 512]),
                reads=[f"psA{bank}"], writes=wk2, tag=f"cvX{dt}_{g4}")



def stage_b_layer(p: Prog, li: int):
    b = p.b
    IDb = p.ident_bf
    M = p.pbf_f32
    S = p.xf32.bitcast(BF16)
    Sf = p.xf32
    PT = p.pbf_f32
    lnss = PT[:, 16384:18432]
    statsA = PT[0:1, 16896:17408]
    statsB = PT[0:1, 17408:17920]
    w2tmp = PT[:, 16896:17408]

    otm = S[:, 0:16384]
    qp = S[:, 16384:18432]
    kp = S[:, 18432:20480]
    qrot = S[:, 20480:22528]
    krot = S[:, 22528:24576]
    vstage = S[:, 24576:26624]
    vtm = S[:, 26624:28672]
    tmp2 = S[:, 28672:30720]
    attnS = [S[:, 30720:31104], S[:, 31104:31488]]
    attnTS = [S[:, 31488:31872], S[:, 31872:32256]]
    ofm = S[:, 16384:32768]
    ff = S[:, 0:22528]
    xn = S[:, 22528:30720]
    a_sb = S[:, 30720:31744]
    g_sb = S[:, 31744:32768]
    sqb = [p.pbf[:, 32768:33280], p.pbf[:, 33280:33792]]
    w2slotB = p.pbf[:, 35840:38656]
    wsl = [p.wbuf[:, 0:1024], p.wbuf[:, 1024:2048]]
    w2slotA = p.wbuf[:, 2048:4864]
    wosl = [p.wbuf[:, 2048:3072], p.wbuf[:, 3072:4096]]

    stats = p.scr
    eps_col = p.scr[:, 24:25]
    e24_col = p.scr[:, 25:26]
    negmaxS = [p.scr[:, 26:27], p.scr[:, 27:28]]
    rs = p.scr[:, 32:48]
    rcp = p.scr[:, 48:64]
    qs_col = p.ccol(f"qs{li}")
    ks_col = p.ccol(f"ks{li}")

    def xsh(k):
        def sl(t0, w):
            base = k * 2049 + 1 + t0 - (1 if k >= 4 else 0)
            return p.xbf[:, base: base + w]
        return sl

    L = f"L{li}_"
    XB = [f"xb{d}" for d in range(ND)]
    MK_ = [f"M{d}" for d in range(ND)]

    for pr in range(8):
        pk = f"{L}p{pr}_"
        for sl_i, mbase, dstbuf, dkey in ((0, pr * 128, qp, pk + "qp"),
                                          (1, 1024 + pr * 128, kp, pk + "kp"),
                                          (2, 2048 + pr * 128, vstage, pk + "vs")):
            slot = wsl[sl_i % 2]
            skey = f"wB{sl_i % 2}"
            b.dma("w", slot.rearrange("p (k c) -> p k c", c=128),
                  p.h_wqkv_b[li, :, mbase:mbase + 128].rearrange("(k p) c -> p k c", p=128),
                  writes=[skey], tag=pk + f"wd{sl_i}")
            for tc in range(4):
                bank = 2 + tc % 2
                def em(t, slot=slot, tc=tc, bank=bank):
                    ins = None
                    for k in range(ND):
                        ins = t.matmul(p.psA[:, bank * 512:(bank + 1) * 512],
                                       slot[:, k * 128:(k + 1) * 128],
                                       xsh(k)(tc * 512, 512),
                                       start=(k == 0), stop=(k == ND - 1))
                    return ins
                b.op("PE", em, reads=[skey] + XB, writes=[f"psA{bank}"],
                     tag=pk + f"qkv{sl_i}_{tc}")
                b.op("ACT", lambda s, dstbuf=dstbuf, tc=tc, bank=bank: s.activation(
                    dstbuf[:, tc * 512:(tc + 1) * 512],
                    p.psA[:, bank * 512:(bank + 1) * 512], AF.Identity),
                    reads=[f"psA{bank}"], writes=[dkey], tag=pk + f"qkvev{sl_i}_{tc}")
        for half in range(2):
            def em(t, half=half):
                ins = None
                for g in range(8):
                    tt = half * 8 + g
                    ins = t.transpose(p.psD[:, half * 1024 + g * 128: half * 1024 + (g + 1) * 128],
                                      vstage[:, tt * 128:(tt + 1) * 128], IDb)
                return ins
            b.op("PE", em, reads=[pk + "vs", "tb_ident"], writes=[f"psD{half}"],
                 tag=pk + f"vtr{half}")
            def evv(v, half=half):
                ins = None
                for g in range(8):
                    tt = half * 8 + g
                    ins = v.tensor_copy(vtm[:, tt * 128:(tt + 1) * 128],
                                        p.psD[:, half * 1024 + g * 128: half * 1024 + (g + 1) * 128])
                return ins
            b.op("DVE", evv, reads=[f"psD{half}"], writes=[pk + "vtm"], tag=pk + f"vev{half}")

        T2 = pk + "tmp2"
        for which, buf, rot, scol, ctab, stab, bkey in (
                ("q", qp, qrot, qs_col, TB_COSS8, TB_SINS8, pk + "qp"),
                ("k", kp, krot, ks_col, TB_COSI, TB_SINI, pk + "kp")):
            wk = pk + which
            b.op("DVE", lambda v, buf=buf: v.tensor_tensor(tmp2, buf, buf, ALU.mult),
                 reads=[bkey], writes=[T2], tag=wk + "sq")
            def em(t):
                ins = None
                for c0 in range(0, 2048, 512):
                    ins = t.matmul(p.psA[0:2, c0:c0 + 512],
                                   p.tab_bf[:, TB_ONES2:TB_ONES2 + 2],
                                   tmp2[:, c0:c0 + 512], start=True, stop=True)
                return ins
            b.op("PE", em, reads=[T2, "tb_tail"],
                 writes=["psA0", "psA1", "psA2", "psA3"], tag=wk + "ss")
            b.op("ACT", lambda s: s.activation(lnss[0:2, :], p.psA[0:2, 0:2048], AF.Ln,
                                               bias=e24_col[0:2, :]),
                 reads=["psA0", "psA1", "psA2", "psA3", "eps"], writes=[wk + "lnss"],
                 tag=wk + "ln")
            b.op("ACT", lambda s: s.activation(tmp2[0:2, 0:2048], lnss[0:2, :], AF.Exp,
                                               scale=-0.5),
                 reads=[wk + "lnss"], writes=[T2], tag=wk + "exp")
            def em2(t):
                ins = None
                for c0 in range(0, 2048, 512):
                    ins = t.matmul(p.psA[:, c0:c0 + 512],
                                   p.tab_bf[0:2, TB_BCAST2:TB_BCAST2 + 128],
                                   tmp2[0:2, c0:c0 + 512], start=True, stop=True)
                return ins
            b.op("PE", em2, reads=[T2, "tb_tail"],
                 writes=["psA0", "psA1", "psA2", "psA3"], tag=wk + "bc")
            b.op("DVE", lambda v, buf=buf, scol=scol: v.scalar_tensor_tensor(
                buf, buf, scol, p.psA[:, 0:2048], ALU.mult, ALU.mult),
                reads=[bkey, "consts", "psA0", "psA1", "psA2", "psA3"],
                writes=[bkey], tag=wk + "nrm")
            def em3(t, buf=buf):
                ins = None
                for c0 in range(0, 2048, 512):
                    ins = t.matmul(p.psA[:, c0:c0 + 512],
                                   p.tab_bf[:, TB_PSHIFT:TB_PSHIFT + 128],
                                   buf[:, c0:c0 + 512], start=True, stop=True)
                return ins
            b.op("PE", em3, reads=[bkey, "tb_pshift"],
                 writes=["psA0", "psA1", "psA2", "psA3"], tag=wk + "sh")
            b.op("DVE", lambda v, buf=buf, rot=rot, ctab=ctab: v.tensor_tensor(
                rot, buf, p.tab_bf[:, ctab:ctab + 2048], ALU.mult),
                reads=[bkey, "tb_rot"], writes=[wk + "rc"], tag=wk + "rc")
            b.op("DVE", lambda v, stab=stab: v.tensor_tensor(
                tmp2, p.psA[:, 0:2048], p.tab_bf[:, stab:stab + 2048], ALU.mult),
                reads=["psA0", "psA1", "psA2", "psA3", "tb_rot"],
                writes=[T2], tag=wk + "rs2")
            b.op("DVE", lambda v, rot=rot: v.tensor_tensor(rot, rot, tmp2, ALU.add),
                 reads=[wk + "rc", T2], writes=[pk + which + "rot"], tag=wk + "rot")

        for hh in range(2):
            hbase = hh * 64
            hk = pk + f"h{hh}_"
            for qt in range(NT):
                j0t = max(0, qt - 2)
                nj = qt - j0t + 1
                w = nj * 128
                sl2 = qt % 2
                sb_ = qt % 2
                attn = attnS[sl2]
                attnT = attnTS[sl2]
                negmax = negmaxS[sl2]
                def em(t, qt=qt, j0t=j0t, w=w, sb_=sb_, hbase=hbase):
                    t.matmul(p.psA[:, sb_ * 512: sb_ * 512 + w],
                             qrot[hbase:hbase + 64, qt * 128:(qt + 1) * 128],
                             krot[hbase:hbase + 64, j0t * 128: j0t * 128 + w],
                             start=True, stop=False)
                    return t.matmul(p.psA[:, sb_ * 512: sb_ * 512 + w],
                                    IDb,
                                    p.tab_bf[:, TB_LMASKBF + 384 - w: TB_LMASKBF + 384],
                                    start=False, stop=True)
                b.op("PE", em, reads=[pk + "qrot", pk + "krot", "tb_ident", "tb_tail"],
                     writes=[f"psA{sb_}"], tag=hk + f"sc{qt}")
                b.op("DVE", lambda v, w=w, sb_=sb_, negmax=negmax: v.reduce_max(
                    negmax, p.psA[:, sb_ * 512: sb_ * 512 + w], axis=AX.X, negate=True),
                    reads=[f"psA{sb_}"], writes=[f"negmax{sl2}"], tag=hk + f"mx{qt}")
                b.op("ACT", lambda s, w=w, sb_=sb_, qt=qt, attn=attn, negmax=negmax: s.activation(
                    attn[:, 0:w], p.psA[:, sb_ * 512: sb_ * 512 + w], AF.Exp,
                    bias=negmax, accum_out=rs[:, qt:qt + 1]),
                    reads=[f"psA{sb_}", f"negmax{sl2}"], writes=[f"attn{sl2}", f"rs{qt}"],
                    tag=hk + f"exp{qt}")
                dsl = qt % 2
                def emt(t, nj=nj, dsl=dsl, attn=attn):
                    ins = None
                    for jj in range(nj):
                        ins = t.transpose(
                            p.psD[:, dsl * 1024 + jj * 128: dsl * 1024 + (jj + 1) * 128],
                            attn[:, jj * 128:(jj + 1) * 128], IDb)
                    return ins
                b.op("PE", emt, reads=[f"attn{sl2}", "tb_ident"], writes=[f"psD{dsl}"],
                     tag=hk + f"tr{qt}")
                b.op("DVE", lambda v, w=w, dsl=dsl, attnT=attnT: v.tensor_copy(
                    attnT[:, 0:w], p.psD[:, dsl * 1024: dsl * 1024 + w]),
                    reads=[f"psD{dsl}"], writes=[f"attnT{sl2}"], tag=hk + f"trev{qt}")
                abank = qt // 8
                def emav(t, qt=qt, nj=nj, j0t=j0t, hbase=hbase, attnT=attnT):
                    ins = None
                    for jj in range(nj):
                        ins = t.matmul(
                            p.psB[:, qt * 64:(qt + 1) * 64],
                            attnT[:, jj * 128:(jj + 1) * 128],
                            vtm[:, (j0t + jj) * 128 + hbase: (j0t + jj) * 128 + hbase + 64],
                            start=(jj == 0), stop=(jj == nj - 1))
                    return ins
                b.op("PE", emav, reads=[f"attnT{sl2}", pk + "vtm"], writes=[f"psB{abank}"],
                     tag=hk + f"av{qt}")
            b.op("DVE", lambda v: v.reciprocal(rcp, rs),
                 reads=[f"rs{qt}" for qt in range(NT)], writes=["rcp"], tag=hk + "rcp")
            for qt in range(NT):
                abank = qt // 8
                h = 2 * pr + hh
                b.op("ACT", lambda s, qt=qt, h=h: s.activation(
                    otm[:, qt * 1024 + h * 64: qt * 1024 + h * 64 + 64],
                    p.psB[:, qt * 64:(qt + 1) * 64], AF.Identity,
                    scale=rcp[:, qt:qt + 1]),
                    reads=[f"psB{abank}", "rcp"], writes=[f"otm{qt}"], tag=hk + f"oev{qt}")

    OTM = [f"otm{t}" for t in range(NT)]
    for dd in range(8):
        for half in range(2):
            def em(t, dd=dd, half=half):
                ins = None
                for g in range(8):
                    tt = half * 8 + g
                    ins = t.transpose(p.psD[:, half * 1024 + g * 128: half * 1024 + (g + 1) * 128],
                                      otm[:, tt * 1024 + dd * 128: tt * 1024 + dd * 128 + 128],
                                      IDb)
                return ins
            b.op("PE", em, reads=OTM + ["tb_ident"], writes=[f"psD{half}"],
                 tag=L + f"otr{dd}_{half}")
            wk = [L + f"ofm{dd}"] + (["ofm_init"] + [pk + x for pk in [f"{L}p{q}_" for q in range(8)]
                                                    for x in ("qp", "kp", "vs", "vtm", "qrot", "krot")]
                   if (dd == 0 and half == 0) else [])
            b.op("DVE", lambda v, dd=dd, half=half: v.tensor_copy(
                ofm[:, dd * 2048 + half * 1024: dd * 2048 + (half + 1) * 1024],
                p.psD[:, half * 1024:(half + 1) * 1024]),
                reads=[f"psD{half}"], writes=wk, tag=L + f"oev{dd}_{half}")
    OFM = [L + f"ofm{d}" for d in range(8)]
    for m in range(8):
        slot = wosl[m % 2]
        skey = f"wO{m % 2}"
        b.dma("w", slot.rearrange("p (k c) -> p k c", c=128),
              p.h_wo_b[li, :, m * 128:(m + 1) * 128].rearrange("(k p) c -> p k c", p=128),
              writes=[skey], tag=L + f"wod{m}")
        for tc in range(4):
            bank = tc % 2
            def em(t, slot=slot, tc=tc, bank=bank):
                ins = None
                for k in range(ND):
                    ins = t.matmul(p.psB[:, bank * 512:(bank + 1) * 512],
                                   slot[:, k * 128:(k + 1) * 128],
                                   ofm[:, k * 2048 + tc * 512: k * 2048 + (tc + 1) * 512],
                                   start=(k == 0), stop=(k == ND - 1))
                return ins
            b.op("PE", em, reads=[skey] + OFM, writes=[f"psB{bank}"], tag=L + f"wo{m}_{tc}")
            Msl = M[:, m * 2048 + tc * 512: m * 2048 + (tc + 1) * 512]
            b.op("DVE", lambda v, Msl=Msl, bank=bank: v.tensor_tensor(
                Msl, Msl, p.psB[:, bank * 512:(bank + 1) * 512], ALU.add),
                reads=[f"psB{bank}", f"M{m}"], writes=[f"M{m}"], tag=L + f"wor{m}_{tc}")
            b.op("DVE", lambda v, m=m, tc=tc, Msl=Msl: v.tensor_copy(
                p.xbf[:, m * 2049 + 1 + tc * 512: m * 2049 + 1 + (tc + 1) * 512], Msl),
                reads=[f"M{m}"], writes=[f"xb{m}"], tag=L + f"wox{m}_{tc}")

    if p.debug_out == f"b{li}h":
        b.dma("out", p.h_out.rearrange("(a p) d -> p a d", p=128),
              M[:, 0:16384].rearrange("p (a d) -> p a d", d=1024),
              reads=[f"M{d}" for d in range(ND)], queue="SP", tag="st_Mh",
              is_output=True)
    if p.debug_out == f"dbgB{li}":
        b.dma("dbg", p.h_dbg1, qp, reads=[f"{L}p7_qp"], tag="dbg_qp", is_output=True)
        b.dma("dbg", p.h_dbg2, krot, reads=[f"{L}p7_krot"], tag="dbg_krot", is_output=True)
        b.dma("dbg", p.h_dbg3, otm, reads=[f"otm{t}" for t in range(NT)],
              tag="dbg_otm", is_output=True)

    for half in range(2):
        fk = L + f"f{half}_"
        t0h = half * 1024
        for tci in range(2):
            t0 = t0h + tci * 512
            ck = fk + f"c{tci}_"
            for k in range(ND):
                sq = sqb[k % 2]
                b.op("DVE", lambda v, k=k, t0=t0, sq=sq: v.tensor_tensor(
                    sq, xsh(k)(t0, 512), xsh(k)(t0, 512), ALU.mult),
                    reads=[f"xb{k}"], writes=[L + f"sqb{k % 2}"], tag=ck + f"sq{k}")
                def em(t, k=k, t0=t0, sq=sq):
                    t.matmul(p.psB[0:1, 0:512], p.tab_bf[:, TB_ONES1:TB_ONES1 + 1],
                             xsh(k)(t0, 512), start=(k == 0), stop=(k == ND - 1))
                    return t.matmul(p.psB[0:1, 512:1024], p.tab_bf[:, TB_ONES1:TB_ONES1 + 1],
                                    sq, start=(k == 0), stop=(k == ND - 1))
                b.op("PE", em, reads=[f"xb{k}", L + f"sqb{k % 2}", "tb_tail"],
                     writes=["psB0", "psB1"], tag=ck + f"ssum{k}")
            b.op("DVE", lambda v: v.tensor_scalar_mul(statsA, p.psB[0:1, 0:512], 1.0 / 1024.0),
                 reads=["psB0"], writes=[ck + "mu", "PTtmp"], tag=ck + "mu")
            b.op("DVE", lambda v: v.tensor_scalar_mul(statsB, p.psB[0:1, 512:1024], 1.0 / 1024.0),
                 reads=["psB1"], writes=[ck + "e2"], tag=ck + "e2")
            b.op("DVE", lambda v: v.tensor_tensor(p.psB[0:1, 0:512], statsA, statsA, ALU.mult),
                 reads=[ck + "mu"], writes=["psB0"], tag=ck + "mu2")
            b.op("DVE", lambda v: v.tensor_tensor(statsB, statsB, p.psB[0:1, 0:512], ALU.subtract),
                 reads=[ck + "e2", "psB0"], writes=[ck + "var"], tag=ck + "var")
            b.op("ACT", lambda s: s.activation(statsB, statsB, AF.Ln, bias=eps_col[0:1, :]),
                 reads=[ck + "var", "eps"], writes=[ck + "lnv"], tag=ck + "lnv")
            b.op("ACT", lambda s: s.activation(statsB, statsB, AF.Exp, scale=-0.5),
                 reads=[ck + "lnv"], writes=[ck + "rstd"], tag=ck + "rstd")
            b.op("DVE", lambda v: v.tensor_tensor(statsA, statsA, statsB, ALU.mult),
                 reads=[ck + "mu", ck + "rstd"], writes=[ck + "mur", "PTtmp"], tag=ck + "mur")
            b.op("DVE", lambda v: v.tensor_scalar_mul(statsA, statsA, -1.0),
                 reads=[ck + "mur"], writes=[ck + "B", "PTtmp"], tag=ck + "negB")
            def emb(t):
                t.matmul(p.psB[:, 0:512], p.tab_f32[0:1, TF_ONESROW:TF_ONESROW + 128],
                         statsB, start=True, stop=True)
                return t.matmul(p.psB[:, 512:1024], p.tab_f32[0:1, TF_ONESROW:TF_ONESROW + 128],
                                statsA, start=True, stop=True)
            b.op("PE", emb, reads=[ck + "rstd", ck + "B", "tab_f32"],
                 writes=["psB0", "psB1"], tag=ck + "bc")
            for k in range(ND):
                sq = sqb[k % 2]
                b.op("DVE", lambda v, k=k, t0=t0, sq=sq: v.tensor_tensor(
                    sq, xsh(k)(t0, 512), p.psB[:, 0:512], ALU.mult),
                    reads=[f"xb{k}", "psB0"], writes=[L + f"sqb{k % 2}"], tag=ck + f"xa{k}")
                b.op("DVE", lambda v, sq=sq, k=k: v.tensor_tensor(
                    sq, sq, p.psB[:, 512:1024], ALU.add),
                    reads=[L + f"sqb{k % 2}", "psB1"], writes=[L + f"sqb{k % 2}"],
                    tag=ck + f"xb{k}")
                b.op("ACT", lambda s, k=k, tci=tci, sq=sq: s.activation(
                    xn[:, k * 1024 + tci * 512: k * 1024 + (tci + 1) * 512], sq,
                    AF.Identity, scale=p.ccol(f"lng{li}", k, 1),
                    bias=p.ccol(f"lnb{li}", k, 1)),
                    reads=[L + f"sqb{k % 2}", "consts"],
                    writes=[fk + f"xn{k}"], tag=ck + f"xn{k}")
        XN = [fk + f"xn{k}" for k in range(ND)]
        for r in range(22):
            for part, roff, func in ((0, r, AF.Identity), (1, r + 22, AF.Gelu)):
                slot = wsl[part]
                skey = f"wB{part}"
                b.dma("w", slot.rearrange("p (k c) -> p k c", c=128),
                      p.h_w1p[li, :, roff * 128:(roff + 1) * 128].rearrange("(k p) c -> p k c", p=128),
                      writes=[skey], tag=fk + f"w1d{r}_{part}")
                for tci in range(2):
                    bank = (r * 2 + tci) % 4
                    def em(t, slot=slot, tci=tci, bank=bank):
                        ins = None
                        for k in range(ND):
                            ins = t.matmul(p.psA[:, bank * 512:(bank + 1) * 512],
                                           slot[:, k * 128:(k + 1) * 128],
                                           xn[:, k * 1024 + tci * 512: k * 1024 + (tci + 1) * 512],
                                           start=(k == 0), stop=(k == ND - 1))
                        return ins
                    b.op("PE", em, reads=[skey] + XN, writes=[f"psA{bank}"],
                         tag=fk + f"w1{r}_{part}_{tci}")
                    dst = (a_sb if part == 0 else g_sb)
                    b.op("ACT", lambda s, dst=dst, tci=tci, bank=bank, func=func, roff=roff: s.activation(
                        dst[:, tci * 512:(tci + 1) * 512], p.psA[:, bank * 512:(bank + 1) * 512],
                        func, bias=p.ccol(f"b1{li}", roff, 1)),
                        reads=[f"psA{bank}", "consts"], writes=[fk + ("a" if part == 0 else "g") + str(tci)],
                        tag=fk + f"w1ev{r}_{part}_{tci}")
                    if part == 1:
                        b.op("DVE", lambda v, r=r, tci=tci: v.tensor_tensor(
                            ff[:, r * 1024 + tci * 512: r * 1024 + (tci + 1) * 512],
                            a_sb[:, tci * 512:(tci + 1) * 512],
                            g_sb[:, tci * 512:(tci + 1) * 512], ALU.mult),
                            reads=[fk + f"a{tci}", fk + f"g{tci}"],
                            writes=[fk + f"ff{r}", fk + f"a{tci}", fk + f"g{tci}"],
                            tag=fk + f"ffm{r}_{tci}")
        FF = [fk + f"ff{r}" for r in range(22)]
        for m2 in range(8):
            slot = w2slotA if m2 % 2 == 0 else w2slotB
            skey = f"wW2{m2 % 2}"
            b.dma("w", slot.rearrange("p (k c) -> p k c", c=128),
                  p.h_w2p[li, :, m2 * 128:(m2 + 1) * 128].rearrange("(k p) c -> p k c", p=128),
                  writes=[skey], tag=fk + f"w2d{m2}")
            for tci in range(2):
                bank = tci % 2
                def em(t, slot=slot, tci=tci, bank=bank):
                    ins = None
                    for k2 in range(22):
                        ins = t.matmul(p.psB[:, bank * 512:(bank + 1) * 512],
                                       slot[:, k2 * 128:(k2 + 1) * 128],
                                       ff[:, k2 * 1024 + tci * 512: k2 * 1024 + (tci + 1) * 512],
                                       start=(k2 == 0), stop=(k2 == 21))
                    return ins
                b.op("PE", em, reads=[skey] + FF, writes=[f"psB{bank}"],
                     tag=fk + f"w2{m2}_{tci}")
                b.op("ACT", lambda s, bank=bank, m2=m2: s.activation(
                    w2tmp, p.psB[:, bank * 512:(bank + 1) * 512], AF.Identity,
                    bias=p.ccol(f"b2{li}", m2, 1)),
                    reads=[f"psB{bank}", "consts"], writes=[fk + "w2t", "PTtmp"],
                    tag=fk + f"w2b{m2}_{tci}")
                t0 = t0h + tci * 512
                Msl = M[:, m2 * 2048 + t0: m2 * 2048 + t0 + 512]
                b.op("DVE", lambda v, Msl=Msl: v.tensor_tensor(Msl, Msl, w2tmp, ALU.add),
                     reads=[fk + "w2t", f"M{m2}"], writes=[f"M{m2}"],
                     tag=fk + f"w2r{m2}_{tci}")
                b.op("DVE", lambda v, m2=m2, t0=t0, Msl=Msl: v.tensor_copy(
                    p.xbf[:, m2 * 2049 + 1 + t0: m2 * 2049 + 1 + t0 + 512], Msl),
                    reads=[f"M{m2}"], writes=[f"xb{m2}"], tag=fk + f"w2x{m2}_{tci}")



def block_ln_fm(p: Prog, tag: str, add_x1sp: bool):
    b = p.b
    M = p.pbf_f32
    Sf = p.xf32
    x1c = Sf[:, 0:4096]
    sq0 = p.pbf[:, 32768:33280]
    sq1 = p.pbf[:, 33280:33792]
    statsA = p.pbf_f32[0:1, 16896:17408]
    statsB = p.pbf_f32[0:1, 17408:17920]
    eps_col = p.scr[:, 24:25]
    for tc in range(4):
        t0 = tc * 512
        ck = tag + f"c{tc}_"
        if add_x1sp:
            b.dma("spill", x1c.rearrange("p (a c) -> p a c", c=512),
                  p.h_x1sp.rearrange("p (a c) -> p a c", c=2048)[:, :, t0:t0 + 512],
                  reads=["x1sp"], writes=[tag + "x1c"], queue="SP", tag=ck + "ldx1")
            for d in range(ND):
                Msl = M[:, d * 2048 + t0: d * 2048 + t0 + 512]
                b.op("DVE", lambda v, d=d, Msl=Msl: v.tensor_tensor(
                    Msl, Msl, x1c[:, d * 512:(d + 1) * 512], ALU.add),
                    reads=[tag + "x1c", f"M{d}"], writes=[f"M{d}"], tag=ck + f"add{d}")
        for d in range(ND):
            sq = (sq0 if d % 2 == 0 else sq1)
            Msl = M[:, d * 2048 + t0: d * 2048 + t0 + 512]
            b.op("DVE", lambda v, sq=sq, Msl=Msl: v.tensor_copy(sq, Msl),
                 reads=[f"M{d}"], writes=[tag + f"zb{d % 2}"], tag=ck + f"cz{d}")
            def em(t, sq=sq, d=d):
                return t.matmul(p.psB[0:1, 0:512], p.tab_bf[:, TB_ONES1:TB_ONES1 + 1],
                                sq, start=(d == 0), stop=(d == ND - 1))
            b.op("PE", em, reads=[tag + f"zb{d % 2}", "tb_tail"],
                 writes=["psB0"], tag=ck + f"s1{d}")
            b.op("DVE", lambda v, sq=sq: v.tensor_tensor(sq, sq, sq, ALU.mult),
                 reads=[tag + f"zb{d % 2}"], writes=[tag + f"zb{d % 2}"], tag=ck + f"sq{d}")
            def em2(t, sq=sq, d=d):
                return t.matmul(p.psB[0:1, 512:1024], p.tab_bf[:, TB_ONES1:TB_ONES1 + 1],
                                sq, start=(d == 0), stop=(d == ND - 1))
            b.op("PE", em2, reads=[tag + f"zb{d % 2}", "tb_tail"],
                 writes=["psB1"], tag=ck + f"s2{d}")
        b.op("DVE", lambda v: v.tensor_scalar_mul(statsA, p.psB[0:1, 0:512], 1.0 / 1024.0),
             reads=["psB0"], writes=[ck + "mu", "PTtmp"], tag=ck + "mu")
        b.op("DVE", lambda v: v.tensor_scalar_mul(statsB, p.psB[0:1, 512:1024], 1.0 / 1024.0),
             reads=["psB1"], writes=[ck + "e2"], tag=ck + "e2")
        b.op("DVE", lambda v: v.tensor_tensor(p.psB[0:1, 0:512], statsA, statsA, ALU.mult),
             reads=[ck + "mu"], writes=["psB0"], tag=ck + "mu2")
        b.op("DVE", lambda v: v.tensor_tensor(statsB, statsB, p.psB[0:1, 0:512], ALU.subtract),
             reads=[ck + "e2", "psB0"], writes=[ck + "var"], tag=ck + "var")
        b.op("ACT", lambda s: s.activation(statsB, statsB, AF.Ln, bias=eps_col[0:1, :]),
             reads=[ck + "var", "eps"], writes=[ck + "lnv"], tag=ck + "lnv")
        b.op("ACT", lambda s: s.activation(statsB, statsB, AF.Exp, scale=-0.5),
             reads=[ck + "lnv"], writes=[ck + "rstd"], tag=ck + "rstd")
        b.op("DVE", lambda v: v.tensor_tensor(statsA, statsA, statsB, ALU.mult),
             reads=[ck + "mu", ck + "rstd"], writes=[ck + "mur", "PTtmp"], tag=ck + "mur")
        b.op("DVE", lambda v: v.tensor_scalar_mul(statsA, statsA, -1.0),
             reads=[ck + "mur"], writes=[ck + "B", "PTtmp"], tag=ck + "negB")
        def emb(t):
            t.matmul(p.psB[:, 0:512], p.tab_f32[0:1, TF_ONESROW:TF_ONESROW + 128],
                     statsB, start=True, stop=True)
            return t.matmul(p.psB[:, 512:1024], p.tab_f32[0:1, TF_ONESROW:TF_ONESROW + 128],
                            statsA, start=True, stop=True)
        b.op("PE", emb, reads=[ck + "rstd", ck + "B", "tab_f32"],
             writes=["psB0", "psB1"], tag=ck + "bc")
        for d in range(ND):
            Msl = M[:, d * 2048 + t0: d * 2048 + t0 + 512]
            b.op("DVE", lambda v, Msl=Msl: v.tensor_tensor(Msl, Msl, p.psB[:, 0:512], ALU.mult),
                 reads=[f"M{d}", "psB0"], writes=[f"M{d}"], tag=ck + f"na{d}")
            b.op("DVE", lambda v, Msl=Msl: v.tensor_tensor(Msl, Msl, p.psB[:, 512:1024], ALU.add),
                 reads=[f"M{d}", "psB1"], writes=[f"M{d}"], tag=ck + f"nb{d}")
            b.op("DVE", lambda v, d=d, t0=t0, Msl=Msl: v.tensor_copy(
                p.xbf[:, d * 2049 + 1 + t0: d * 2049 + 1 + t0 + 512], Msl),
                reads=[f"M{d}"], writes=[f"xb{d}"], tag=ck + f"nx{d}")



def stage_c(p: Prog):
    import os
    trunc = os.environ.get("MK_CTRUNC", "")
    b = p.b
    M = p.pbf_f32
    S = p.xf32.bitcast(BF16)
    Sf = p.xf32
    out_fm = S[:, 0:16384]
    qj = S[:, 16384:18432]
    kj = S[:, 18432:20480]
    vj = S[:, 20480:22528]
    decay_full = Sf[:, 11264:13312]
    prod = S[:, 26624:28672]
    phi_tS = [Sf[:, 14336:14848], Sf[:, 14848:15360]]
    phi_t2S = [S[:, 30720:31744], S[:, 31744:32768]]
    out2c = S[:, 16384:20480]
    vloadS = [S[:, 20480:20992], S[:, 20992:21504]]
    gbuf = S[:, 21504:22528]
    kvbuf = p.pbf[:, 32768:34816]
    runb = p.pbf_f32[:, 17408:19456]
    dcol = p.scr[:, 64:72]
    wsl = [p.wbuf[:, 0:1024], p.wbuf[:, 1024:2048]]

    XB = [f"xb{d}" for d in range(ND)]

    b.op("ACT", lambda s: s.activation(dcol, p.ccol("decay", 0, 8), AF.Sigmoid),
         reads=["consts"], writes=["dcol"], tag="c_dcol")

    if trunc == "ln2":
        return
    for j in range(ND):
        pk = f"C{j}_"
        b.op("DVE", lambda v: v.memset(decay_full, 1.0),
             writes=["Cdec"], tag=pk + "dec0")
        b.op("DVE", lambda v, j=j: v.tensor_scalar(
            decay_full, decay_full, dcol[:, j:j + 1], None, ALU.mult),
            reads=["dcol", "Cdec"], writes=["Cdec"], tag="Cdec")
        for sl_i, mbase, dstbuf, dkey, isphi in ((0, j * 128, qj, "Cq", True),
                                                 (1, 1024 + j * 128, kj, "Ck", True),
                                                 (2, 2048 + j * 128, vj, "Cv", False)):
            slot = wsl[sl_i % 2]
            skey = f"wC{sl_i % 2}"
            b.dma("w", slot.rearrange("p (k c) -> p k c", c=128),
                  p.h_wqkvT_c[:, mbase:mbase + 128].rearrange("(k p) c -> p k c", p=128),
                  writes=[skey], tag=pk + f"wd{sl_i}")
            for tc in range(4):
                bank = tc % 4
                def em(t, slot=slot, tc=tc, bank=bank):
                    ins = None
                    for k in range(ND):
                        ins = t.matmul(p.psA[:, bank * 512:(bank + 1) * 512],
                                       slot[:, k * 128:(k + 1) * 128],
                                       p.xbf[:, k * 2049 + 1 + tc * 512: k * 2049 + 1 + (tc + 1) * 512],
                                       start=(k == 0), stop=(k == ND - 1))
                    return ins
                b.op("PE", em, reads=[skey] + XB, writes=[f"psA{bank}"],
                     tag=pk + f"mm{sl_i}_{tc}")
                if not isphi:
                    b.op("ACT", lambda s, dstbuf=dstbuf, tc=tc, bank=bank: s.activation(
                        dstbuf[:, tc * 512:(tc + 1) * 512],
                        p.psA[:, bank * 512:(bank + 1) * 512], AF.Identity),
                        reads=[f"psA{bank}"], writes=[dkey], tag=pk + f"ev{sl_i}_{tc}")
                else:
                    ps_ = tc % 2
                    phi_t = phi_tS[ps_]
                    phi_t2 = phi_t2S[ps_]
                    b.op("DVE", lambda v, bank=bank, phi_t=phi_t: v.tensor_scalar_min(
                        phi_t, p.psA[:, bank * 512:(bank + 1) * 512], 0.0),
                        reads=[f"psA{bank}"], writes=[f"Cpt{ps_}"], tag=pk + f"mn{sl_i}_{tc}")
                    b.op("ACT", lambda s, phi_t=phi_t, phi_t2=phi_t2: s.activation(
                        phi_t2[:, 0:512], phi_t, AF.Exp),
                        reads=[f"Cpt{ps_}"], writes=[f"Cpe{ps_}"], tag=pk + f"ex{sl_i}_{tc}")
                    b.op("ACT", lambda s, bank=bank, phi_t2=phi_t2: s.activation(
                        phi_t2[:, 512:1024], p.psA[:, bank * 512:(bank + 1) * 512], AF.Relu),
                        reads=[f"psA{bank}"], writes=[f"Cpr{ps_}"], tag=pk + f"rl{sl_i}_{tc}")
                    b.op("DVE", lambda v, dstbuf=dstbuf, tc=tc, phi_t2=phi_t2: v.tensor_tensor(
                        dstbuf[:, tc * 512:(tc + 1) * 512], phi_t2[:, 0:512],
                        phi_t2[:, 512:1024], ALU.add),
                        reads=[f"Cpe{ps_}", f"Cpr{ps_}"], writes=[dkey], tag=pk + f"ph{sl_i}_{tc}")
        b.dma("spill", p.h_vsp[:, j * 2048:(j + 1) * 2048], vj,
              reads=["Cv"], writes=["vsp"], queue="SP", tag=pk + "vspill")
        b.op("DVE", lambda v: v.tensor_tensor(kvbuf, kj, vj, ALU.mult),
             reads=["Ck", "Cv"], writes=["Ckv"], tag="Ckv")
        b.op("DVE", lambda v: v.tensor_tensor_scan(runb, decay_full, kj, 0.0,
                                                   ALU.mult, ALU.add),
             reads=["Cdec", "Ck"], writes=["Crun"], tag=pk + "scank")
        b.op("DVE", lambda v: v.tensor_copy(kj, runb),
             reads=["Crun"], writes=["Ck"], tag="Ck")
        b.op("DVE", lambda v: v.tensor_tensor(prod, qj, kj, ALU.mult),
             reads=["Cq", "Ck"], writes=["Cprod"], tag="Cprod")
        def emd(t):
            ins = None
            for c0 in range(0, 2048, 512):
                ins = t.matmul(p.psA[0:2, c0:c0 + 512], p.tab_bf[:, TB_ONES2:TB_ONES2 + 2],
                               prod[:, c0:c0 + 512], start=True, stop=True)
            return ins
        b.op("PE", emd, reads=["Cprod", "tb_tail"],
             writes=["psA0", "psA1", "psA2", "psA3"], tag=pk + "den")
        b.op("DVE", lambda v: v.tensor_scalar_max(p.psA[0:2, 0:2048], p.psA[0:2, 0:2048], 1e-6),
             reads=["psA0", "psA1", "psA2", "psA3"],
             writes=["psA0", "psA1", "psA2", "psA3"], tag=pk + "clip")
        b.op("DVE", lambda v: v.reciprocal(p.psA[0:2, 0:2048], p.psA[0:2, 0:2048]),
             reads=["psA0", "psA1", "psA2", "psA3"],
             writes=["psA0", "psA1", "psA2", "psA3"], tag=pk + "rcp")
        b.op("DVE", lambda v: v.tensor_copy(prod[0:2, 0:2048], p.psA[0:2, 0:2048]),
             reads=["psA0", "psA1", "psA2", "psA3"], writes=["Cdinv"], tag=pk + "dcp")
        def emb2(t):
            ins = None
            for c0 in range(0, 2048, 512):
                ins = t.matmul(p.psA[:, c0:c0 + 512], p.tab_bf[0:2, TB_BCAST2:TB_BCAST2 + 128],
                               prod[0:2, c0:c0 + 512], start=True, stop=True)
            return ins
        b.op("PE", emb2, reads=["Cdinv", "tb_tail"],
             writes=["psA0", "psA1", "psA2", "psA3"], tag=pk + "dbc")
        b.op("DVE", lambda v: v.tensor_tensor_scan(runb, decay_full, kvbuf, 0.0,
                                                   ALU.mult, ALU.add),
             reads=["Cdec", "Ckv"], writes=["Crun"],
             tag=pk + "scankv")
        b.op("DVE", lambda v: v.tensor_copy(kj, runb),
             reads=["Crun"], writes=["Ck"], tag="Ck")
        b.op("DVE", lambda v: v.tensor_tensor(prod, qj, kj, ALU.mult),
             reads=["Cq", "Ck"], writes=["Cprod", "Cdinv"], tag=pk + "num")
        b.op("DVE", lambda v, j=j: v.tensor_tensor(
            out_fm[:, j * 2048:(j + 1) * 2048], prod, p.psA[:, 0:2048], ALU.mult),
            reads=["Cprod", "psA0", "psA1", "psA2", "psA3"],
            writes=[f"cof{j}"], tag=pk + "out")

    COF = [f"cof{j}" for j in range(ND)]
    if trunc == "tiles":
        return
    b.barrier()
    for tc in range(4):
        gk = f"Cg{tc}_"
        t0 = tc * 512
        for m in range(ND):
            slot = wsl[m % 2]
            skey = f"wC{m % 2}"
            b.dma("w", slot.rearrange("p (k c) -> p k c", c=128),
                  p.h_wgateT[:, m * 128:(m + 1) * 128].rearrange("(k p) c -> p k c", p=128),
                  writes=[skey], tag=gk + f"gd{m}")
            def em(t, slot=slot, t0=t0):
                ins = None
                for k in range(ND):
                    ins = t.matmul(p.psB[:, 0:512], slot[:, k * 128:(k + 1) * 128],
                                   out_fm[:, k * 2048 + t0: k * 2048 + t0 + 512],
                                   start=(k == 0), stop=(k == ND - 1))
                return ins
            b.op("PE", em, reads=[skey] + COF, writes=["psB0"], tag=gk + f"gm{m}")
            b.op("ACT", lambda s, m=m: s.activation(
                gbuf[:, 0:512], p.psB[:, 0:512], AF.Sigmoid,
                bias=p.ccol("bgate", m, 1)),
                reads=["psB0", "consts"], writes=["Cgg"], tag=gk + f"gs{m}")
            vsl = m % 2
            vload = vloadS[vsl]
            b.dma("spill", vload, p.h_vsp[:, m * 2048 + t0: m * 2048 + t0 + 512],
                  reads=["vsp"], writes=[f"Cgv{vsl}"], queue="SP", tag=gk + f"vld{m}")
            b.op("DVE", lambda v, m=m, t0=t0, vload=vload: v.tensor_tensor(
                gbuf[:, 512:1024], out_fm[:, m * 2048 + t0: m * 2048 + t0 + 512],
                vload, ALU.subtract),
                reads=[COF[m], f"Cgv{vsl}"], writes=["Cgd"], tag=gk + f"d{m}")
            b.op("DVE", lambda v: v.tensor_tensor(
                gbuf[:, 512:1024], gbuf[:, 512:1024], gbuf[:, 0:512], ALU.mult),
                reads=["Cgd", "Cgg"], writes=["Cgd"], tag=gk + f"gd2{m}")
            b.op("DVE", lambda v, m=m, vload=vload: v.tensor_tensor(
                out2c[:, m * 512:(m + 1) * 512], gbuf[:, 512:1024], vload, ALU.add),
                reads=["Cgd", f"Cgv{vsl}"], writes=[f"Cgo2_{m}"], tag=gk + f"o2{m}")
        for mo in range(ND):
            slot = wsl[mo % 2]
            skey = f"wC{mo % 2}"
            b.dma("w", slot.rearrange("p (k c) -> p k c", c=128),
                  p.h_woutT[:, mo * 128:(mo + 1) * 128].rearrange("(k p) c -> p k c", p=128),
                  writes=[skey], tag=gk + f"od{mo}")
            def em(t, slot=slot):
                ins = None
                for k in range(ND):
                    ins = t.matmul(p.psB[:, 512:1024], slot[:, k * 128:(k + 1) * 128],
                                   out2c[:, k * 512:(k + 1) * 512],
                                   start=(k == 0), stop=(k == ND - 1))
                return ins
            b.op("PE", em, reads=[skey] + [f"Cgo2_{k}" for k in range(ND)],
                 writes=["psB1"], tag=gk + f"om{mo}")
            Msl = M[:, mo * 2048 + t0: mo * 2048 + t0 + 512]
            b.op("DVE", lambda v, Msl=Msl: v.tensor_tensor(
                Msl, Msl, p.psB[:, 512:1024], ALU.add),
                reads=["psB1", f"M{mo}"], writes=[f"M{mo}"], tag=gk + f"or{mo}")
    if trunc == "gate":
        return
    b.barrier()
    block_ln_fm(p, "LN3_", add_x1sp=False)



def stage_d(p: Prog):
    b = p.b
    M = p.pbf_f32
    S = p.xf32.bitcast(BF16)
    Sf = p.xf32
    IDf = p.ident_f32

    for dt in range(ND):
        for g4 in range(4):
            bank = g4 % 4
            def em(t, dt=dt, g4=g4, bank=bank):
                ins = None
                for q in range(4):
                    tt = g4 * 4 + q
                    ins = t.transpose(p.psA[:, bank * 512 + q * 128: bank * 512 + (q + 1) * 128],
                                      M[:, dt * 2048 + tt * 128: dt * 2048 + tt * 128 + 128],
                                      IDf)
                return ins
            b.op("PE", em, reads=[f"M{dt}", "tab_f32"], writes=[f"psA{bank}"],
                 tag=f"Dtr{dt}_{g4}")
            def ev(v, dt=dt, g4=g4, bank=bank):
                ins = None
                for q in range(4):
                    tt = g4 * 4 + q
                    ins = v.tensor_copy(
                        Sf[:, tt * 1024 + dt * 128: tt * 1024 + dt * 128 + 128],
                        p.psA[:, bank * 512 + q * 128: bank * 512 + (q + 1) * 128])
                return ins
            b.op("DVE", ev, reads=[f"psA{bank}"], writes=[f"Dtm{g4 * 4}"],
                 tag=f"Dte{dt}_{g4}")
    b.barrier()
    b.dma("spill", M[:, 0:16384], Sf[:, 0:16384],
          reads=[f"Dtm{t}" for t in range(0, 16, 4)],
          writes=[f"M{d}" for d in range(ND)] + [f"OT{t}" for t in range(NT)],
          queue="SP", tag="D_tmcopy")
    b.barrier()

    wz = S[:, 0:16384]
    wc = S[:, 16384:24576]
    zblk = S[:, 24576:32768]
    rm_bf = p.pbf[:, 32768:33792]
    ru_bf = p.pbf[:, 33792:37888]
    wzev = p.pbf[:, 37888:38400]
    eps_col = p.scr[:, 24:25]

    for h in range(4):
        hk = f"Dw{h}_"
        b.dma("w", rm_bf[0:16, :].rearrange("p (r c) -> p r c", c=16),
              p.h_rmT[h].rearrange("r c i -> c r i"),
              writes=["rmb"], tag=hk + "rmd")
        b.dma("w", ru_bf[0:16, :].rearrange("p (r c) -> p r c", c=64),
              p.h_ruT[h].rearrange("r c i -> c r i"),
              writes=["rub"], tag=hk + "rud")
        for B in range(8):
            def em(t, B=B):
                ins = None
                for a in range(8):
                    r = B * 8 + a
                    ins = t.matmul(p.psB[0:16, a * 64:(a + 1) * 64],
                                   rm_bf[0:16, r * 16:(r + 1) * 16],
                                   ru_bf[0:16, r * 64:(r + 1) * 64],
                                   start=True, stop=True)
                return ins
            b.op("PE", em, reads=["rmb", "rub"], writes=["psB0", "psB1"], tag=hk + f"mz{B}")
            b.op("DVE", lambda v: v.tensor_copy(wzev[0:16, :], p.psB[0:16, 0:512]),
                 reads=["psB0", "psB1"], writes=["wzev"], tag=hk + f"ze{B}")
            def em2(t, B=B):
                ins = None
                for a in range(8):
                    ins = t.matmul(p.psA[:, a * 64:(a + 1) * 64],
                                   p.tab_bf[0:16, TB_SEL + a * 128: TB_SEL + (a + 1) * 128],
                                   wzev[0:16, a * 64:(a + 1) * 64],
                                   start=True, stop=True)
                return ins
            b.op("PE", em2, reads=["wzev", "tb_tail"], writes=["psA0"], tag=hk + f"pl{B}")
            def ev(v, h=h, B=B):
                return v.tensor_copy(
                    wz[:, (h * 32 + 4 * B) * 128: (h * 32 + 4 * B) * 128 + 512],
                    p.psA[:, 0:512])
            b.op("DVE", ev, reads=["psA0"], writes=[f"wz{h}_{B}"], tag=hk + f"zf{B}")
    WZ = [f"wz{h}_{B}" for h in range(4) for B in range(8)]

    XB = [f"xb{d}" for d in range(ND)]
    for half in range(2):
        t0h = half * 1024
        for blk in range(16):
            bk = f"Dz{half}_{blk}_"
            b.dma("w", wc.rearrange("p (k c) -> p k c", c=1024),
                  p.h_wcT[blk * 1024:(blk + 1) * 1024, :].rearrange("(k p) c -> p k c", p=128),
                  writes=["wcb"], tag=bk + "wcd")
            for zz in range(8):
                zc = blk * 8 + zz
                xt = (zc % 32) // 4
                zb = zz % 2
                for c0 in range(0, 1024, 512):
                    def emz(t, zc=zc, xt=xt, t0h=t0h, zb=zb, c0=c0):
                        return t.matmul(p.psB[:, zb * 512: zb * 512 + 512],
                                        wz[:, zc * 128:(zc + 1) * 128],
                                        p.xbf[:, xt * 2049 + 1 + t0h + c0: xt * 2049 + 1 + t0h + c0 + 512],
                                        start=True, stop=True)
                    b.op("PE", emz, reads=WZ + XB, writes=[f"psB{zb}"], tag=bk + f"zm{zz}_{c0}")
                    b.op("ACT", lambda s, zz=zz, zb=zb, c0=c0: s.activation(
                        zblk[:, zz * 1024 + c0: zz * 1024 + c0 + 512],
                        p.psB[:, zb * 512: zb * 512 + 512], AF.Gelu),
                        reads=[f"psB{zb}"], writes=[f"Dzb{zz}"], tag=bk + f"zg{zz}_{c0}")
            for ttl in range(8):
                tt = half * 8 + ttl
                po = (ttl % 2) * 1024
                pk2 = ["psA0", "psA1"] if ttl % 2 == 0 else ["psA2", "psA3"]
                def em(t, ttl=ttl, blk=blk, po=po):
                    ins = None
                    for zz in range(8):
                        for c0 in range(0, 1024, 512):
                            ins = t.matmul(
                                p.psA[:, po + c0: po + c0 + 512],
                                zblk[:, zz * 1024 + ttl * 128: zz * 1024 + ttl * 128 + 128],
                                wc[:, zz * 1024 + c0: zz * 1024 + c0 + 512],
                                start=(zz == 0), stop=(zz == 7))
                    return ins
                b.op("PE", em, reads=[f"Dzb{zz}" for zz in range(8)] + ["wcb"],
                     writes=pk2, tag=bk + f"cp{ttl}")
                OT = M[:, tt * 1024:(tt + 1) * 1024]
                b.op("DVE", lambda v, OT=OT, po=po: v.tensor_tensor(
                    OT, OT, p.psA[:, po:po + 1024], ALU.add),
                    reads=pk2 + [f"OT{tt}"], writes=[f"OT{tt}"], tag=bk + f"ca{ttl}")

    for tt in range(NT):
        lk = f"LN4_{tt}_"
        OT = M[:, tt * 1024:(tt + 1) * 1024]
        st = p.scr
        b.op("DVE", lambda v, OT=OT: v.reduce_sum(st[:, 80:81], OT, axis=AX.X),
             reads=[f"OT{tt}"], writes=["l4s"], tag=lk + "s")
        b.op("DVE", lambda v: v.tensor_scalar_mul(st[:, 81:82], st[:, 80:81], 1.0 / 1024.0),
             reads=["l4s"], writes=["l4m"], tag=lk + "m")
        b.op("DVE", lambda v, OT=OT: v.tensor_scalar(OT, OT, st[:, 81:82], None, ALU.subtract),
             reads=[f"OT{tt}", "l4m"], writes=[f"OT{tt}"], tag=lk + "c")
        b.op("ACT", lambda s, OT=OT: s.activation(p.pbf[:, 32768:33792], OT, AF.Square,
                                                  accum_out=st[:, 82:83]),
             reads=[f"OT{tt}"], writes=["l4q", "rmb", "rub"], tag=lk + "q")
        b.op("ACT", lambda s: s.activation(st[:, 83:84], st[:, 82:83], AF.Ln,
                                           scale=1.0 / 1024.0, bias=eps_col),
             reads=["l4q", "eps"], writes=["l4l"], tag=lk + "l")
        b.op("ACT", lambda s: s.activation(st[:, 84:85], st[:, 83:84], AF.Exp, scale=-0.5),
             reads=["l4l"], writes=["l4r"], tag=lk + "r")
        b.op("DVE", lambda v, OT=OT: v.tensor_scalar(OT, OT, st[:, 84:85], None, ALU.mult),
             reads=[f"OT{tt}", "l4r"], writes=[f"OT{tt}"], tag=lk + "n")
    b.dma("out", p.h_out.rearrange("(a p) d -> p a d", p=128),
          M[:, 0:16384].rearrange("p (a d) -> p a d", d=1024),
          reads=[f"OT{t}" for t in range(NT)], queue="SP", tag="st_out",
          is_output=True)


def build_kernel_program(stages=("A",), debug_out=None, clayout=None, nlayers_b=6):
    p = Prog(clayout, stages, debug_out)
    p.load_commons()
    if "A" in stages:
        stage_a(p)
    if "B" in stages:
        p.b.barrier()
        conv_a2b(p)
        p.b.dma("spill", p.h_x1sp, p.pbf_f32[:, 0:16384],
                reads=[f"M{d}" for d in range(ND)], writes=["x1sp"], queue="SP",
                tag="sp_x1")
        p.load_rotary()
        for li in range(nlayers_b):
            p.b.barrier()
            stage_b_layer(p, li)
        if p.debug_out in ("b0", "b1", "b2", "b3", "b4", "b5", "xfm"):
            p.b.dma("out", p.h_out.rearrange("(a p) d -> p a d", p=128),
                    p.pbf_f32[:, 0:16384].rearrange("p (a d) -> p a d", d=1024),
                    reads=[f"M{d}" for d in range(ND)], queue="SP", tag="st_M",
                    is_output=True)
    if "C" in stages:
        p.b.barrier()
        block_ln_fm(p, "LN2_", add_x1sp=True)
        p.b.barrier()
        stage_c(p)
        if p.debug_out in ("x2", "x3"):
            p.b.dma("out", p.h_out.rearrange("(a p) d -> p a d", p=128),
                    p.pbf_f32[:, 0:16384].rearrange("p (a d) -> p a d", d=1024),
                    reads=[f"M{d}" for d in range(ND)], queue="SP", tag="st_M",
                    is_output=True)
    if "D" in stages:
        p.b.barrier()
        stage_d(p)
    p.b.emit()
    return p


def kernel(**inputs):
    shared, clayout = _host_prep(inputs)
    x = np.asarray(inputs["L0_attn_x"], np.float32)
    p = build_kernel_program(stages=("A", "B", "C", "D"), debug_out=None,
                             clayout=clayout)
    in_maps = [dict(shared, x=np.ascontiguousarray(x[i])) for i in range(8)]
    res = run_bass_kernel_spmd(p.nc, in_maps, core_ids=list(range(8)))
    p.b.close()
    out = np.stack([r["out"] for r in res.results], axis=0)
    return out.astype(np.float32)


# version 05 (final)
# speedup vs baseline: 1.3717x; 1.0256x over previous
import contextlib
import numpy as np
import ml_dtypes

import concourse.bass as bass
import concourse.mybir as mybir
from concourse.bass_utils import run_bass_kernel_spmd

F32 = mybir.dt.float32
BF16 = mybir.dt.bfloat16
AF = mybir.ActivationFunctionType
ALU = mybir.AluOpType
AX = mybir.AxisListType

T = 2048
D = 1024
NT = 16
ND = 8

ENGINES = ("PE", "DVE", "ACT", "POOL", "SP")


class _Node:
    __slots__ = ("idx", "engine", "kind", "emit", "deps", "pos", "semname", "semval",
                 "need_inc", "incval", "retained", "know", "tag", "cost", "nusers",
                 "ready", "fin")

    def __init__(self, idx, engine, kind, emit, deps, pos, tag):
        self.idx = idx
        self.engine = engine
        self.kind = kind
        self.emit = emit
        self.deps = deps
        self.pos = pos
        self.semname = None
        self.semval = None
        self.need_inc = False
        self.incval = None
        self.retained = None
        self.know = None
        self.tag = tag


class MK:

    SERIAL = False
    ENDCLEAR = True
    SAME_ENGINE_SYNC = True

    def __init__(self, nc):
        self.nc = nc
        self.es = contextlib.ExitStack()
        self.nodes = []
        self.res_w = {}
        self.res_r = {}
        self.eng_pos = {e: 0 for e in ENGINES}
        self.streams = {}
        self.sem_names = ["sem_" + e for e in ENGINES]
        self.out_events = []
        self.last_barrier = []

    def sb(self, name, shape, dtype):
        return self.es.enter_context(self.nc.sbuf_tensor(name, shape, dtype))

    def ps(self, name, shape, dtype):
        return self.es.enter_context(self.nc.psum_tensor(name, shape, dtype))

    def _deps_for(self, reads, writes):
        deps = []
        seen = set()

        def add(n):
            if n is not None and id(n) not in seen:
                seen.add(id(n))
                deps.append(n)
        for r in reads:
            add(self.res_w.get(r))
        for w in writes:
            add(self.res_w.get(w))
            for rd in self.res_r.get(w, ()):
                add(rd)
        return deps

    def _update_res(self, node, reads, writes):
        for r in reads:
            self.res_r.setdefault(r, []).append(node)
        for w in writes:
            self.res_w[w] = node
            self.res_r[w] = []

    def op(self, engine, emit, reads=(), writes=(), tag="", cost=None):
        if engine in ("DVE", "ACT"):
            shadow = [r + "#vs" for r in reads
                      if r.startswith("psA") or r.startswith("psB") or r.startswith("psD")]
            if shadow:
                writes = list(writes) + shadow
        deps = self._deps_for(reads, writes)
        for n in self.last_barrier:
            if n not in deps:
                deps.append(n)
        if MK.SERIAL and self.nodes and self.nodes[-1] not in deps:
            deps.append(self.nodes[-1])
        node = _Node(len(self.nodes), engine, "c", emit, deps, self.eng_pos[engine], tag)
        node.cost = cost if cost is not None else {"PE": 400, "DVE": 700, "ACT": 600,
                                                   "POOL": 500, "SP": 500}[engine]
        self.eng_pos[engine] += 1
        self.nodes.append(node)
        self._update_res(node, reads, writes)
        return node

    def stream(self, name, queue, parity=2):
        if name not in self.streams:
            sems = []
            for i in range(parity):
                sn = f"dma_{name}_{i}"
                sems.append(sn)
                self.sem_names.append(sn)
            self.streams[name] = dict(queue=queue, sems=sems, cnt=[0] * parity,
                                      last=[None] * parity, i=0)
        return self.streams[name]

    def dma(self, stream_name, out_ap, in_ap, reads=(), writes=(), tag="", queue="POOL",
            parity=2, is_output=False):
        st = self.stream(stream_name, queue, parity)
        queue = st["queue"]
        slot = st["i"] % len(st["sems"])
        st["i"] += 1
        deps = self._deps_for(reads, writes)
        for n in self.last_barrier:
            if n not in deps:
                deps.append(n)
        if MK.SERIAL and self.nodes and self.nodes[-1] not in deps:
            deps.append(self.nodes[-1])
        prev = st["last"][slot]
        if prev is not None and prev not in deps:
            deps.append(prev)
        node = _Node(len(self.nodes), queue, "d",
                     (lambda eng, o=out_ap, i=in_ap: eng.dma_start(out=o, in_=i)),
                     deps, self.eng_pos[queue], tag)
        node.cost = 3000
        st["cnt"][slot] += 1
        node.semname = st["sems"][slot]
        node.semval = 16 * st["cnt"][slot]
        st["last"][slot] = node
        self.eng_pos[queue] += 1
        self.nodes.append(node)
        self._update_res(node, reads, writes)
        if is_output:
            self.out_events.append(node)
        return node

    def barrier(self):
        lasts = {}
        for n in reversed(self.nodes):
            if n.engine not in lasts:
                lasts[n.engine] = n
            if len(lasts) == len(ENGINES):
                break
        ln = list(lasts.values())
        for st in self.streams.values():
            for nd in st["last"]:
                if nd is not None and nd not in ln:
                    ln.append(nd)
        self.last_barrier = ln

    RESCHEDULE = True

    def schedule(self):
        import heapq
        n = len(self.nodes)
        users = [[] for _ in range(n)]
        indeg = [0] * n
        for nd in self.nodes:
            for d in nd.deps:
                users[d.idx].append(nd.idx)
                indeg[nd.idx] += 1
        eng_clock = {e: 0.0 for e in ENGINES}
        ready = []
        for nd in self.nodes:
            nd.ready = 0.0
            nd.fin = 0.0
            if indeg[nd.idx] == 0:
                heapq.heappush(ready, (0.0, nd.idx))
        order = []
        nodes = self.nodes
        while ready:
            rt, i = heapq.heappop(ready)
            nd = nodes[i]
            start = max(rt, eng_clock[nd.engine])
            issue = 100.0 if nd.kind == "d" else nd.cost
            eng_clock[nd.engine] = start + issue
            nd.fin = start + nd.cost
            order.append(nd)
            for u in users[i]:
                un = nodes[u]
                if nd.fin > un.ready:
                    un.ready = nd.fin
                indeg[u] -= 1
                if indeg[u] == 0:
                    heapq.heappush(ready, (un.ready, u))
        assert len(order) == n, (len(order), n)
        self.nodes = order
        pos = {e: 0 for e in ENGINES}
        for nd in self.nodes:
            nd.pos = pos[nd.engine]
            pos[nd.engine] += 1

    def finalize(self):
        if MK.RESCHEDULE and not MK.SERIAL:
            self.schedule()
        know_stream = {e: {} for e in ENGINES}
        for node in self.nodes:
            know = dict(know_stream[node.engine])
            retained = []
            for dep in node.deps:
                if (not MK.SAME_ENGINE_SYNC) and dep.kind == "c" and node.kind == "c" \
                        and dep.engine == node.engine:
                    continue
                if dep.kind == "c":
                    key, val = ("E:" + dep.engine), dep.pos
                else:
                    key, val = dep.semname, dep.semval
                if know.get(key, -1) >= val:
                    continue
                retained.append(dep)
                if dep.know:
                    for k2, v2 in dep.know.items():
                        if know.get(k2, -1) < v2:
                            know[k2] = v2
                if know.get(key, -1) < val:
                    know[key] = val
            node.retained = retained
            for dep in retained:
                if dep.kind == "c":
                    dep.need_inc = True
            myknow = dict(know)
            if node.kind == "d":
                myknow[node.semname] = node.semval
            else:
                myknow["E:" + node.engine] = node.pos
            node.know = myknow
            know_stream[node.engine] = know
        cnt = {e: 0 for e in ENGINES}
        for node in self.nodes:
            if node.kind == "c" and node.need_inc:
                cnt[node.engine] += 1
                node.incval = cnt[node.engine]

    def emit(self):
        self.finalize()
        nc = self.nc
        self.sem_names.append("done")
        assert len(self.sem_names) <= 56, len(self.sem_names)
        sems = {}
        for i, sn in enumerate(self.sem_names):
            sems[sn] = self.es.enter_context(nc.semaphore(sn, num=255 - i))
        per_engine = {e: [] for e in ENGINES}
        for node in self.nodes:
            per_engine[node.engine].append(node)
        final_out = [(n.semname, n.semval) for n in self.out_events]
        dma_totals = []
        for st in self.streams.values():
            for sn, c in zip(st["sems"], st["cnt"]):
                if c:
                    dma_totals.append((sn, 16 * c))

        def emit_engine(eng_obj, ename):
            for node in per_engine[ename]:
                for dep in node.retained:
                    if dep.kind == "c":
                        assert dep.incval is not None, (dep.tag, node.tag)
                        eng_obj.wait_ge(sems["sem_" + dep.engine], dep.incval)
                    else:
                        eng_obj.wait_ge(sems[dep.semname], dep.semval)
                instr = node.emit(eng_obj)
                if node.kind == "d":
                    instr.then_inc(sems[node.semname], 16)
                elif node.need_inc:
                    if instr is None:
                        raise RuntimeError(f"op {node.tag} returned no instruction")
                    instr.then_inc(sems["sem_" + node.engine], 1)
            if ename == "POOL":
                for sn, v in final_out:
                    eng_obj.wait_ge(sems[sn], v)
                if False:
                    for sn, v in dma_totals:
                        eng_obj.wait_ge(sems[sn], v)
                    eng_obj.wait_ge(sems["done"], 4)
                    for sn in self.sem_names:
                        eng_obj.sem_clear(sems[sn])
            elif MK.ENDCLEAR:
                eng_obj.sem_inc(sems["done"], 1)

        with nc.Block() as block:
            block.gpsimd(lambda g: emit_engine(g, "POOL"))
            block.sync(lambda s: emit_engine(s, "SP"))
            block.vector(lambda v: emit_engine(v, "DVE"))
            block.scalar(lambda s: emit_engine(s, "ACT"))
            block.tensor(lambda t: emit_engine(t, "PE"))

    def close(self):
        self.es.close()



def _cols(v, n=128):
    v = np.asarray(v, np.float32)
    assert v.size % n == 0
    return np.ascontiguousarray(v.reshape(-1, n).T)


def _host_tables():
    ident_f32 = np.eye(128, dtype=np.float32)
    ident_bf = np.eye(128, dtype=np.float64)
    cm = np.where(np.tril(np.ones((128, 128), bool)), 0.0, -1e30).astype(np.float32)
    il = np.arange(128)[:, None]
    jl = np.arange(384)[None, :]
    ok = (jl >= il) & (jl <= il + 256)
    lm = np.where(ok, 0.0, -1e30).astype(np.float32)
    DIM_HEAD, WINDOW, n = 64, 256, 2048
    half = np.arange(0, DIM_HEAD, 2, dtype=np.float64)
    inv_freq = 1.0 / 10000.0 ** (half / DIM_HEAD)
    t = np.arange(n, dtype=np.float64)
    freqs = t[:, None] * inv_freq[None, :]
    freqs = np.concatenate([freqs, freqs], -1)
    base_scale = (half + 0.4 * DIM_HEAD) / (1.4 * DIM_HEAD)
    power = (t - n // 2) / (WINDOW // 2)
    scale = base_scale[None, :] ** power[:, None]
    scale = np.concatenate([scale, scale], -1)
    cos = np.cos(freqs)
    sin = np.sin(freqs)
    sign = np.where(np.arange(64) < 32, -1.0, 1.0)
    QK = 8.0
    cosS8 = (cos * scale * QK).T
    sinS8 = (sin * scale * QK * sign[None, :]).T
    cosI = (cos / scale).T
    sinI = (sin / scale * sign[None, :]).T

    def dup(a):
        return np.concatenate([a, a], axis=0)
    rot = [dup(a) for a in (cosS8, sinS8, cosI, sinI)]
    P = np.zeros((128, 128), dtype=np.float64)
    for m in range(128):
        base = (m // 64) * 64
        d = m % 64
        P[base + (d + 32) % 64, m] = 1.0
    pidx = np.arange(128)
    ones2 = np.stack([(pidx < 64).astype(np.float64), (pidx >= 64).astype(np.float64)], 1)
    bcast2 = np.zeros((128, 128), np.float64)
    bcast2[0, 0:64] = 1.0
    bcast2[1, 64:128] = 1.0
    ones1 = np.ones((128, 1), np.float64)
    lmaskbf = np.zeros((128, 384), np.float64)
    lmaskbf[:, :] = lm
    sel = np.zeros((128, 8 * 128), np.float64)
    for a in range(8):
        for i in range(16):
            sel[i, a * 128 + 16 * a + i] = 1.0
    tab_bf = np.concatenate([ident_bf] + rot + [P, ones2, bcast2, ones1, lmaskbf, sel],
                            axis=1).astype(ml_dtypes.bfloat16)
    onesrow_f = np.zeros((128, 128), np.float32)
    onesrow_f[0, :] = 1.0
    tab_f32 = np.concatenate([ident_f32, cm, lm, onesrow_f], axis=1).astype(np.float32)
    return np.ascontiguousarray(tab_bf), np.ascontiguousarray(tab_f32)


TB_IDENT = 0
TB_COSS8 = 128
TB_SINS8 = 128 + 2048
TB_COSI = 128 + 2 * 2048
TB_SINI = 128 + 3 * 2048
TB_PSHIFT = 128 + 4 * 2048
TB_ONES2 = TB_PSHIFT + 128
TB_BCAST2 = TB_ONES2 + 2
TB_ONES1 = TB_BCAST2 + 128
TB_LMASKBF = TB_ONES1 + 1
TB_SEL = TB_LMASKBF + 384
TB_COLS = TB_SEL + 8 * 128

TF_IDENT = 0
TF_CAUSAL = 128
TF_LOCALMASK = 256
TF_ONESROW = 256 + 384
TF_COLS = TF_ONESROW + 128


def _host_consts(inputs):
    cols = []
    layout = {}

    def add(name, arr):
        arr = np.asarray(arr, np.float32)
        layout[name] = (sum(c.shape[1] for c in cols), arr.shape[1])
        cols.append(arr)

    add("bqkv_a", _cols(inputs["L0_attn_b_qkv"]))
    for i in range(6):
        add(f"qs{i}", _cols(np.tile(np.asarray(inputs["L0_ffn_q_scale"])[i], 2)))
        add(f"ks{i}", _cols(np.tile(np.asarray(inputs["L0_ffn_k_scale"])[i], 2)))
        add(f"lng{i}", _cols(inputs["L0_ffn_ln_g"][i]))
        add(f"lnb{i}", _cols(inputs["L0_ffn_ln_b"][i]))
        b1 = np.asarray(inputs["L0_ffn_b1"][i], np.float32)
        b1p = np.zeros(5632, np.float32)
        b1p[:2730] = b1[:2730]
        b1p[2816:2816 + 2730] = b1[2730:]
        add(f"b1{i}", _cols(b1p))
        add(f"b2{i}", _cols(inputs["L0_ffn_b2"][i]))
    add("bgate", _cols(inputs["L1_attn_bgate"]))
    add("decay", _cols(np.repeat(np.asarray(inputs["L1_attn_decay_param"]), 64)))
    consts = np.concatenate(cols, axis=1).astype(np.float32)
    return np.ascontiguousarray(consts), layout


def _host_prep(inputs):
    tab_bf, tab_f32 = _host_tables()
    consts, clayout = _host_consts(inputs)
    W1 = np.asarray(inputs["L0_ffn_W1"], np.float32)
    W2 = np.asarray(inputs["L0_ffn_W2"], np.float32)
    W1p = np.zeros((6, 1024, 5632), np.float32)
    W1p[:, :, :2730] = W1[:, :, :2730]
    W1p[:, :, 2816:2816 + 2730] = W1[:, :, 2730:]
    W2p = np.zeros((6, 2816, 1024), np.float32)
    W2p[:, :2730, :] = W2
    shared = {
        "tab_bf": tab_bf,
        "tab_f32": tab_f32,
        "consts": consts,
        "wqkvT_a": np.ascontiguousarray(np.asarray(inputs["L0_attn_W_qkv"], np.float32).T),
        "wqkv_b": np.ascontiguousarray(np.asarray(inputs["L0_ffn_Wqkv"], np.float32)),
        "wo_b": np.ascontiguousarray(np.asarray(inputs["L0_ffn_Wo"], np.float32)),
        "w1p": W1p,
        "w2p": W2p,
        "wqkvT_c": np.ascontiguousarray(np.asarray(inputs["L1_attn_Wqkv"], np.float32).T),
        "wgateT": np.ascontiguousarray(np.asarray(inputs["L1_attn_Wgate"], np.float32).T),
        "woutT": np.ascontiguousarray(np.asarray(inputs["L1_attn_Wout"], np.float32).T),
        "rmT": np.ascontiguousarray(np.asarray(inputs["L1_ffn_rail_maker"], np.float32).transpose(0, 1, 3, 2)),
        "ruT": np.ascontiguousarray(np.asarray(inputs["L1_ffn_rail_up_proj"], np.float32).transpose(0, 1, 3, 2)),
        "wcT": np.ascontiguousarray(np.asarray(inputs["L1_ffn_w_cproj"], np.float32).T),
    }
    return shared, clayout



PBF_COLS = 38912


class Prog:
    def __init__(self, clayout, stages, debug_out=None):
        self.nc = bass.Bass("TRN2", target_bir_lowering=False)
        self.b = MK(self.nc)
        self.clayout = clayout
        self.stages = stages
        self.debug_out = debug_out
        nc, b = self.nc, self.b

        ap = lambda t: t.ap()
        self.h_x = ap(nc.dram_tensor("x", [T, D], F32, kind="ExternalInput"))
        self.h_tab_bf = ap(nc.dram_tensor("tab_bf", [128, TB_COLS], BF16, kind="ExternalInput"))
        self.h_tab_f32 = ap(nc.dram_tensor("tab_f32", [128, TF_COLS], F32, kind="ExternalInput"))
        ccols = max(off + w for off, w in clayout.values())
        self.h_consts = ap(nc.dram_tensor("consts", [128, ccols], F32, kind="ExternalInput"))
        self.h_wqkvT_a = ap(nc.dram_tensor("wqkvT_a", [D, 3 * D], F32, kind="ExternalInput"))
        self.h_wqkv_b = ap(nc.dram_tensor("wqkv_b", [6, D, 3 * D], F32, kind="ExternalInput"))
        self.h_wo_b = ap(nc.dram_tensor("wo_b", [6, D, D], F32, kind="ExternalInput"))
        self.h_w1p = ap(nc.dram_tensor("w1p", [6, D, 5632], F32, kind="ExternalInput"))
        self.h_w2p = ap(nc.dram_tensor("w2p", [6, 2816, D], F32, kind="ExternalInput"))
        self.h_wqkvT_c = ap(nc.dram_tensor("wqkvT_c", [D, 3 * D], F32, kind="ExternalInput"))
        self.h_wgateT = ap(nc.dram_tensor("wgateT", [D, D], F32, kind="ExternalInput"))
        self.h_woutT = ap(nc.dram_tensor("woutT", [D, D], F32, kind="ExternalInput"))
        self.h_rmT = ap(nc.dram_tensor("rmT", [4, 64, 16, 16], F32, kind="ExternalInput"))
        self.h_ruT = ap(nc.dram_tensor("ruT", [4, 64, 16, 64], F32, kind="ExternalInput"))
        self.h_wcT = ap(nc.dram_tensor("wcT", [16 * D, D], F32, kind="ExternalInput"))
        self.h_out = ap(nc.dram_tensor("out", [T, D], F32, kind="ExternalOutput"))
        self.h_x1sp = ap(nc.dram_tensor("x1sp", [128, 16384], F32, kind="Internal"))
        self.h_vsp = ap(nc.dram_tensor("vsp", [128, 16384], BF16, kind="Internal"))
        self.h_dbg1 = ap(nc.dram_tensor("dbg1", [128, 16384], F32, kind="ExternalOutput"))
        self.h_dbg2 = ap(nc.dram_tensor("dbg2", [128, 16384], F32, kind="ExternalOutput"))
        self.h_dbg3 = ap(nc.dram_tensor("dbg3", [128, 16384], F32, kind="ExternalOutput"))
        self.h_dbg4 = ap(nc.dram_tensor("dbg4", [128, 4096], F32, kind="ExternalOutput"))

        sb = b.sb
        self.xf32 = sb("xf32", [128, 16 * 1024], F32)
        self.xbf = sb("xbf", [128, 8 * 2049], BF16)
        self.tab_bf = sb("tab_bf_sb", [128, TB_COLS], BF16)
        self.tab_f32 = sb("tab_f32_sb", [128, TF_COLS], F32)
        self.consts = sb("consts_sb", [128, ccols], F32)
        self.scr = sb("scr", [128, 96], F32)
        self.wbuf = sb("wbuf", [128, 2 * 2816], BF16)
        self.pbf = sb("pbf", [128, PBF_COLS], BF16)
        self.pbf_f32 = self.pbf.bitcast(F32)

        self.psA = b.ps("psA", [128, 2048], F32)
        self.psB = b.ps("psB", [128, 1024], F32)
        self.psD = b.ps("psD", [128, 2048], BF16)
        rem = nc.sbuf_bytes_remaining
        self.sbuf_remaining = rem

    def ccol(self, name, j0=0, w=1):
        off, width = self.clayout[name]
        assert j0 + w <= width
        return self.consts[:, off + j0: off + j0 + w]

    def load_commons(self):
        b = self.b
        b.dma("tabs", self.tab_bf[:, 0:128], self.h_tab_bf[:, 0:128],
              writes=["tb_ident"], queue="SP", tag="ld_ident_bf")
        b.dma("tabs", self.tab_bf[:, TB_PSHIFT:TB_COLS],
              self.h_tab_bf[:, TB_PSHIFT:TB_COLS],
              writes=["tb_pshift", "tb_tail"], queue="SP", tag="ld_pshift_tail")
        b.dma("tabs", self.tab_f32[:], self.h_tab_f32,
              writes=["tab_f32"], queue="SP", tag="ld_tabf32")
        b.dma("tabs", self.consts[:], self.h_consts,
              writes=["consts"], queue="SP", tag="ld_consts")

    def load_rotary(self):
        self.b.dma("tabs", self.tab_bf[:, 128:128 + 4 * 2048],
                   self.h_tab_bf[:, 128:128 + 4 * 2048],
                   writes=["tb_rot", "wqA"], queue="SP", tag="ld_rot")

    @property
    def ident_bf(self):
        return self.tab_bf[:, 0:128]

    @property
    def ident_f32(self):
        return self.tab_f32[:, 0:128]



def stage_a(p: Prog):
    b = p.b
    IDb = p.ident_bf

    k_fm = p.pbf[:, 0:16384]
    xbftm = p.pbf[:, 0:16384]
    v_tm = p.pbf[:, 16384:32768]
    q_q = p.pbf[:, 32768:36864]
    v_stage = p.pbf[:, 36864:38912]
    dump = p.pbf[:, 36864:38912]
    xfm = lambda d0: p.xbf[:, d0 * 2049 + 1: d0 * 2049 + 1 + 2048]
    wq = p.tab_bf[:, 128:128 + 8192]
    wkv0 = p.wbuf[:, 0:1024]
    wkv1 = p.wbuf[:, 1024:2048]
    attn_bf = p.wbuf[:, 2048:4096]
    attnT = [p.wbuf[:, 4096:4224], p.wbuf[:, 4224:4352]]
    stats = p.scr[:, 0:16]
    bq32 = p.scr[:, 16:24]
    eps_col = p.scr[:, 24:25]

    b.op("DVE", lambda v: v.memset(eps_col, 1e-5), writes=["eps"], tag="eps")
    b.op("DVE", lambda v: v.memset(p.scr[:, 25:26], 1e-24), writes=["eps"], tag="e24")

    b.dma("xin", p.xf32[:].rearrange("p (a d) -> p a d", d=1024),
          p.h_x.rearrange("(a p) d -> p a d", p=128),
          writes=[f"xtm{t}" for t in range(NT)], queue="SP", tag="ld_x")

    b.op("DVE", lambda v: v.memset(p.xbf[:].rearrange("p (a c) -> p a c", c=2049)[:, :, 0:1], 0.0),
         writes=["xbf_zc"], tag="zc")

    b.op("DVE", lambda v: v.tensor_copy(xbftm, p.xf32[:]),
         reads=[f"xtm{t}" for t in range(NT)], writes=["xbftm"], tag="cast_xtm")

    for dt in range(ND):
        for half in range(2):
            def em(t, dt=dt, half=half):
                ins = None
                for g in range(8):
                    tt = half * 8 + g
                    ins = t.transpose(p.psD[:, half * 1024 + g * 128: half * 1024 + (g + 1) * 128],
                                      xbftm[:, tt * 1024 + dt * 128: tt * 1024 + dt * 128 + 128],
                                      IDb)
                return ins
            b.op("PE", em, reads=["xbftm", "tb_ident"], writes=[f"psD{half}"], tag=f"trA{dt}_{half}")
            b.op("DVE", lambda v, dt=dt, half=half: v.tensor_copy(
                xfm(dt)[:, half * 1024: half * 1024 + 1024],
                p.psD[:, half * 1024: half * 1024 + 1024]),
                reads=[f"psD{half}"], writes=[f"xfmbf{dt}"], tag=f"evA{dt}_{half}")

    XFM_KEYS = [f"xfmbf{d}" for d in range(ND)]

    b.dma("wA", wq.rearrange("p (k c) -> p k c", c=1024),
          p.h_wqkvT_a[:, 0:1024].rearrange("(k p) c -> p k c", p=128),
          writes=["wqA"], tag="ld_wq")
    b.op("DVE", lambda v: v.tensor_scalar_mul(bq32, p.ccol("bqkv_a", 0, 8), 1.0 / 32.0),
         reads=["consts"], writes=["bq32"], tag="bq32")

    for m in range(8, 24):
        slot = [wkv0, wkv1][m % 2]
        skey = f"wkv{m % 2}"
        b.dma("wA", slot.rearrange("p (k c) -> p k c", c=128),
              p.h_wqkvT_a[:, m * 128:(m + 1) * 128].rearrange("(k p) c -> p k c", p=128),
              writes=[skey], tag=f"wA{m}")
        for tc in range(4):
            bank = tc % 2
            def em(t, slot=slot, tc=tc, bank=bank):
                ins = None
                for k in range(ND):
                    ins = t.matmul(p.psB[:, bank * 512:(bank + 1) * 512],
                                   slot[:, k * 128:(k + 1) * 128],
                                   xfm(k)[:, tc * 512:(tc + 1) * 512],
                                   start=(k == 0), stop=(k == ND - 1))
                return ins
            b.op("PE", em, reads=[skey] + XFM_KEYS, writes=[f"psB{bank}"],
                 tag=f"kv_mm{m}_{tc}")
            if m < 16:
                dst = k_fm[:, (m - 8) * 2048 + tc * 512: (m - 8) * 2048 + (tc + 1) * 512]
                b.op("ACT", lambda s, dst=dst, bank=bank, m=m: s.activation(
                    dst, p.psB[:, bank * 512:(bank + 1) * 512], AF.Identity,
                    bias=p.ccol("bqkv_a", m, 1)),
                    reads=[f"psB{bank}", "consts"],
                    writes=[f"k{m - 8}", "xbftm"], tag=f"ev_k{m}_{tc}")
            else:
                dst = v_stage[:, tc * 512:(tc + 1) * 512]
                b.op("ACT", lambda s, dst=dst, bank=bank, m=m: s.activation(
                    dst, p.psB[:, bank * 512:(bank + 1) * 512], AF.Identity,
                    bias=p.ccol("bqkv_a", m, 1)),
                    reads=[f"psB{bank}", "consts"], writes=[f"vs{tc}"], tag=f"ev_v{m}_{tc}")
        if m >= 16:
            vd = m - 16
            for half in range(2):
                def em(t, half=half):
                    ins = None
                    for g in range(8):
                        tt = half * 8 + g
                        ins = t.transpose(p.psD[:, half * 1024 + g * 128: half * 1024 + (g + 1) * 128],
                                          v_stage[:, tt * 128:(tt + 1) * 128], IDb)
                    return ins
                b.op("PE", em, reads=[f"vs{half * 2}", f"vs{half * 2 + 1}", "tb_ident"],
                     writes=[f"psD{half}"], tag=f"trV{vd}_{half}")
                def evv(v, vd=vd, half=half):
                    ins = None
                    for g in range(8):
                        tt = half * 8 + g
                        ins = v.tensor_copy(
                            v_tm[:, tt * 1024 + vd * 128: tt * 1024 + vd * 128 + 128],
                            p.psD[:, half * 1024 + g * 128: half * 1024 + (g + 1) * 128])
                    return ins
                b.op("DVE", evv, reads=[f"psD{half}"], writes=["vtm", f"vs{half*2}", f"vs{half*2+1}"],
                     tag=f"evV{vd}_{half}")

    if p.debug_out == "dbgA":
        b.dma("dbg", p.h_dbg2, k_fm, reads=[f"k{kk}" for kk in range(8)],
              tag="dbg_k", is_output=True)
        b.dma("dbg", p.h_dbg3, v_tm, reads=["vtm"], tag="dbg_v", is_output=True)

    for qq in range(4):
        for mq in range(8):
            bank = mq % 2
            def em(t, mq=mq, qq=qq, bank=bank):
                ins = None
                for k in range(ND):
                    ins = t.matmul(p.psB[:, bank * 512:(bank + 1) * 512],
                                   wq[:, k * 1024 + mq * 128: k * 1024 + mq * 128 + 128],
                                   xfm(k)[:, qq * 512:(qq + 1) * 512],
                                   start=(k == 0), stop=(k == ND - 1))
                return ins
            b.op("PE", em, reads=["wqA"] + XFM_KEYS, writes=[f"psB{bank}"], tag=f"qmm{qq}_{mq}")
            b.op("ACT", lambda s, mq=mq, bank=bank: s.activation(
                q_q[:, mq * 512:(mq + 1) * 512],
                p.psB[:, bank * 512:(bank + 1) * 512],
                AF.Identity, bias=bq32[:, mq:mq + 1], scale=1.0 / 32.0),
                reads=[f"psB{bank}", "bq32"], writes=["qq"], tag=f"qev{qq}_{mq}")

        if p.debug_out == "dbgA" and qq == 0:
            b.dma("dbg", p.h_dbg1[:, 0:4096], q_q, reads=["qq"], tag="dbg_q", is_output=True)
        for il in range(4):
            i = qq * 4 + il
            ncol = (i + 1) * 128
            nj = i + 1
            def em_s(t, i=i, ncol=ncol, il=il):
                ins = None
                for k in range(ND):
                    q_l = q_q[:, k * 512 + il * 128: k * 512 + il * 128 + 128]
                    for c0 in range(0, ncol, 512):
                        w = min(512, ncol - c0)
                        ins = t.matmul(p.psA[:, c0:c0 + w], q_l,
                                       k_fm[:, k * 2048 + c0: k * 2048 + c0 + w],
                                       start=(k == 0), stop=(k == ND - 1))
                return ins
            b.op("PE", em_s, reads=["qq"] + [f"k{kk}" for kk in range(ND)],
                 writes=["psA0", "psA1", "psA2", "psA3"], tag=f"sc{i}")
            b.op("DVE", lambda v, ncol=ncol: v.tensor_tensor(
                p.psA[:, ncol - 128:ncol], p.psA[:, ncol - 128:ncol],
                p.tab_f32[:, TF_CAUSAL:TF_CAUSAL + 128], ALU.add),
                reads=["psA0", "psA1", "psA2", "psA3", "tab_f32"],
                writes=[f"psA{(ncol - 1) // 512}"], tag=f"cm{i}")
            b.op("DVE", lambda v, ncol=ncol: v.reduce_max(
                stats[:, 0:1], p.psA[:, 0:ncol], axis=AX.X, negate=True),
                reads=["psA0", "psA1", "psA2", "psA3"], writes=["negmax"], tag=f"mx{i}")
            b.op("ACT", lambda s, ncol=ncol: s.activation(
                attn_bf[:, 0:ncol], p.psA[:, 0:ncol], AF.Exp,
                bias=stats[:, 0:1], accum_out=stats[:, 1:2]),
                reads=["psA0", "psA1", "psA2", "psA3", "negmax"],
                writes=["attn", "rowsum", "wkv0", "wkv1"], tag=f"exp{i}")
            for jc in range(nj):
                sl = jc % 2
                b.op("PE", lambda t, jc=jc, sl=sl: t.transpose(
                    p.psD[:, sl * 1024:sl * 1024 + 128],
                    attn_bf[:, jc * 128:(jc + 1) * 128], IDb),
                    reads=["attn", "tb_ident"], writes=[f"psD{sl}"], tag=f"trS{i}_{jc}")
                b.op("DVE", lambda v, sl=sl: v.tensor_copy(
                    attnT[sl], p.psD[:, sl * 1024:sl * 1024 + 128]),
                    reads=[f"psD{sl}"], writes=[f"attnT{sl}"], tag=f"evS{i}_{jc}")
                def em_av(t, jc=jc, nj=nj, sl=sl):
                    ins = None
                    at = attnT[sl]
                    for h in range(2):
                        ins = t.matmul(p.psB[:, h * 512:(h + 1) * 512], at,
                                       v_tm[:, jc * 1024 + h * 512: jc * 1024 + (h + 1) * 512],
                                       start=(jc == 0), stop=(jc == nj - 1))
                    return ins
                b.op("PE", em_av, reads=[f"attnT{sl}", "vtm"], writes=["psB0", "psB1"],
                     tag=f"av{i}_{jc}")
            if p.debug_out == "dbgA" and i == 2:
                b.dma("dbg", p.h_dbg1[:, 4096:4096 + 2048], attn_bf[:, 0:2048],
                      reads=["attn"], tag="dbg_attn", is_output=True)
                def cpy(v):
                    return v.tensor_copy(p.pbf_f32[:, 16384:16384 + 1024], p.psB[:])
                b.op("DVE", cpy, reads=["psB0", "psB1"], writes=["dbgy", "qq"], tag="dbg_ycp")
                b.dma("dbg", p.h_dbg4[:, 0:1024], p.pbf_f32[:, 16384:16384 + 1024],
                      reads=["dbgy"], tag="dbg_y", is_output=True)
                b.dma("dbg", p.h_dbg4[:, 1024:1040], stats, reads=["rowsum", "negmax"],
                      tag="dbg_st", is_output=True)
            xt = p.xf32[:, i * 1024:(i + 1) * 1024]
            b.op("DVE", lambda v: v.reciprocal(stats[:, 2:3], stats[:, 1:2]),
                 reads=["rowsum"], writes=["rsinv"], tag=f"ri{i}")
            b.op("DVE", lambda v, xt=xt: v.scalar_tensor_tensor(
                xt, p.psB[:], stats[:, 2:3], xt, ALU.mult, ALU.add),
                reads=["psB0", "psB1", "rsinv", f"xtm{i}"], writes=[f"xtm{i}"],
                tag=f"res{i}")
            b.op("DVE", lambda v, xt=xt: v.reduce_sum(stats[:, 3:4], xt, axis=AX.X),
                 reads=[f"xtm{i}"], writes=["musum"], tag=f"mu{i}")
            b.op("DVE", lambda v: v.tensor_scalar_mul(stats[:, 4:5], stats[:, 3:4], 1.0 / 1024.0),
                 reads=["musum"], writes=["mu"], tag=f"mu2{i}")
            b.op("DVE", lambda v, xt=xt: v.tensor_scalar(
                xt, xt, stats[:, 4:5], None, ALU.subtract),
                reads=[f"xtm{i}", "mu"], writes=[f"xtm{i}"], tag=f"ctr{i}")
            b.op("ACT", lambda s, xt=xt: s.activation(
                dump[:, 0:1024], xt, AF.Square, accum_out=stats[:, 5:6]),
                reads=[f"xtm{i}"], writes=["sqsum", "vs0", "vs1", "vs2", "vs3"], tag=f"sq{i}")
            b.op("ACT", lambda s: s.activation(stats[:, 6:7], stats[:, 5:6], AF.Ln,
                                               scale=1.0 / 1024.0, bias=eps_col),
                 reads=["sqsum", "eps"], writes=["lnvar"], tag=f"lnv{i}")
            b.op("ACT", lambda s: s.activation(stats[:, 7:8], stats[:, 6:7], AF.Exp,
                                               scale=-0.5),
                 reads=["lnvar"], writes=["rstd"], tag=f"rstd{i}")
            b.op("DVE", lambda v, xt=xt: v.tensor_scalar(
                xt, xt, stats[:, 7:8], None, ALU.mult),
                reads=[f"xtm{i}", "rstd"], writes=[f"xtm{i}"], tag=f"ln{i}")

    if p.debug_out in ("x1", "dbgA"):
        b.dma("out", p.h_out.rearrange("(a p) d -> p a d", p=128),
              p.xf32[:].rearrange("p (a d) -> p a d", d=1024),
              reads=[f"xtm{t}" for t in range(NT)], queue="SP", tag="st_x1",
              is_output=True)





def conv_a2b(p: Prog):
    b = p.b
    M = p.pbf_f32
    oldkeys = [f"k{kk}" for kk in range(8)] + ["vtm", "qq", "xbftm"]
    oldx = [f"xfmbf{d}" for d in range(ND)]
    for dt in range(ND):
        for g4 in range(4):
            bank = g4 % 4
            def em(t, dt=dt, g4=g4, bank=bank):
                ins = None
                for q in range(4):
                    tt = g4 * 4 + q
                    ins = t.transpose(p.psA[:, bank * 512 + q * 128: bank * 512 + (q + 1) * 128],
                                      p.xf32[:, tt * 1024 + dt * 128: tt * 1024 + dt * 128 + 128],
                                      p.ident_f32)
                return ins
            b.op("PE", em,
                 reads=[f"xtm{g4 * 4 + q}" for q in range(4)] + ["tab_f32"],
                 writes=[f"psA{bank}"], tag=f"cv{dt}_{g4}")
            wk = [f"M{dt}"] + (oldkeys if (dt == 0 and g4 == 0) else [])
            b.op("DVE", lambda v, dt=dt, g4=g4, bank=bank: v.tensor_copy(
                M[:, dt * 2048 + g4 * 512: dt * 2048 + (g4 + 1) * 512],
                p.psA[:, bank * 512:(bank + 1) * 512]),
                reads=[f"psA{bank}"], writes=wk, tag=f"cvM{dt}_{g4}")
            wk2 = [f"xb{dt}"] + (oldx if (dt == 0 and g4 == 0) else [])
            b.op("DVE", lambda v, dt=dt, g4=g4, bank=bank: v.tensor_copy(
                p.xbf[:, dt * 2049 + 1 + g4 * 512: dt * 2049 + 1 + (g4 + 1) * 512],
                p.psA[:, bank * 512:(bank + 1) * 512]),
                reads=[f"psA{bank}"], writes=wk2, tag=f"cvX{dt}_{g4}")



def stage_b_layer(p: Prog, li: int):
    b = p.b
    IDb = p.ident_bf
    M = p.pbf_f32
    S = p.xf32.bitcast(BF16)
    Sf = p.xf32
    PT = p.pbf_f32
    lnss = PT[:, 16384:18432]
    statsA = PT[0:1, 16896:17408]
    statsB = PT[0:1, 17408:17920]
    w2tmp = PT[:, 16896:17408]

    otm = S[:, 0:16384]
    qp = S[:, 16384:18432]
    kp = S[:, 18432:20480]
    qrot = S[:, 20480:22528]
    krot = S[:, 22528:24576]
    vstage = S[:, 24576:26624]
    vtm = S[:, 26624:28672]
    tmp2 = S[:, 28672:30720]
    attnS = [S[:, 30720:31104], S[:, 31104:31488],
             vstage[:, 0:384], vstage[:, 384:768]]
    attnTS = [S[:, 31488:31872], S[:, 31872:32256],
              vstage[:, 768:1152], vstage[:, 1152:1536]]
    ofm = S[:, 16384:32768]
    ff = S[:, 0:22528]
    xn = S[:, 22528:30720]
    a_sb = S[:, 30720:31744]
    g_sb = S[:, 31744:32768]
    sqb = [p.pbf[:, 32768:33280], p.pbf[:, 33280:33792]]
    w2slotB = p.pbf[:, 35840:38656]
    wsl = [p.wbuf[:, 0:1024], p.wbuf[:, 1024:2048]]
    w2slotA = p.wbuf[:, 2048:4864]
    wosl = [p.wbuf[:, 2048:3072], p.wbuf[:, 3072:4096]]

    stats = p.scr
    eps_col = p.scr[:, 24:25]
    e24_col = p.scr[:, 25:26]
    negmaxS = [p.scr[:, 26:27], p.scr[:, 27:28], p.scr[:, 28:29], p.scr[:, 29:30]]
    rs = p.scr[:, 32:64]
    rcp = p.scr[:, 64:80]
    qs_col = p.ccol(f"qs{li}")
    ks_col = p.ccol(f"ks{li}")

    def xsh(k):
        def sl(t0, w):
            base = k * 2049 + 1 + t0 - (1 if k >= 4 else 0)
            return p.xbf[:, base: base + w]
        return sl

    L = f"L{li}_"
    XB = [f"xb{d}" for d in range(ND)]
    MK_ = [f"M{d}" for d in range(ND)]

    for pr in range(8):
        pk = f"{L}p{pr}_"
        for sl_i, mbase, dstbuf, dkey in ((0, pr * 128, qp, pk + "qp"),
                                          (1, 1024 + pr * 128, kp, pk + "kp"),
                                          (2, 2048 + pr * 128, vstage, pk + "vs")):
            slot = wsl[sl_i % 2]
            skey = f"wB{sl_i % 2}"
            b.dma("w", slot.rearrange("p (k c) -> p k c", c=128),
                  p.h_wqkv_b[li, :, mbase:mbase + 128].rearrange("(k p) c -> p k c", p=128),
                  writes=[skey], tag=pk + f"wd{sl_i}")
            for tc in range(4):
                bank = 2 + tc % 2
                def em(t, slot=slot, tc=tc, bank=bank):
                    ins = None
                    for k in range(ND):
                        ins = t.matmul(p.psA[:, bank * 512:(bank + 1) * 512],
                                       slot[:, k * 128:(k + 1) * 128],
                                       xsh(k)(tc * 512, 512),
                                       start=(k == 0), stop=(k == ND - 1))
                    return ins
                b.op("PE", em, reads=[skey] + XB, writes=[f"psA{bank}"],
                     tag=pk + f"qkv{sl_i}_{tc}")
                wk_ev = [dkey] + (["attn2", "attn3", "attnT2", "attnT3"]
                                  if sl_i == 2 else [])
                b.op("ACT", lambda s, dstbuf=dstbuf, tc=tc, bank=bank: s.activation(
                    dstbuf[:, tc * 512:(tc + 1) * 512],
                    p.psA[:, bank * 512:(bank + 1) * 512], AF.Identity),
                    reads=[f"psA{bank}"], writes=wk_ev, tag=pk + f"qkvev{sl_i}_{tc}")
        for half in range(2):
            def em(t, half=half):
                ins = None
                for g in range(8):
                    tt = half * 8 + g
                    ins = t.transpose(p.psD[:, half * 1024 + g * 128: half * 1024 + (g + 1) * 128],
                                      vstage[:, tt * 128:(tt + 1) * 128], IDb)
                return ins
            b.op("PE", em, reads=[pk + "vs", "tb_ident"], writes=[f"psD{half}"],
                 tag=pk + f"vtr{half}")
            def evv(v, half=half):
                ins = None
                for g in range(8):
                    tt = half * 8 + g
                    ins = v.tensor_copy(vtm[:, tt * 128:(tt + 1) * 128],
                                        p.psD[:, half * 1024 + g * 128: half * 1024 + (g + 1) * 128])
                return ins
            b.op("DVE", evv, reads=[f"psD{half}"], writes=[pk + "vtm"], tag=pk + f"vev{half}")

        T2 = pk + "tmp2"
        for which, buf, rot, scol, ctab, stab, bkey in (
                ("q", qp, qrot, qs_col, TB_COSS8, TB_SINS8, pk + "qp"),
                ("k", kp, krot, ks_col, TB_COSI, TB_SINI, pk + "kp")):
            wk = pk + which
            b.op("DVE", lambda v, buf=buf: v.tensor_tensor(tmp2, buf, buf, ALU.mult),
                 reads=[bkey], writes=[T2], tag=wk + "sq")
            def em(t):
                ins = None
                for c0 in range(0, 2048, 512):
                    ins = t.matmul(p.psA[0:2, c0:c0 + 512],
                                   p.tab_bf[:, TB_ONES2:TB_ONES2 + 2],
                                   tmp2[:, c0:c0 + 512], start=True, stop=True)
                return ins
            b.op("PE", em, reads=[T2, "tb_tail"],
                 writes=["psA0", "psA1", "psA2", "psA3"], tag=wk + "ss")
            b.op("ACT", lambda s: s.activation(lnss[0:2, :], p.psA[0:2, 0:2048], AF.Ln,
                                               bias=e24_col[0:2, :]),
                 reads=["psA0", "psA1", "psA2", "psA3", "eps"], writes=[wk + "lnss"],
                 tag=wk + "ln")
            b.op("ACT", lambda s: s.activation(tmp2[0:2, 0:2048], lnss[0:2, :], AF.Exp,
                                               scale=-0.5),
                 reads=[wk + "lnss"], writes=[T2], tag=wk + "exp")
            def em2(t):
                ins = None
                for c0 in range(0, 2048, 512):
                    ins = t.matmul(p.psA[:, c0:c0 + 512],
                                   p.tab_bf[0:2, TB_BCAST2:TB_BCAST2 + 128],
                                   tmp2[0:2, c0:c0 + 512], start=True, stop=True)
                return ins
            b.op("PE", em2, reads=[T2, "tb_tail"],
                 writes=["psA0", "psA1", "psA2", "psA3"], tag=wk + "bc")
            b.op("DVE", lambda v, buf=buf, scol=scol: v.scalar_tensor_tensor(
                buf, buf, scol, p.psA[:, 0:2048], ALU.mult, ALU.mult),
                reads=[bkey, "consts", "psA0", "psA1", "psA2", "psA3"],
                writes=[bkey], tag=wk + "nrm")
            def em3(t, buf=buf):
                ins = None
                for c0 in range(0, 2048, 512):
                    ins = t.matmul(p.psA[:, c0:c0 + 512],
                                   p.tab_bf[:, TB_PSHIFT:TB_PSHIFT + 128],
                                   buf[:, c0:c0 + 512], start=True, stop=True)
                return ins
            b.op("PE", em3, reads=[bkey, "tb_pshift"],
                 writes=["psA0", "psA1", "psA2", "psA3"], tag=wk + "sh")
            b.op("DVE", lambda v, buf=buf, rot=rot, ctab=ctab: v.tensor_tensor(
                rot, buf, p.tab_bf[:, ctab:ctab + 2048], ALU.mult),
                reads=[bkey, "tb_rot"], writes=[wk + "rc"], tag=wk + "rc")
            b.op("DVE", lambda v, stab=stab: v.tensor_tensor(
                tmp2, p.psA[:, 0:2048], p.tab_bf[:, stab:stab + 2048], ALU.mult),
                reads=["psA0", "psA1", "psA2", "psA3", "tb_rot"],
                writes=[T2], tag=wk + "rs2")
            b.op("DVE", lambda v, rot=rot: v.tensor_tensor(rot, rot, tmp2, ALU.add),
                 reads=[wk + "rc", T2], writes=[pk + which + "rot"], tag=wk + "rot")

        for qt in range(NT):
            j0t = max(0, qt - 2)
            nj = qt - j0t + 1
            w = nj * 128
            for hh in range(2):
                u = 2 * qt + hh
                sl4 = u % 4
                bank = u % 4
                hbase = hh * 64
                hk = pk + f"h{hh}_"
                attn = attnS[sl4]
                attnT = attnTS[sl4]
                negmax = negmaxS[sl4]
                b.op("PE", lambda t, qt=qt, j0t=j0t, w=w, bank=bank, hbase=hbase: t.matmul(
                    p.psA[:, bank * 512: bank * 512 + w],
                    qrot[hbase:hbase + 64, qt * 128:(qt + 1) * 128],
                    krot[hbase:hbase + 64, j0t * 128: j0t * 128 + w],
                    start=True, stop=False, tile_position=(hbase, 0),
                    skip_group_check=True),
                    reads=[pk + "qrot", pk + "krot"], writes=[f"psA{bank}"],
                    tag=hk + f"sc{qt}", cost=250)
                b.op("PE", lambda t, w=w, bank=bank: t.matmul(
                    p.psA[:, bank * 512: bank * 512 + w], IDb,
                    p.tab_bf[:, TB_LMASKBF + 384 - w: TB_LMASKBF + 384],
                    start=False, stop=True, skip_group_check=True),
                    reads=["tb_ident", "tb_tail", f"psA{bank}"], writes=[f"psA{bank}"],
                    tag=hk + f"msk{qt}", cost=200)
                b.op("DVE", lambda v, w=w, bank=bank, negmax=negmax: v.reduce_max(
                    negmax, p.psA[:, bank * 512: bank * 512 + w], axis=AX.X, negate=True),
                    reads=[f"psA{bank}"], writes=[f"negmax{sl4}"], tag=hk + f"mx{qt}",
                    cost=600)
                b.op("ACT", lambda s, w=w, bank=bank, attn=attn, negmax=negmax, hh=hh, qt=qt: s.activation(
                    attn[:, 0:w], p.psA[:, bank * 512: bank * 512 + w], AF.Exp,
                    bias=negmax, accum_out=rs[:, hh * 16 + qt: hh * 16 + qt + 1]),
                    reads=[f"psA{bank}", f"negmax{sl4}"],
                    writes=[f"attn{sl4}", f"rs{hh}_{qt}"],
                    tag=hk + f"exp{qt}", cost=800)
                dsl = u % 2
                def emt(t, nj=nj, dsl=dsl, attn=attn):
                    ins = None
                    for jj in range(nj):
                        ins = t.transpose(
                            p.psD[:, dsl * 1024 + jj * 128: dsl * 1024 + (jj + 1) * 128],
                            attn[:, jj * 128:(jj + 1) * 128], IDb)
                    return ins
                b.op("PE", emt, reads=[f"attn{sl4}", "tb_ident"], writes=[f"psD{dsl}"],
                     tag=hk + f"tr{qt}", cost=300 * nj)
                b.op("DVE", lambda v, w=w, dsl=dsl, attnT=attnT: v.tensor_copy(
                    attnT[:, 0:w], p.psD[:, dsl * 1024: dsl * 1024 + w]),
                    reads=[f"psD{dsl}"], writes=[f"attnT{sl4}"], tag=hk + f"trev{qt}",
                    cost=450)
                def emav(t, qt=qt, nj=nj, j0t=j0t, hbase=hbase, hh=hh, attnT=attnT):
                    ins = None
                    for jj in range(nj):
                        ins = t.matmul(
                            p.psB[:, hh * 512 + (qt % 8) * 64: hh * 512 + (qt % 8) * 64 + 64],
                            attnT[:, jj * 128:(jj + 1) * 128],
                            vtm[:, (j0t + jj) * 128 + hbase: (j0t + jj) * 128 + hbase + 64],
                            start=(jj == 0), stop=(jj == nj - 1))
                    return ins
                b.op("PE", emav, reads=[f"attnT{sl4}", pk + "vtm"], writes=[f"psB{hh}"],
                     tag=hk + f"av{qt}", cost=150 * nj)
            if qt % 8 == 7:
                half8 = qt // 8
                for hh in range(2):
                    b.op("DVE", lambda v, hh=hh, half8=half8: v.reciprocal(
                        rcp[:, hh * 8: hh * 8 + 8],
                        rs[:, hh * 16 + half8 * 8: hh * 16 + half8 * 8 + 8]),
                        reads=[f"rs{hh}_{q2}" for q2 in range(half8 * 8, half8 * 8 + 8)],
                        writes=[f"rcp{hh}"], tag=pk + f"rcp{hh}_{half8}", cost=200)
                    for q2 in range(half8 * 8, half8 * 8 + 8):
                        h = 2 * pr + hh
                        b.op("ACT", lambda s, q2=q2, h=h, hh=hh: s.activation(
                            otm[:, q2 * 1024 + h * 64: q2 * 1024 + h * 64 + 64],
                            p.psB[:, hh * 512 + (q2 % 8) * 64: hh * 512 + (q2 % 8) * 64 + 64],
                            AF.Identity, scale=rcp[:, hh * 8 + q2 % 8: hh * 8 + q2 % 8 + 1]),
                            reads=[f"psB{hh}", f"rcp{hh}"], writes=[f"otm{q2}"],
                            tag=pk + f"oev{hh}_{q2}", cost=250)

    OTM = [f"otm{t}" for t in range(NT)]
    for dd in range(8):
        for half in range(2):
            def em(t, dd=dd, half=half):
                ins = None
                for g in range(8):
                    tt = half * 8 + g
                    ins = t.transpose(p.psD[:, half * 1024 + g * 128: half * 1024 + (g + 1) * 128],
                                      otm[:, tt * 1024 + dd * 128: tt * 1024 + dd * 128 + 128],
                                      IDb)
                return ins
            b.op("PE", em, reads=OTM + ["tb_ident"], writes=[f"psD{half}"],
                 tag=L + f"otr{dd}_{half}")
            wk = [L + f"ofm{dd}"] + (["ofm_init"] + [pk + x for pk in [f"{L}p{q}_" for q in range(8)]
                                                    for x in ("qp", "kp", "vs", "vtm", "qrot", "krot")]
                   if (dd == 0 and half == 0) else [])
            b.op("DVE", lambda v, dd=dd, half=half: v.tensor_copy(
                ofm[:, dd * 2048 + half * 1024: dd * 2048 + (half + 1) * 1024],
                p.psD[:, half * 1024:(half + 1) * 1024]),
                reads=[f"psD{half}"], writes=wk, tag=L + f"oev{dd}_{half}")
    OFM = [L + f"ofm{d}" for d in range(8)]
    for m in range(8):
        slot = wosl[m % 2]
        skey = f"wO{m % 2}"
        b.dma("w", slot.rearrange("p (k c) -> p k c", c=128),
              p.h_wo_b[li, :, m * 128:(m + 1) * 128].rearrange("(k p) c -> p k c", p=128),
              writes=[skey], tag=L + f"wod{m}")
        for tc in range(4):
            bank = tc % 2
            def em(t, slot=slot, tc=tc, bank=bank):
                ins = None
                for k in range(ND):
                    ins = t.matmul(p.psB[:, bank * 512:(bank + 1) * 512],
                                   slot[:, k * 128:(k + 1) * 128],
                                   ofm[:, k * 2048 + tc * 512: k * 2048 + (tc + 1) * 512],
                                   start=(k == 0), stop=(k == ND - 1))
                return ins
            b.op("PE", em, reads=[skey] + OFM, writes=[f"psB{bank}"], tag=L + f"wo{m}_{tc}")
            Msl = M[:, m * 2048 + tc * 512: m * 2048 + (tc + 1) * 512]
            b.op("DVE", lambda v, Msl=Msl, bank=bank: v.tensor_tensor(
                Msl, Msl, p.psB[:, bank * 512:(bank + 1) * 512], ALU.add),
                reads=[f"psB{bank}", f"M{m}"], writes=[f"M{m}"], tag=L + f"wor{m}_{tc}")
            b.op("DVE", lambda v, m=m, tc=tc, Msl=Msl: v.tensor_copy(
                p.xbf[:, m * 2049 + 1 + tc * 512: m * 2049 + 1 + (tc + 1) * 512], Msl),
                reads=[f"M{m}"], writes=[f"xb{m}"], tag=L + f"wox{m}_{tc}")

    if p.debug_out == f"b{li}h":
        b.dma("out", p.h_out.rearrange("(a p) d -> p a d", p=128),
              M[:, 0:16384].rearrange("p (a d) -> p a d", d=1024),
              reads=[f"M{d}" for d in range(ND)], queue="SP", tag="st_Mh",
              is_output=True)
    if p.debug_out == f"dbgB{li}":
        b.dma("dbg", p.h_dbg1, qp, reads=[f"{L}p7_qp"], tag="dbg_qp", is_output=True)
        b.dma("dbg", p.h_dbg2, krot, reads=[f"{L}p7_krot"], tag="dbg_krot", is_output=True)
        b.dma("dbg", p.h_dbg3, otm, reads=[f"otm{t}" for t in range(NT)],
              tag="dbg_otm", is_output=True)

    for half in range(2):
        fk = L + f"f{half}_"
        t0h = half * 1024
        for tci in range(2):
            t0 = t0h + tci * 512
            ck = fk + f"c{tci}_"
            for k in range(ND):
                sq = sqb[k % 2]
                b.op("DVE", lambda v, k=k, t0=t0, sq=sq: v.tensor_tensor(
                    sq, xsh(k)(t0, 512), xsh(k)(t0, 512), ALU.mult),
                    reads=[f"xb{k}"], writes=[L + f"sqb{k % 2}"], tag=ck + f"sq{k}")
                def em(t, k=k, t0=t0, sq=sq):
                    t.matmul(p.psB[0:1, 0:512], p.tab_bf[:, TB_ONES1:TB_ONES1 + 1],
                             xsh(k)(t0, 512), start=(k == 0), stop=(k == ND - 1))
                    return t.matmul(p.psB[0:1, 512:1024], p.tab_bf[:, TB_ONES1:TB_ONES1 + 1],
                                    sq, start=(k == 0), stop=(k == ND - 1))
                b.op("PE", em, reads=[f"xb{k}", L + f"sqb{k % 2}", "tb_tail"],
                     writes=["psB0", "psB1"], tag=ck + f"ssum{k}")
            b.op("DVE", lambda v: v.tensor_scalar_mul(statsA, p.psB[0:1, 0:512], 1.0 / 1024.0),
                 reads=["psB0"], writes=[ck + "mu", "PTtmp"], tag=ck + "mu")
            b.op("DVE", lambda v: v.tensor_scalar_mul(statsB, p.psB[0:1, 512:1024], 1.0 / 1024.0),
                 reads=["psB1"], writes=[ck + "e2"], tag=ck + "e2")
            b.op("DVE", lambda v: v.tensor_tensor(p.psB[0:1, 0:512], statsA, statsA, ALU.mult),
                 reads=[ck + "mu"], writes=["psB0"], tag=ck + "mu2")
            b.op("DVE", lambda v: v.tensor_tensor(statsB, statsB, p.psB[0:1, 0:512], ALU.subtract),
                 reads=[ck + "e2", "psB0"], writes=[ck + "var"], tag=ck + "var")
            b.op("ACT", lambda s: s.activation(statsB, statsB, AF.Ln, bias=eps_col[0:1, :]),
                 reads=[ck + "var", "eps"], writes=[ck + "lnv"], tag=ck + "lnv")
            b.op("ACT", lambda s: s.activation(statsB, statsB, AF.Exp, scale=-0.5),
                 reads=[ck + "lnv"], writes=[ck + "rstd"], tag=ck + "rstd")
            b.op("DVE", lambda v: v.tensor_tensor(statsA, statsA, statsB, ALU.mult),
                 reads=[ck + "mu", ck + "rstd"], writes=[ck + "mur", "PTtmp"], tag=ck + "mur")
            b.op("DVE", lambda v: v.tensor_scalar_mul(statsA, statsA, -1.0),
                 reads=[ck + "mur"], writes=[ck + "B", "PTtmp"], tag=ck + "negB")
            def emb(t):
                t.matmul(p.psB[:, 0:512], p.tab_f32[0:1, TF_ONESROW:TF_ONESROW + 128],
                         statsB, start=True, stop=True)
                return t.matmul(p.psB[:, 512:1024], p.tab_f32[0:1, TF_ONESROW:TF_ONESROW + 128],
                                statsA, start=True, stop=True)
            b.op("PE", emb, reads=[ck + "rstd", ck + "B", "tab_f32"],
                 writes=["psB0", "psB1"], tag=ck + "bc")
            for k in range(ND):
                sq = sqb[k % 2]
                b.op("DVE", lambda v, k=k, t0=t0, sq=sq: v.tensor_tensor(
                    sq, xsh(k)(t0, 512), p.psB[:, 0:512], ALU.mult),
                    reads=[f"xb{k}", "psB0"], writes=[L + f"sqb{k % 2}"], tag=ck + f"xa{k}")
                b.op("DVE", lambda v, sq=sq, k=k: v.tensor_tensor(
                    sq, sq, p.psB[:, 512:1024], ALU.add),
                    reads=[L + f"sqb{k % 2}", "psB1"], writes=[L + f"sqb{k % 2}"],
                    tag=ck + f"xb{k}")
                b.op("ACT", lambda s, k=k, tci=tci, sq=sq: s.activation(
                    xn[:, k * 1024 + tci * 512: k * 1024 + (tci + 1) * 512], sq,
                    AF.Identity, scale=p.ccol(f"lng{li}", k, 1),
                    bias=p.ccol(f"lnb{li}", k, 1)),
                    reads=[L + f"sqb{k % 2}", "consts"],
                    writes=[fk + f"xn{k}"], tag=ck + f"xn{k}")
        XN = [fk + f"xn{k}" for k in range(ND)]
        for r in range(22):
            for part, roff, func in ((0, r, AF.Identity), (1, r + 22, AF.Gelu)):
                slot = wsl[part]
                skey = f"wB{part}"
                b.dma("w", slot.rearrange("p (k c) -> p k c", c=128),
                      p.h_w1p[li, :, roff * 128:(roff + 1) * 128].rearrange("(k p) c -> p k c", p=128),
                      writes=[skey], tag=fk + f"w1d{r}_{part}")
                for tci in range(2):
                    bank = (r * 2 + tci) % 4
                    def em(t, slot=slot, tci=tci, bank=bank):
                        ins = None
                        for k in range(ND):
                            ins = t.matmul(p.psA[:, bank * 512:(bank + 1) * 512],
                                           slot[:, k * 128:(k + 1) * 128],
                                           xn[:, k * 1024 + tci * 512: k * 1024 + (tci + 1) * 512],
                                           start=(k == 0), stop=(k == ND - 1))
                        return ins
                    b.op("PE", em, reads=[skey] + XN, writes=[f"psA{bank}"],
                         tag=fk + f"w1{r}_{part}_{tci}")
                    dst = (a_sb if part == 0 else g_sb)
                    b.op("ACT", lambda s, dst=dst, tci=tci, bank=bank, func=func, roff=roff: s.activation(
                        dst[:, tci * 512:(tci + 1) * 512], p.psA[:, bank * 512:(bank + 1) * 512],
                        func, bias=p.ccol(f"b1{li}", roff, 1)),
                        reads=[f"psA{bank}", "consts"], writes=[fk + ("a" if part == 0 else "g") + str(tci)],
                        tag=fk + f"w1ev{r}_{part}_{tci}")
                    if part == 1:
                        b.op("DVE", lambda v, r=r, tci=tci: v.tensor_tensor(
                            ff[:, r * 1024 + tci * 512: r * 1024 + (tci + 1) * 512],
                            a_sb[:, tci * 512:(tci + 1) * 512],
                            g_sb[:, tci * 512:(tci + 1) * 512], ALU.mult),
                            reads=[fk + f"a{tci}", fk + f"g{tci}"],
                            writes=[fk + f"ff{r}", fk + f"a{tci}", fk + f"g{tci}"],
                            tag=fk + f"ffm{r}_{tci}")
        FF = [fk + f"ff{r}" for r in range(22)]
        for m2 in range(8):
            slot = w2slotA if m2 % 2 == 0 else w2slotB
            skey = f"wW2{m2 % 2}"
            b.dma("w", slot.rearrange("p (k c) -> p k c", c=128),
                  p.h_w2p[li, :, m2 * 128:(m2 + 1) * 128].rearrange("(k p) c -> p k c", p=128),
                  writes=[skey], tag=fk + f"w2d{m2}")
            for tci in range(2):
                bank = tci % 2
                def em(t, slot=slot, tci=tci, bank=bank):
                    ins = None
                    for k2 in range(22):
                        ins = t.matmul(p.psB[:, bank * 512:(bank + 1) * 512],
                                       slot[:, k2 * 128:(k2 + 1) * 128],
                                       ff[:, k2 * 1024 + tci * 512: k2 * 1024 + (tci + 1) * 512],
                                       start=(k2 == 0), stop=(k2 == 21))
                    return ins
                b.op("PE", em, reads=[skey] + FF, writes=[f"psB{bank}"],
                     tag=fk + f"w2{m2}_{tci}")
                b.op("ACT", lambda s, bank=bank, m2=m2: s.activation(
                    w2tmp, p.psB[:, bank * 512:(bank + 1) * 512], AF.Identity,
                    bias=p.ccol(f"b2{li}", m2, 1)),
                    reads=[f"psB{bank}", "consts"], writes=[fk + "w2t", "PTtmp"],
                    tag=fk + f"w2b{m2}_{tci}")
                t0 = t0h + tci * 512
                Msl = M[:, m2 * 2048 + t0: m2 * 2048 + t0 + 512]
                b.op("DVE", lambda v, Msl=Msl: v.tensor_tensor(Msl, Msl, w2tmp, ALU.add),
                     reads=[fk + "w2t", f"M{m2}"], writes=[f"M{m2}"],
                     tag=fk + f"w2r{m2}_{tci}")
                b.op("DVE", lambda v, m2=m2, t0=t0, Msl=Msl: v.tensor_copy(
                    p.xbf[:, m2 * 2049 + 1 + t0: m2 * 2049 + 1 + t0 + 512], Msl),
                    reads=[f"M{m2}"], writes=[f"xb{m2}"], tag=fk + f"w2x{m2}_{tci}")



def block_ln_fm(p: Prog, tag: str, add_x1sp: bool):
    b = p.b
    M = p.pbf_f32
    Sf = p.xf32
    x1c = Sf[:, 0:4096]
    sq0 = p.pbf[:, 32768:33280]
    sq1 = p.pbf[:, 33280:33792]
    statsA = p.pbf_f32[0:1, 16896:17408]
    statsB = p.pbf_f32[0:1, 17408:17920]
    eps_col = p.scr[:, 24:25]
    for tc in range(4):
        t0 = tc * 512
        ck = tag + f"c{tc}_"
        if add_x1sp:
            b.dma("spill", x1c.rearrange("p (a c) -> p a c", c=512),
                  p.h_x1sp.rearrange("p (a c) -> p a c", c=2048)[:, :, t0:t0 + 512],
                  reads=["x1sp"], writes=[tag + "x1c"], queue="SP", tag=ck + "ldx1")
            for d in range(ND):
                Msl = M[:, d * 2048 + t0: d * 2048 + t0 + 512]
                b.op("DVE", lambda v, d=d, Msl=Msl: v.tensor_tensor(
                    Msl, Msl, x1c[:, d * 512:(d + 1) * 512], ALU.add),
                    reads=[tag + "x1c", f"M{d}"], writes=[f"M{d}"], tag=ck + f"add{d}")
        for d in range(ND):
            sq = (sq0 if d % 2 == 0 else sq1)
            Msl = M[:, d * 2048 + t0: d * 2048 + t0 + 512]
            b.op("DVE", lambda v, sq=sq, Msl=Msl: v.tensor_copy(sq, Msl),
                 reads=[f"M{d}"], writes=[tag + f"zb{d % 2}"], tag=ck + f"cz{d}")
            def em(t, sq=sq, d=d):
                return t.matmul(p.psB[0:1, 0:512], p.tab_bf[:, TB_ONES1:TB_ONES1 + 1],
                                sq, start=(d == 0), stop=(d == ND - 1))
            b.op("PE", em, reads=[tag + f"zb{d % 2}", "tb_tail"],
                 writes=["psB0"], tag=ck + f"s1{d}")
            b.op("DVE", lambda v, sq=sq: v.tensor_tensor(sq, sq, sq, ALU.mult),
                 reads=[tag + f"zb{d % 2}"], writes=[tag + f"zb{d % 2}"], tag=ck + f"sq{d}")
            def em2(t, sq=sq, d=d):
                return t.matmul(p.psB[0:1, 512:1024], p.tab_bf[:, TB_ONES1:TB_ONES1 + 1],
                                sq, start=(d == 0), stop=(d == ND - 1))
            b.op("PE", em2, reads=[tag + f"zb{d % 2}", "tb_tail"],
                 writes=["psB1"], tag=ck + f"s2{d}")
        b.op("DVE", lambda v: v.tensor_scalar_mul(statsA, p.psB[0:1, 0:512], 1.0 / 1024.0),
             reads=["psB0"], writes=[ck + "mu", "PTtmp"], tag=ck + "mu")
        b.op("DVE", lambda v: v.tensor_scalar_mul(statsB, p.psB[0:1, 512:1024], 1.0 / 1024.0),
             reads=["psB1"], writes=[ck + "e2"], tag=ck + "e2")
        b.op("DVE", lambda v: v.tensor_tensor(p.psB[0:1, 0:512], statsA, statsA, ALU.mult),
             reads=[ck + "mu"], writes=["psB0"], tag=ck + "mu2")
        b.op("DVE", lambda v: v.tensor_tensor(statsB, statsB, p.psB[0:1, 0:512], ALU.subtract),
             reads=[ck + "e2", "psB0"], writes=[ck + "var"], tag=ck + "var")
        b.op("ACT", lambda s: s.activation(statsB, statsB, AF.Ln, bias=eps_col[0:1, :]),
             reads=[ck + "var", "eps"], writes=[ck + "lnv"], tag=ck + "lnv")
        b.op("ACT", lambda s: s.activation(statsB, statsB, AF.Exp, scale=-0.5),
             reads=[ck + "lnv"], writes=[ck + "rstd"], tag=ck + "rstd")
        b.op("DVE", lambda v: v.tensor_tensor(statsA, statsA, statsB, ALU.mult),
             reads=[ck + "mu", ck + "rstd"], writes=[ck + "mur", "PTtmp"], tag=ck + "mur")
        b.op("DVE", lambda v: v.tensor_scalar_mul(statsA, statsA, -1.0),
             reads=[ck + "mur"], writes=[ck + "B", "PTtmp"], tag=ck + "negB")
        def emb(t):
            t.matmul(p.psB[:, 0:512], p.tab_f32[0:1, TF_ONESROW:TF_ONESROW + 128],
                     statsB, start=True, stop=True)
            return t.matmul(p.psB[:, 512:1024], p.tab_f32[0:1, TF_ONESROW:TF_ONESROW + 128],
                            statsA, start=True, stop=True)
        b.op("PE", emb, reads=[ck + "rstd", ck + "B", "tab_f32"],
             writes=["psB0", "psB1"], tag=ck + "bc")
        for d in range(ND):
            Msl = M[:, d * 2048 + t0: d * 2048 + t0 + 512]
            b.op("DVE", lambda v, Msl=Msl: v.tensor_tensor(Msl, Msl, p.psB[:, 0:512], ALU.mult),
                 reads=[f"M{d}", "psB0"], writes=[f"M{d}"], tag=ck + f"na{d}")
            b.op("DVE", lambda v, Msl=Msl: v.tensor_tensor(Msl, Msl, p.psB[:, 512:1024], ALU.add),
                 reads=[f"M{d}", "psB1"], writes=[f"M{d}"], tag=ck + f"nb{d}")
            b.op("DVE", lambda v, d=d, t0=t0, Msl=Msl: v.tensor_copy(
                p.xbf[:, d * 2049 + 1 + t0: d * 2049 + 1 + t0 + 512], Msl),
                reads=[f"M{d}"], writes=[f"xb{d}"], tag=ck + f"nx{d}")



def stage_c(p: Prog):
    import os
    trunc = os.environ.get("MK_CTRUNC", "")
    b = p.b
    M = p.pbf_f32
    S = p.xf32.bitcast(BF16)
    Sf = p.xf32
    out_fm = S[:, 0:16384]
    qj = S[:, 16384:18432]
    kj = S[:, 18432:20480]
    vj = S[:, 20480:22528]
    decay_full = Sf[:, 11264:13312]
    prod = S[:, 26624:28672]
    phi_tS = [Sf[:, 14336:14848], Sf[:, 14848:15360]]
    phi_t2S = [S[:, 30720:31744], S[:, 31744:32768]]
    out2c = S[:, 16384:20480]
    vloadS = [S[:, 20480:20992], S[:, 20992:21504]]
    gbuf = S[:, 21504:22528]
    kvbuf = p.pbf[:, 32768:34816]
    runb = p.pbf_f32[:, 17408:19456]
    dcol = p.scr[:, 88:96]
    wsl = [p.wbuf[:, 0:1024], p.wbuf[:, 1024:2048]]

    XB = [f"xb{d}" for d in range(ND)]

    b.op("ACT", lambda s: s.activation(dcol, p.ccol("decay", 0, 8), AF.Sigmoid),
         reads=["consts"], writes=["dcol"], tag="c_dcol")

    if trunc == "ln2":
        return
    for j in range(ND):
        pk = f"C{j}_"
        b.op("DVE", lambda v: v.memset(decay_full, 1.0),
             writes=["Cdec"], tag=pk + "dec0")
        b.op("DVE", lambda v, j=j: v.tensor_scalar(
            decay_full, decay_full, dcol[:, j:j + 1], None, ALU.mult),
            reads=["dcol", "Cdec"], writes=["Cdec"], tag="Cdec")
        for sl_i, mbase, dstbuf, dkey, isphi in ((0, j * 128, qj, "Cq", True),
                                                 (1, 1024 + j * 128, kj, "Ck", True),
                                                 (2, 2048 + j * 128, vj, "Cv", False)):
            slot = wsl[sl_i % 2]
            skey = f"wC{sl_i % 2}"
            b.dma("w", slot.rearrange("p (k c) -> p k c", c=128),
                  p.h_wqkvT_c[:, mbase:mbase + 128].rearrange("(k p) c -> p k c", p=128),
                  writes=[skey], tag=pk + f"wd{sl_i}")
            for tc in range(4):
                bank = tc % 4
                def em(t, slot=slot, tc=tc, bank=bank):
                    ins = None
                    for k in range(ND):
                        ins = t.matmul(p.psA[:, bank * 512:(bank + 1) * 512],
                                       slot[:, k * 128:(k + 1) * 128],
                                       p.xbf[:, k * 2049 + 1 + tc * 512: k * 2049 + 1 + (tc + 1) * 512],
                                       start=(k == 0), stop=(k == ND - 1))
                    return ins
                b.op("PE", em, reads=[skey] + XB, writes=[f"psA{bank}"],
                     tag=pk + f"mm{sl_i}_{tc}")
                if not isphi:
                    b.op("ACT", lambda s, dstbuf=dstbuf, tc=tc, bank=bank: s.activation(
                        dstbuf[:, tc * 512:(tc + 1) * 512],
                        p.psA[:, bank * 512:(bank + 1) * 512], AF.Identity),
                        reads=[f"psA{bank}"], writes=[dkey], tag=pk + f"ev{sl_i}_{tc}")
                else:
                    ps_ = tc % 2
                    phi_t = phi_tS[ps_]
                    phi_t2 = phi_t2S[ps_]
                    b.op("DVE", lambda v, bank=bank, phi_t=phi_t: v.tensor_scalar_min(
                        phi_t, p.psA[:, bank * 512:(bank + 1) * 512], 0.0),
                        reads=[f"psA{bank}"], writes=[f"Cpt{ps_}"], tag=pk + f"mn{sl_i}_{tc}")
                    b.op("ACT", lambda s, phi_t=phi_t, phi_t2=phi_t2: s.activation(
                        phi_t2[:, 0:512], phi_t, AF.Exp),
                        reads=[f"Cpt{ps_}"], writes=[f"Cpe{ps_}"], tag=pk + f"ex{sl_i}_{tc}")
                    b.op("ACT", lambda s, bank=bank, phi_t2=phi_t2: s.activation(
                        phi_t2[:, 512:1024], p.psA[:, bank * 512:(bank + 1) * 512], AF.Relu),
                        reads=[f"psA{bank}"], writes=[f"Cpr{ps_}"], tag=pk + f"rl{sl_i}_{tc}")
                    b.op("DVE", lambda v, dstbuf=dstbuf, tc=tc, phi_t2=phi_t2: v.tensor_tensor(
                        dstbuf[:, tc * 512:(tc + 1) * 512], phi_t2[:, 0:512],
                        phi_t2[:, 512:1024], ALU.add),
                        reads=[f"Cpe{ps_}", f"Cpr{ps_}"], writes=[dkey], tag=pk + f"ph{sl_i}_{tc}")
        b.dma("spill", p.h_vsp[:, j * 2048:(j + 1) * 2048], vj,
              reads=["Cv"], writes=["vsp"], queue="SP", tag=pk + "vspill")
        b.op("DVE", lambda v: v.tensor_tensor(kvbuf, kj, vj, ALU.mult),
             reads=["Ck", "Cv"], writes=["Ckv"], tag="Ckv")
        b.op("DVE", lambda v: v.tensor_tensor_scan(runb, decay_full, kj, 0.0,
                                                   ALU.mult, ALU.add),
             reads=["Cdec", "Ck"], writes=["Crun"], tag=pk + "scank")
        b.op("DVE", lambda v: v.tensor_copy(kj, runb),
             reads=["Crun"], writes=["Ck"], tag="Ck")
        b.op("DVE", lambda v: v.tensor_tensor(prod, qj, kj, ALU.mult),
             reads=["Cq", "Ck"], writes=["Cprod"], tag="Cprod")
        def emd(t):
            ins = None
            for c0 in range(0, 2048, 512):
                ins = t.matmul(p.psA[0:2, c0:c0 + 512], p.tab_bf[:, TB_ONES2:TB_ONES2 + 2],
                               prod[:, c0:c0 + 512], start=True, stop=True)
            return ins
        b.op("PE", emd, reads=["Cprod", "tb_tail"],
             writes=["psA0", "psA1", "psA2", "psA3"], tag=pk + "den")
        b.op("DVE", lambda v: v.tensor_scalar_max(p.psA[0:2, 0:2048], p.psA[0:2, 0:2048], 1e-6),
             reads=["psA0", "psA1", "psA2", "psA3"],
             writes=["psA0", "psA1", "psA2", "psA3"], tag=pk + "clip")
        b.op("DVE", lambda v: v.reciprocal(p.psA[0:2, 0:2048], p.psA[0:2, 0:2048]),
             reads=["psA0", "psA1", "psA2", "psA3"],
             writes=["psA0", "psA1", "psA2", "psA3"], tag=pk + "rcp")
        b.op("DVE", lambda v: v.tensor_copy(prod[0:2, 0:2048], p.psA[0:2, 0:2048]),
             reads=["psA0", "psA1", "psA2", "psA3"], writes=["Cdinv"], tag=pk + "dcp")
        def emb2(t):
            ins = None
            for c0 in range(0, 2048, 512):
                ins = t.matmul(p.psA[:, c0:c0 + 512], p.tab_bf[0:2, TB_BCAST2:TB_BCAST2 + 128],
                               prod[0:2, c0:c0 + 512], start=True, stop=True)
            return ins
        b.op("PE", emb2, reads=["Cdinv", "tb_tail"],
             writes=["psA0", "psA1", "psA2", "psA3"], tag=pk + "dbc")
        b.op("DVE", lambda v: v.tensor_tensor_scan(runb, decay_full, kvbuf, 0.0,
                                                   ALU.mult, ALU.add),
             reads=["Cdec", "Ckv"], writes=["Crun"],
             tag=pk + "scankv")
        b.op("DVE", lambda v: v.tensor_copy(kj, runb),
             reads=["Crun"], writes=["Ck"], tag="Ck")
        b.op("DVE", lambda v: v.tensor_tensor(prod, qj, kj, ALU.mult),
             reads=["Cq", "Ck"], writes=["Cprod", "Cdinv"], tag=pk + "num")
        b.op("DVE", lambda v, j=j: v.tensor_tensor(
            out_fm[:, j * 2048:(j + 1) * 2048], prod, p.psA[:, 0:2048], ALU.mult),
            reads=["Cprod", "psA0", "psA1", "psA2", "psA3"],
            writes=[f"cof{j}"], tag=pk + "out")

    COF = [f"cof{j}" for j in range(ND)]
    if trunc == "tiles":
        return
    b.barrier()
    for tc in range(4):
        gk = f"Cg{tc}_"
        t0 = tc * 512
        for m in range(ND):
            slot = wsl[m % 2]
            skey = f"wC{m % 2}"
            b.dma("w", slot.rearrange("p (k c) -> p k c", c=128),
                  p.h_wgateT[:, m * 128:(m + 1) * 128].rearrange("(k p) c -> p k c", p=128),
                  writes=[skey], tag=gk + f"gd{m}")
            def em(t, slot=slot, t0=t0):
                ins = None
                for k in range(ND):
                    ins = t.matmul(p.psB[:, 0:512], slot[:, k * 128:(k + 1) * 128],
                                   out_fm[:, k * 2048 + t0: k * 2048 + t0 + 512],
                                   start=(k == 0), stop=(k == ND - 1))
                return ins
            b.op("PE", em, reads=[skey] + COF, writes=["psB0"], tag=gk + f"gm{m}")
            b.op("ACT", lambda s, m=m: s.activation(
                gbuf[:, 0:512], p.psB[:, 0:512], AF.Sigmoid,
                bias=p.ccol("bgate", m, 1)),
                reads=["psB0", "consts"], writes=["Cgg"], tag=gk + f"gs{m}")
            vsl = m % 2
            vload = vloadS[vsl]
            b.dma("spill", vload, p.h_vsp[:, m * 2048 + t0: m * 2048 + t0 + 512],
                  reads=["vsp"], writes=[f"Cgv{vsl}"], queue="SP", tag=gk + f"vld{m}")
            b.op("DVE", lambda v, m=m, t0=t0, vload=vload: v.tensor_tensor(
                gbuf[:, 512:1024], out_fm[:, m * 2048 + t0: m * 2048 + t0 + 512],
                vload, ALU.subtract),
                reads=[COF[m], f"Cgv{vsl}"], writes=["Cgd"], tag=gk + f"d{m}")
            b.op("DVE", lambda v: v.tensor_tensor(
                gbuf[:, 512:1024], gbuf[:, 512:1024], gbuf[:, 0:512], ALU.mult),
                reads=["Cgd", "Cgg"], writes=["Cgd"], tag=gk + f"gd2{m}")
            b.op("DVE", lambda v, m=m, vload=vload: v.tensor_tensor(
                out2c[:, m * 512:(m + 1) * 512], gbuf[:, 512:1024], vload, ALU.add),
                reads=["Cgd", f"Cgv{vsl}"], writes=[f"Cgo2_{m}"], tag=gk + f"o2{m}")
        for mo in range(ND):
            slot = wsl[mo % 2]
            skey = f"wC{mo % 2}"
            b.dma("w", slot.rearrange("p (k c) -> p k c", c=128),
                  p.h_woutT[:, mo * 128:(mo + 1) * 128].rearrange("(k p) c -> p k c", p=128),
                  writes=[skey], tag=gk + f"od{mo}")
            def em(t, slot=slot):
                ins = None
                for k in range(ND):
                    ins = t.matmul(p.psB[:, 512:1024], slot[:, k * 128:(k + 1) * 128],
                                   out2c[:, k * 512:(k + 1) * 512],
                                   start=(k == 0), stop=(k == ND - 1))
                return ins
            b.op("PE", em, reads=[skey] + [f"Cgo2_{k}" for k in range(ND)],
                 writes=["psB1"], tag=gk + f"om{mo}")
            Msl = M[:, mo * 2048 + t0: mo * 2048 + t0 + 512]
            b.op("DVE", lambda v, Msl=Msl: v.tensor_tensor(
                Msl, Msl, p.psB[:, 512:1024], ALU.add),
                reads=["psB1", f"M{mo}"], writes=[f"M{mo}"], tag=gk + f"or{mo}")
    if trunc == "gate":
        return
    b.barrier()
    block_ln_fm(p, "LN3_", add_x1sp=False)



def stage_d(p: Prog):
    b = p.b
    M = p.pbf_f32
    S = p.xf32.bitcast(BF16)
    Sf = p.xf32
    IDf = p.ident_f32

    for dt in range(ND):
        for g4 in range(4):
            bank = g4 % 4
            def em(t, dt=dt, g4=g4, bank=bank):
                ins = None
                for q in range(4):
                    tt = g4 * 4 + q
                    ins = t.transpose(p.psA[:, bank * 512 + q * 128: bank * 512 + (q + 1) * 128],
                                      M[:, dt * 2048 + tt * 128: dt * 2048 + tt * 128 + 128],
                                      IDf)
                return ins
            b.op("PE", em, reads=[f"M{dt}", "tab_f32"], writes=[f"psA{bank}"],
                 tag=f"Dtr{dt}_{g4}")
            def ev(v, dt=dt, g4=g4, bank=bank):
                ins = None
                for q in range(4):
                    tt = g4 * 4 + q
                    ins = v.tensor_copy(
                        Sf[:, tt * 1024 + dt * 128: tt * 1024 + dt * 128 + 128],
                        p.psA[:, bank * 512 + q * 128: bank * 512 + (q + 1) * 128])
                return ins
            b.op("DVE", ev, reads=[f"psA{bank}"], writes=[f"Dtm{g4 * 4}"],
                 tag=f"Dte{dt}_{g4}")
    b.barrier()
    b.dma("spill", M[:, 0:16384], Sf[:, 0:16384],
          reads=[f"Dtm{t}" for t in range(0, 16, 4)],
          writes=[f"M{d}" for d in range(ND)] + [f"OT{t}" for t in range(NT)],
          queue="SP", tag="D_tmcopy")
    b.barrier()

    wz = S[:, 0:16384]
    wc = S[:, 16384:24576]
    zblk = S[:, 24576:32768]
    rm_bf = p.pbf[:, 32768:33792]
    ru_bf = p.pbf[:, 33792:37888]
    wzev = p.pbf[:, 37888:38400]
    eps_col = p.scr[:, 24:25]

    for h in range(4):
        hk = f"Dw{h}_"
        b.dma("w", rm_bf[0:16, :].rearrange("p (r c) -> p r c", c=16),
              p.h_rmT[h].rearrange("r c i -> c r i"),
              writes=["rmb"], tag=hk + "rmd")
        b.dma("w", ru_bf[0:16, :].rearrange("p (r c) -> p r c", c=64),
              p.h_ruT[h].rearrange("r c i -> c r i"),
              writes=["rub"], tag=hk + "rud")
        for B in range(8):
            def em(t, B=B):
                ins = None
                for a in range(8):
                    r = B * 8 + a
                    ins = t.matmul(p.psB[0:16, a * 64:(a + 1) * 64],
                                   rm_bf[0:16, r * 16:(r + 1) * 16],
                                   ru_bf[0:16, r * 64:(r + 1) * 64],
                                   start=True, stop=True)
                return ins
            b.op("PE", em, reads=["rmb", "rub"], writes=["psB0", "psB1"], tag=hk + f"mz{B}")
            b.op("DVE", lambda v: v.tensor_copy(wzev[0:16, :], p.psB[0:16, 0:512]),
                 reads=["psB0", "psB1"], writes=["wzev"], tag=hk + f"ze{B}")
            def em2(t, B=B):
                ins = None
                for a in range(8):
                    ins = t.matmul(p.psA[:, a * 64:(a + 1) * 64],
                                   p.tab_bf[0:16, TB_SEL + a * 128: TB_SEL + (a + 1) * 128],
                                   wzev[0:16, a * 64:(a + 1) * 64],
                                   start=True, stop=True)
                return ins
            b.op("PE", em2, reads=["wzev", "tb_tail"], writes=["psA0"], tag=hk + f"pl{B}")
            def ev(v, h=h, B=B):
                return v.tensor_copy(
                    wz[:, (h * 32 + 4 * B) * 128: (h * 32 + 4 * B) * 128 + 512],
                    p.psA[:, 0:512])
            b.op("DVE", ev, reads=["psA0"], writes=[f"wz{h}_{B}"], tag=hk + f"zf{B}")
    WZ = [f"wz{h}_{B}" for h in range(4) for B in range(8)]

    XB = [f"xb{d}" for d in range(ND)]
    for half in range(2):
        t0h = half * 1024
        for blk in range(16):
            bk = f"Dz{half}_{blk}_"
            b.dma("w", wc.rearrange("p (k c) -> p k c", c=1024),
                  p.h_wcT[blk * 1024:(blk + 1) * 1024, :].rearrange("(k p) c -> p k c", p=128),
                  writes=["wcb"], tag=bk + "wcd")
            for zz in range(8):
                zc = blk * 8 + zz
                xt = (zc % 32) // 4
                zb = zz % 2
                for c0 in range(0, 1024, 512):
                    def emz(t, zc=zc, xt=xt, t0h=t0h, zb=zb, c0=c0):
                        return t.matmul(p.psB[:, zb * 512: zb * 512 + 512],
                                        wz[:, zc * 128:(zc + 1) * 128],
                                        p.xbf[:, xt * 2049 + 1 + t0h + c0: xt * 2049 + 1 + t0h + c0 + 512],
                                        start=True, stop=True)
                    b.op("PE", emz, reads=WZ + XB, writes=[f"psB{zb}"], tag=bk + f"zm{zz}_{c0}")
                    b.op("ACT", lambda s, zz=zz, zb=zb, c0=c0: s.activation(
                        zblk[:, zz * 1024 + c0: zz * 1024 + c0 + 512],
                        p.psB[:, zb * 512: zb * 512 + 512], AF.Gelu),
                        reads=[f"psB{zb}"], writes=[f"Dzb{zz}"], tag=bk + f"zg{zz}_{c0}")
            for ttl in range(8):
                tt = half * 8 + ttl
                po = (ttl % 2) * 1024
                pk2 = ["psA0", "psA1"] if ttl % 2 == 0 else ["psA2", "psA3"]
                def em(t, ttl=ttl, blk=blk, po=po):
                    ins = None
                    for zz in range(8):
                        for c0 in range(0, 1024, 512):
                            ins = t.matmul(
                                p.psA[:, po + c0: po + c0 + 512],
                                zblk[:, zz * 1024 + ttl * 128: zz * 1024 + ttl * 128 + 128],
                                wc[:, zz * 1024 + c0: zz * 1024 + c0 + 512],
                                start=(zz == 0), stop=(zz == 7))
                    return ins
                b.op("PE", em, reads=[f"Dzb{zz}" for zz in range(8)] + ["wcb"],
                     writes=pk2, tag=bk + f"cp{ttl}")
                OT = M[:, tt * 1024:(tt + 1) * 1024]
                b.op("DVE", lambda v, OT=OT, po=po: v.tensor_tensor(
                    OT, OT, p.psA[:, po:po + 1024], ALU.add),
                    reads=pk2 + [f"OT{tt}"], writes=[f"OT{tt}"], tag=bk + f"ca{ttl}")

    for tt in range(NT):
        lk = f"LN4_{tt}_"
        OT = M[:, tt * 1024:(tt + 1) * 1024]
        st = p.scr
        b.op("DVE", lambda v, OT=OT: v.reduce_sum(st[:, 80:81], OT, axis=AX.X),
             reads=[f"OT{tt}"], writes=["l4s"], tag=lk + "s")
        b.op("DVE", lambda v: v.tensor_scalar_mul(st[:, 81:82], st[:, 80:81], 1.0 / 1024.0),
             reads=["l4s"], writes=["l4m"], tag=lk + "m")
        b.op("DVE", lambda v, OT=OT: v.tensor_scalar(OT, OT, st[:, 81:82], None, ALU.subtract),
             reads=[f"OT{tt}", "l4m"], writes=[f"OT{tt}"], tag=lk + "c")
        b.op("ACT", lambda s, OT=OT: s.activation(p.pbf[:, 32768:33792], OT, AF.Square,
                                                  accum_out=st[:, 82:83]),
             reads=[f"OT{tt}"], writes=["l4q", "rmb", "rub"], tag=lk + "q")
        b.op("ACT", lambda s: s.activation(st[:, 83:84], st[:, 82:83], AF.Ln,
                                           scale=1.0 / 1024.0, bias=eps_col),
             reads=["l4q", "eps"], writes=["l4l"], tag=lk + "l")
        b.op("ACT", lambda s: s.activation(st[:, 84:85], st[:, 83:84], AF.Exp, scale=-0.5),
             reads=["l4l"], writes=["l4r"], tag=lk + "r")
        b.op("DVE", lambda v, OT=OT: v.tensor_scalar(OT, OT, st[:, 84:85], None, ALU.mult),
             reads=[f"OT{tt}", "l4r"], writes=[f"OT{tt}"], tag=lk + "n")
    b.dma("out", p.h_out.rearrange("(a p) d -> p a d", p=128),
          M[:, 0:16384].rearrange("p (a d) -> p a d", d=1024),
          reads=[f"OT{t}" for t in range(NT)], queue="SP", tag="st_out",
          is_output=True)


def build_kernel_program(stages=("A",), debug_out=None, clayout=None, nlayers_b=6):
    p = Prog(clayout, stages, debug_out)
    p.load_commons()
    if "A" in stages:
        stage_a(p)
    if "B" in stages:
        p.b.barrier()
        conv_a2b(p)
        p.b.dma("spill", p.h_x1sp, p.pbf_f32[:, 0:16384],
                reads=[f"M{d}" for d in range(ND)], writes=["x1sp"], queue="SP",
                tag="sp_x1")
        p.load_rotary()
        for li in range(nlayers_b):
            p.b.barrier()
            stage_b_layer(p, li)
        if p.debug_out in ("b0", "b1", "b2", "b3", "b4", "b5", "xfm"):
            p.b.dma("out", p.h_out.rearrange("(a p) d -> p a d", p=128),
                    p.pbf_f32[:, 0:16384].rearrange("p (a d) -> p a d", d=1024),
                    reads=[f"M{d}" for d in range(ND)], queue="SP", tag="st_M",
                    is_output=True)
    if "C" in stages:
        p.b.barrier()
        block_ln_fm(p, "LN2_", add_x1sp=True)
        p.b.barrier()
        stage_c(p)
        if p.debug_out in ("x2", "x3"):
            p.b.dma("out", p.h_out.rearrange("(a p) d -> p a d", p=128),
                    p.pbf_f32[:, 0:16384].rearrange("p (a d) -> p a d", d=1024),
                    reads=[f"M{d}" for d in range(ND)], queue="SP", tag="st_M",
                    is_output=True)
    if "D" in stages:
        p.b.barrier()
        stage_d(p)
    p.b.emit()
    return p


def kernel(**inputs):
    shared, clayout = _host_prep(inputs)
    x = np.asarray(inputs["L0_attn_x"], np.float32)
    p = build_kernel_program(stages=("A", "B", "C", "D"), debug_out=None,
                             clayout=clayout)
    in_maps = [dict(shared, x=np.ascontiguousarray(x[i])) for i in range(8)]
    res = run_bass_kernel_spmd(p.nc, in_maps, core_ids=list(range(8)))
    p.b.close()
    out = np.stack([r["out"] for r in res.results], axis=0)
    return out.astype(np.float32)
